# version 15
# baseline (speedup 1.0000x reference)
import contextlib
import numpy as np
import ml_dtypes
import concourse.bass as bass
import concourse.mybir as mybir
from concourse.bass import IndirectOffsetOnAxis
from concourse.bass_utils import run_bass_kernel_spmd
import concourse.bass_utils as _bu

_orig_run_command = _bu.run_command

def _run_command_ldwopt(cmd, *a, **k):
    cmd = ["--enable-ldw-opt=true" if c == "--enable-ldw-opt=false" else c
           for c in cmd]
    return _orig_run_command(cmd, *a, **k)

_bu.run_command = _run_command_ldwopt

F32 = mybir.dt.float32
BF16 = mybir.dt.bfloat16
I32 = mybir.dt.int32
U8 = mybir.dt.uint8
AF = mybir.ActivationFunctionType
ALU = mybir.AluOpType
BF = ml_dtypes.bfloat16

N_CORES = 8
D = 4096
NH = 32
NKV = 8
HD = 128
NBLK = 256
BS = 16
EPS = 1e-5
SCALE = 1.0 / float(np.sqrt(128.0))
ND = D // 128
OS_W = 512


def chunks(total, step=512):
    out = []
    o = 0
    while o < total:
        w = min(step, total - o)
        out.append((o, w))
        o += w
    return out


class Sems:
    def __init__(self, nc, stack):
        self.nc, self.stack, self.s, self.c = nc, stack, {}, {}

    def new(self, *names):
        for n in names:
            self.s[n] = self.stack.enter_context(self.nc.semaphore(n))
            self.c[n] = 0

    def inc(self, inst, name, d=1):
        inst.then_inc(self.s[name], d)
        self.c[name] += d
        return self.c[name]

    def inc2(self, inst, n1, n2, d2=1):
        inst.then_inc(self.s[n1], 1)
        self.c[n1] += 1
        inst.then_inc(self.s[n2], d2)
        self.c[n2] += d2
        return self.c[n2]

    def val(self, name):
        return self.c[name]

    def wait(self, eng, name, v):
        if v > 0:
            eng.wait_ge(self.s[name], v)


class AttnCtx:
    pass


def attn_scores_kt(nc, S, c, u, ui, kt):
    kv, hh, th = u
    h = kv * 4 + hh
    parh = h % 2
    pk = kt % 2
    E = c.Earea[ui % 2]
    S.wait(nc.tensor, "expd", c.scps_gate[pk])
    for (no, nw) in chunks(c.TH):
        last = nc.tensor.matmul(
            c.scps[pk][:, no:no + nw],
            c.KT[kv % 2][:, kt * 128:(kt + 1) * 128],
            c.qth[parh][:, th * c.TH + no: th * c.TH + no + nw],
            start=True, stop=True)
    v2 = S.inc(last, "scmm")
    S.wait(nc.scalar, "scmm", v2)
    c.scps_gate[pk] = S.inc(
        nc.scalar.activation(E[:, kt * c.TH:(kt + 1) * c.TH],
                             c.scps[pk], AF.Exp, scale=SCALE), "expd")


def attn_ctx_tc(nc, S, c, u, ui, tc, ktr=(0, 32)):
    kv, hh, th = u
    E = c.Earea[ui % 2]
    CTC = c.TH // 128
    bk, slot = tc // 3, tc % 3
    if tc == 0 and ktr[0] == 0:
        S.wait(nc.tensor, "cxev", S.val("cxev"))
    for kt in range(*ktr):
        if tc == 0:
            S.wait(nc.tensor, "expd", c.expd_end[ui] - 31 + kt)
        last = nc.tensor.matmul(
            c.cxb[bk][:, slot * 129:slot * 129 + 129],
            E[:, kt * c.TH + tc * 128: kt * c.TH + (tc + 1) * 128],
            c.vp[(kv % 2) * 2 + kt % 2][:, (kt // 2) * 129:(kt // 2) * 129 + 129],
            start=(kt == 0), stop=(kt == 31))
    if ktr[1] < 32:
        return
    v3 = S.inc(last, "ctxg")
    if slot == 2 or tc == CTC - 1:
        S.wait(nc.vector, "ctxg", v3)
        for tcc in range(bk * 3, tc + 1):
            sl = tcc % 3
            S.wait(nc.vector, "vself", S.val("vself"))
            S.wait(nc.vector, "cxev", S.val("cxev"))
            S.inc(nc.vector.reciprocal(
                c.recb, c.cxb[bk][:, sl * 129 + 128: sl * 129 + 129]), "vself")
            S.wait(nc.vector, "vself", S.val("vself"))
            inst = nc.vector.tensor_scalar(
                c.cxn[:, tcc * 128:(tcc + 1) * 128],
                c.cxb[bk][:, sl * 129: sl * 129 + 128],
                c.recb, None, ALU.mult)
            S.inc(inst, "cxev" if tcc == tc else "vself")
    if tc == CTC - 1:
        c.ctxg_end[ui] = S.val("ctxg")


def attn_T(nc, S, c, u, ui):
    kv, hh, th = u
    h = kv * 4 + hh
    parh = h % 2
    CTC = c.TH // 128
    S.wait(nc.tensor, "cxev", S.val("cxev"))
    for g in range((CTC + 3) // 4):
        S.wait(nc.tensor, "trev", S.val("trev"))
        n_in_g = min(4, CTC - g * 4)
        for k in range(n_in_g):
            tc = g * 4 + k
            last = nc.tensor.transpose(c.trpA[:, k * 128:(k + 1) * 128],
                                       c.cxn[:, tc * 128:(tc + 1) * 128], c.ident)
        v4 = S.inc(last, "trp")
        S.wait(nc.vector, "trp", v4)
        if th == 0 and g == 0 and h >= 2:
            S.wait(nc.vector, "ctxsta" if parh == 0 else "ctxstb",
                   (h // 2) * 16)
        dst = bass.AP(c.cxT[parh].tensor,
                      c.cxT[parh].offset + th * c.TH + g * 4 * 128,
                      [[c.cxT[parh].ap[0][0], 128], [128, n_in_g], [1, 128]])
        S.inc(nc.vector.tensor_copy(
            dst, c.trpA.rearrange("p (a b) -> p a b", b=128)[:, 0:n_in_g, :]),
            "trev")
    if th == c.NTH - 1:
        S.wait(nc.sync, "trev", S.val("trev"))
        S.inc(nc.sync.dma_start(c.ctxsp[h], c.cxT[parh]),
              "ctxsta" if parh == 0 else "ctxstb", 16)


def attn_unit(nc, S, c, u, ui, prev):
    kv, hh, th = u
    h = kv * 4 + hh
    parh = h % 2
    qsem = "qlda" if parh == 0 else "qldb"
    if th == 0:
        if h >= 2:
            S.wait(nc.sync, "scmm", S.val("scmm"))
        c.vq[h] = S.inc(nc.sync.dma_start(c.qth[parh], c.qsp[h]), qsem, 16)
    S.wait(nc.tensor, qsem, c.vq[h])
    if hh == 0 and th == 0:
        S.wait(nc.tensor, "trev", S.val("trev"))
    if ui >= 2:
        S.wait(nc.scalar, "ctxg", c.ctxg_end[ui - 2])
    CTC = c.TH // 128
    for j in range(32):
        attn_scores_kt(nc, S, c, u, ui, j)
        if j == 31:
            c.expd_end[ui] = S.val("expd")
        if prev is not None and j // 4 < CTC:
            attn_ctx_tc(nc, S, c, prev, ui - 1, j // 4,
                        (8 * (j % 4), 8 * (j % 4) + 8))
    if prev is not None:
        attn_T(nc, S, c, prev, ui - 1)


def build_program(NT=2048, debug=False):
    assert NT % 256 == 0
    NTC = NT // 128
    TH = min(1024, NT // 2)
    NTH = NT // TH
    CTC = TH // 128
    NS = NT // 512

    nc = bass.Bass("TRN2", target_bir_lowering=False)

    x_in = nc.dram_tensor("x", [NT, D], F32, kind="ExternalInput").ap()
    wq_in = nc.dram_tensor("wq", [NH, 128, D], F32, kind="ExternalInput").ap()
    wo_in = nc.dram_tensor("wo", [D // OS_W, 128, ND * OS_W], F32, kind="ExternalInput").ap()
    w0_in = nc.dram_tensor("w0", [D // OS_W, 128, ND * OS_W], F32, kind="ExternalInput").ap()
    kc_in = nc.dram_tensor("kc", [NKV, 512, BS, HD], F32, kind="ExternalInput").ap()
    vc_in = nc.dram_tensor("vc", [NKV, 512, BS, HD], F32, kind="ExternalInput").ap()
    bt_in = nc.dram_tensor("bt", [NBLK], I32, kind="ExternalInput").ap()
    at_in = nc.dram_tensor("at", [128, ND * 64], F32, kind="ExternalInput").ap()
    bl_in = nc.dram_tensor("bl", [4, 16, D], F32, kind="ExternalInput").ap()
    b0_in = nc.dram_tensor("b0", [1, D], F32, kind="ExternalInput").ap()
    cos_in = nc.dram_tensor("cosT", [128, NT], F32, kind="ExternalInput").ap()
    sin_in = nc.dram_tensor("sinT", [128, NT], F32, kind="ExternalInput").ap()
    m_in = [nc.dram_tensor(f"m{l}", [NT, D], U8, kind="ExternalInput").ap()
            for l in range(1, 4)]
    id_in = nc.dram_tensor("ident", [128, 128], BF16, kind="ExternalInput").ap()
    out_d = nc.dram_tensor("out", [NT, D], F32, kind="ExternalOutput").ap()

    sk = "ExternalOutput" if debug else "Internal"
    qsp = nc.dram_tensor("qsp", [NH, 128, NT], BF16, kind=sk).ap()
    ctxsp = nc.dram_tensor("ctxsp", [NH, 128, NT], BF16, kind=sk).ap()
    attnsp = nc.dram_tensor("attnsp", [NT, D], BF16, kind=sk).ap()
    x2sp = nc.dram_tensor("x2sp", [NT, D], F32, kind=sk).ap()
    ffnsp = nc.dram_tensor("ffnsp", [NT, D], BF16, kind=sk).ap()

    stack = contextlib.ExitStack()
    with stack:
        S = Sems(nc, stack)
        S.new("bar", "cld", "xld", "lnd", "trp", "trev",
              "wql", "qmm", "rope", "qspill",
              "kvg", "qld", "scmm", "expd", "ctxg", "cxev", "ctxst",
              "ctxld", "wsl", "womm", "woev", "wost",
              "hld", "hmm", "hact", "htr", "htev",
              "bld", "ymm", "mld", "comb", "outst", "b0ld", "vone",
              "xlda", "xldb", "wqla", "wqlb", "qlda", "qldb",
              "ctxsta", "ctxstb", "qspa", "qspb", "wosta", "wostb",
              "wsl0", "wsl1", "wsl2", "wsl3", "vself", "sself", "fina", "finb",
              "kvga", "kvgb", "kvgv", "xldc", "xldd")

        BIGT = nc.alloc_sbuf_tensor("BIGT", [128, ND * NT], BF16).ap()
        SS = nc.alloc_sbuf_tensor("SS", [128, 2 * D], F32).ap()
        ident = nc.alloc_sbuf_tensor("identsb", [128, 128], BF16).ap()
        ones1 = nc.alloc_sbuf_tensor("ones1", [1, 128], BF16).ap()
        idxsb = nc.alloc_sbuf_tensor("idxsb", [128, 2], I32).ap()
        stats = nc.alloc_sbuf_tensor("stats", [128, 48], F32).ap()
        aggr = nc.alloc_sbuf_tensor("aggr", [128, 2], F32).ap()
        rstd = nc.alloc_sbuf_tensor("rstd", [128, 2], F32).ap()
        epsb = nc.alloc_sbuf_tensor("epsb", [128, 1], F32).ap()

        barv = [0]

        def barrier():
            engs = [nc.tensor, nc.vector, nc.scalar, nc.gpsimd, nc.sync]
            for e in engs:
                for name, cnt in S.c.items():
                    if cnt > 0 and name != "bar":
                        e.wait_ge(S.s[name], cnt)
            for e in engs:
                e.sem_inc(S.s["bar"], 1)
            barv[0] += 5
            for e in engs:
                e.wait_ge(S.s["bar"], barv[0])

        S.inc(nc.sync.dma_start(ident, id_in), "cld", 16)
        S.inc(nc.sync.dma_start(idxsb[:, 0:1],
                                bt_in[0:128].rearrange("(p a) -> p a", a=1)), "cld", 16)
        S.inc(nc.sync.dma_start(idxsb[:, 1:2],
                                bt_in[128:256].rearrange("(p a) -> p a", a=1)), "cld", 16)
        S.inc(nc.vector.memset(ones1, 1.0), "vself")
        S.inc(nc.vector.memset(epsb, EPS), "vself")
        S.wait(nc.vector, "cld", S.val("cld"))
        barrier()

        def ln_tile(src_ap, dst_ap):
            S.wait(nc.vector, "vself", S.val("vself"))
            S.wait(nc.vector, "lnd", S.val("lnd"))
            for i in range(8):
                S.inc(nc.vector.bn_stats(stats[:, 6 * i:6 * i + 6],
                                         src_ap[:, 512 * i:512 * (i + 1)]), "vself")
            S.wait(nc.vector, "vself", S.val("vself"))
            v = S.inc(nc.vector.bn_aggr(aggr, stats), "lnd")
            S.wait(nc.scalar, "lnd", v)
            S.wait(nc.scalar, "sself", S.val("sself"))
            S.inc(nc.scalar.activation(rstd[:, 0:1], aggr[:, 1:2], AF.Ln, bias=epsb),
                  "sself")
            S.wait(nc.scalar, "sself", S.val("sself"))
            v = S.inc(nc.scalar.activation(rstd[:, 1:2], rstd[:, 0:1], AF.Exp,
                                           scale=-0.5), "lnd")
            S.wait(nc.vector, "lnd", v)
            S.wait(nc.vector, "vself", S.val("vself"))
            return S.inc(nc.vector.tensor_scalar(dst_ap, src_ap, aggr[:, 0:1],
                                                 rstd[:, 1:2], ALU.subtract,
                                                 ALU.mult), "lnd")

        def transpose_to_bigt(src_bf, tc, trp):
            for g in range(8):
                gp = g % 2
                if S.val("trp") >= 2:
                    S.wait(nc.tensor, "trev", S.val("trev") - 1)
                for k in range(4):
                    dd = g * 4 + k
                    last = nc.tensor.transpose(trp[gp][:, k * 128:(k + 1) * 128],
                                               src_bf[:, dd * 128:(dd + 1) * 128], ident)
                v = S.inc(last, "trp")
                S.wait(nc.vector, "trp", v)
                dst = bass.AP(BIGT.tensor, BIGT.offset + (g * 4) * NT + tc * 128,
                              [[BIGT.ap[0][0], 128], [NT, 4], [1, 128]])
                S.inc(nc.vector.tensor_copy(
                    dst, trp[gp].rearrange("p (a b) -> p a b", b=128)), "trev")

        with (
            nc.sbuf_tensor("xnb", [128, D], BF16) as xnb_g,
            nc.psum_tensor("trp0", [128, 512], BF16) as trp0_g,
            nc.psum_tensor("trp1", [128, 512], BF16) as trp1_g,
        ):
            xnb = xnb_g.ap()
            trp = [trp0_g.ap(), trp1_g.ap()]
            lnd_of_tc = {}
            for tc in range(NTC):
                par = tc % 2
                xsem = "xlda" if par == 0 else "xldb"
                xt = SS[:, par * D:(par + 1) * D]
                if tc >= 2:
                    S.wait(nc.sync, "lnd", lnd_of_tc[tc - 2])
                v = S.inc(nc.sync.dma_start(xt, x_in[tc * 128:(tc + 1) * 128, :]),
                          xsem, 16)
                S.wait(nc.vector, xsem, v)
                if tc >= 1:
                    S.wait(nc.vector, "trev", S.val("trev"))
                lnd_of_tc[tc] = ln_tile(xt, xnb)
                S.wait(nc.tensor, "lnd", lnd_of_tc[tc])
                transpose_to_bigt(xnb, tc, trp)
        barrier()

        with (
            nc.sbuf_tensor("wqs0", [128, D], BF16) as wqs0_g,
            nc.sbuf_tensor("wqs1", [128, D], BF16) as wqs1_g,
            nc.sbuf_tensor("qst0", [128, NT], BF16) as qst0_g,
            nc.sbuf_tensor("qst1", [128, NT], BF16) as qst1_g,
            nc.psum_tensor("qps0", [128, NT], F32) as qps0_g,
            nc.psum_tensor("qps1", [128, NT], F32) as qps1_g,
        ):
            wqs = [wqs0_g.ap(), wqs1_g.ap()]
            qst = [qst0_g.ap(), qst1_g.ap()]
            qps = [qps0_g.ap(), qps1_g.ap()]
            cosT, sinT = SS[:, 0:NT], SS[:, NT:2 * NT]
            t1, t2 = SS[:, 2 * NT:3 * NT], SS[:, 3 * NT:4 * NT]
            S.inc(nc.sync.dma_start(cosT, cos_in), "xld", 16)
            vtab = S.inc(nc.sync.dma_start(sinT, sin_in), "xld", 16)
            for h in range(NH):
                par = h % 2
                wsem = "wqla" if par == 0 else "wqlb"
                if h >= 2:
                    S.wait(nc.gpsimd, "qmm", h - 1)
                v = S.inc(nc.gpsimd.dma_start(wqs[par], wq_in[h]), wsem, 16)
                S.wait(nc.tensor, wsem, v)
                if h >= 2:
                    S.wait(nc.tensor, "rope", (h - 1) * 4)
                for dd in range(ND):
                    lhsT = wqs[par][:, dd * 128:(dd + 1) * 128]
                    for (no, nw) in chunks(NT):
                        last = nc.tensor.matmul(
                            qps[par][:, no:no + nw], lhsT,
                            BIGT[:, dd * NT + no: dd * NT + no + nw],
                            start=(dd == 0), stop=(dd == ND - 1))
                v = S.inc(last, "qmm")
                S.wait(nc.vector, "qmm", v)
                S.wait(nc.vector, "xld", vtab)
                if h >= 2:
                    S.wait(nc.vector, "qspa" if par == 0 else "qspb",
                           (h // 2) * 16)
                S.wait(nc.vector, "vself", S.val("vself"))
                S.wait(nc.vector, "rope", S.val("rope"))
                S.inc(nc.vector.tensor_tensor(t1, qps[par], cosT, ALU.mult), "vself")
                S.inc(nc.vector.tensor_tensor(t2[0:64, :], qps[par][64:128, :],
                                              sinT[0:64, :], ALU.mult), "vself")
                S.inc(nc.vector.tensor_tensor(t2[64:128, :], qps[par][0:64, :],
                                              sinT[64:128, :], ALU.mult), "vself")
                S.wait(nc.vector, "vself", S.val("vself"))
                v = S.inc(nc.vector.tensor_tensor(qst[par], t1, t2, ALU.add),
                          "rope", 4)
                S.wait(nc.sync, "rope", v)
                S.inc(nc.sync.dma_start(qsp[h], qst[par]),
                      "qspa" if par == 0 else "qspb", 16)
            S.wait(nc.sync, "qspa", S.val("qspa"))
            S.wait(nc.sync, "qspb", S.val("qspb"))
        barrier()

        if True:
            aes = contextlib.ExitStack()
            _a = aes.enter_context
            SSB = SS.bitcast(BF16)
            kb = [SSB[:, 0:2048], SSB[:, 2048:4096]]
            vraw = SSB[:, 4096:6144]
            vp = [SSB[:, 6144 + i * 2064: 6144 + (i + 1) * 2064] for i in range(4)]
            KT = [_a(nc.sbuf_tensor(f"KT{i}", [128, 32 * 128], BF16)).ap()
                  for i in range(2)]
            qth = [_a(nc.sbuf_tensor(f"qth{i}", [128, NT], BF16)).ap()
                   for i in range(2)]
            cxn = _a(nc.sbuf_tensor("cxn", [128, TH], BF16)).ap()
            cxT = [_a(nc.sbuf_tensor(f"cxT{i}", [128, NT], BF16)).ap()
                   for i in range(2)]
            recb = _a(nc.sbuf_tensor("recb", [128, 1], F32)).ap()
            scps = [_a(nc.psum_tensor(f"scps{i}", [128, TH], F32)).ap()
                    for i in range(2)]
            cxb = [_a(nc.psum_tensor(f"cxb{i}", [128, 512], F32)).ap()
                   for i in range(3)]
            trpA = _a(nc.psum_tensor("trpA", [128, 512], BF16)).ap()

            c = AttnCtx()
            c.TH, c.NTH = TH, NTH
            c.Earea = [BIGT[:, 0:32 * TH], BIGT[:, 32 * TH:64 * TH]]
            c.KT, c.qth, c.cxn, c.cxT, c.recb = KT, qth, cxn, cxT, recb
            c.scps, c.cxb, c.trpA, c.vp = scps, cxb, trpA, vp
            c.ident, c.qsp, c.ctxsp = ident, qsp, ctxsp
            c.vq, c.expd_end, c.ctxg_end = {}, {}, {}
            c.scps_gate = [0, 0]
            c.trp_setup_end = {}
            c.scmm_kv_end = {}
            c.ctxg_kv_end = {}

            def attn_kv_setup(kv):
                kvp = kv % 2
                if kv > 0:
                    S.wait(nc.gpsimd, "trp", c.trp_setup_end[kv - 1])
                ksrc = bass.AP(kc_in.tensor, 0,
                               [[BS * HD, 512 * NKV], [HD, BS], [1, HD]])
                vsrc = bass.AP(vc_in.tensor, 0,
                               [[BS * HD, 512 * NKV], [HD, BS], [1, HD]])
                kbv = []
                for H in range(2):
                    kbv.append(S.inc(nc.gpsimd.indirect_dma_start(
                        kb[H], None, ksrc,
                        IndirectOffsetOnAxis(idxsb[:, H:H + 1], 0),
                        element_offset=kv * 512 * BS * HD),
                        "kvga" if H == 0 else "kvgb", 16))
                if kv > 1:
                    S.wait(nc.vector, "ctxg", c.ctxg_kv_end[kv - 2])
                for H in range(2):
                    S.wait(nc.gpsimd, "vone", S.val("vone"))
                    vv = S.inc(nc.gpsimd.indirect_dma_start(
                        vraw, None, vsrc,
                        IndirectOffsetOnAxis(idxsb[:, H:H + 1], 0),
                        element_offset=kv * 512 * BS * HD), "kvgv", 16)
                    S.wait(nc.vector, "kvgv", vv)
                    if kv < 2:
                        S.inc(nc.vector.memset(
                            vp[kvp * 2 + H].rearrange("p (s e) -> p s e", e=129)
                            [:, :, 128:129], 1.0), "vone", 1)
                    S.inc(nc.vector.tensor_copy(
                        vp[kvp * 2 + H].rearrange("p (s e) -> p s e", e=129)
                        [:, :, 0:128],
                        vraw.rearrange("p (s d) -> p s d", d=128)), "vone", 1)
                S.wait(nc.tensor, "kvga", kbv[0])
                S.wait(nc.tensor, "kvgb", kbv[1])
                S.wait(nc.tensor, "vone", S.val("vone"))
                for g in range(8):
                    if S.val("trp") >= 1:
                        S.wait(nc.tensor, "trev", S.val("trev"))
                    for k in range(4):
                        kt = g * 4 + k
                        s_, H_ = kt // 2, kt % 2
                        last = nc.tensor.transpose(
                            trpA[:, k * 128:(k + 1) * 128],
                            kb[H_][:, s_ * 128:(s_ + 1) * 128], ident)
                    v = S.inc(last, "trp")
                    S.wait(nc.vector, "trp", v)
                    if kv > 1 and g == 0:
                        S.wait(nc.vector, "scmm", c.scmm_kv_end[kv - 2])
                    S.inc(nc.vector.tensor_copy(KT[kvp][:, g * 512:(g + 1) * 512],
                                                trpA), "trev")
                c.trp_setup_end[kv] = S.val("trp")

            units = [(kv, hh, th) for kv in range(NKV) for hh in range(4)
                     for th in range(NTH)]
            prev = None
            for ui, u in enumerate(units):
                if u[1] == 0 and u[2] == 0:
                    if u[0] > 0:
                        c.scmm_kv_end[u[0] - 1] = S.val("scmm")
                        if ui >= 2:
                            c.ctxg_kv_end[u[0] - 1] = None
                    attn_kv_setup(u[0])
                attn_A_dummy = None
                attn_unit(nc, S, c, u, ui, prev)
                if prev is not None and prev[1] == 3 and prev[2] == NTH - 1:
                    c.ctxg_kv_end[prev[0]] = c.ctxg_end[ui - 1]
                prev = u
            lastui = len(units) - 1
            for tc in range(CTC):
                attn_ctx_tc(nc, S, c, prev, lastui, tc)
            attn_T(nc, S, c, prev, lastui)
            S.wait(nc.sync, "ctxsta", S.val("ctxsta"))
            S.wait(nc.sync, "ctxstb", S.val("ctxstb"))
            aes.close()
        barrier()

        for h in range(NH):
            S.inc(nc.sync.dma_start(BIGT[:, h * NT:(h + 1) * NT], ctxsp[h]),
                  "ctxld", 16)
        S.wait(nc.tensor, "ctxld", S.val("ctxld"))

        def out_proj(w_dram, dst_dram, tag, lora=False, hT=None, b0sb=None,
                     blsb=None, mtb=None, accb=None, yps=None, wsl_ap=None):
            with (
                contextlib.nullcontext() if wsl_ap is not None else
                nc.sbuf_tensor(f"wsl{tag}", [128, ND * OS_W], BF16) as wsl_g,
                nc.sbuf_tensor(f"ost{tag}0", [128, OS_W], BF16) as ost0_g,
                nc.sbuf_tensor(f"ost{tag}1", [128, OS_W], BF16) as ost1_g,
                nc.psum_tensor(f"wps{tag}0", [128, OS_W], F32) as wps0_g,
                nc.psum_tensor(f"wps{tag}1", [128, OS_W], F32) as wps1_g,
            ):
                wsl = wsl_ap if wsl_ap is not None else wsl_g.ap()
                ost = [ost0_g.ap(), ost1_g.ap()]
                wps = [wps0_g.ap(), wps1_g.ap()]
                nos = D // OS_W
                it = 0
                b_woev = S.val("woev")
                b_comb = S.val("comb")
                b_wosta = S.val("wosta")
                b_wostb = S.val("wostb")
                for os_ in range(nos):
                    if os_ > 0:
                        S.wait(nc.gpsimd, "womm", S.val("womm"))
                    QW = ND // 4 * OS_W
                    for q in range(4):
                        S.inc(nc.gpsimd.dma_start(
                            wsl[:, q * QW:(q + 1) * QW],
                            w_dram[os_][:, q * QW:(q + 1) * QW]), f"wsl{q}", 16)
                    if lora:
                        S.wait(nc.gpsimd, "ymm", S.val("ymm"))
                        for l in range(3):
                            S.inc(nc.gpsimd.dma_start(
                                blsb[0:16, l * OS_W:(l + 1) * OS_W],
                                bl_in[l + 1][:, os_ * OS_W:(os_ + 1) * OS_W]),
                                "bld", 16)
                        S.inc(nc.gpsimd.dma_start(
                            blsb[0:16, 3 * OS_W:4 * OS_W],
                            bl_in[0][:, os_ * OS_W:(os_ + 1) * OS_W]), "bld", 16)
                    for tc in range(NTC):
                        pw = it % 2
                        pm = it % 2
                        if lora:
                            S.wait(nc.sync, "comb", max(0, S.val("comb") - 1))
                            for l in range(3):
                                S.inc(nc.sync.dma_start(
                                    mtb[:, (pm * 3 + l) * OS_W:(pm * 3 + l + 1) * OS_W],
                                    m_in[l][tc * 128:(tc + 1) * 128,
                                            os_ * OS_W:(os_ + 1) * OS_W]), "mld", 16)
                            vml = S.val("mld")
                        if it >= 2:
                            S.wait(nc.tensor, "woev" if not lora else "comb",
                                   (b_woev if not lora else b_comb) + it - 1)
                        for dd in range(ND):
                            if tc == 0 and dd % (ND // 4) == 0:
                                S.wait(nc.tensor, f"wsl{dd // (ND // 4)}",
                                       S.val(f"wsl{dd // (ND // 4)}"))
                            last = nc.tensor.matmul(
                                wps[pw],
                                BIGT[:, dd * NT + tc * 128: dd * NT + (tc + 1) * 128],
                                wsl[:, dd * OS_W:(dd + 1) * OS_W],
                                start=(dd == 0), stop=(not lora) and (dd == ND - 1))
                        if lora:
                            S.wait(nc.tensor, "b0ld", 16)
                            last = nc.tensor.matmul(
                                wps[pw], ones1,
                                b0sb[0:1, os_ * OS_W:(os_ + 1) * OS_W],
                                start=False, stop=True)
                        vbase = S.inc(last, "womm")
                        if lora:
                            if it >= 1:
                                S.wait(nc.tensor, "comb", b_comb + it)
                            S.wait(nc.tensor, "bld", S.val("bld"))
                            S.wait(nc.tensor, "htev", S.val("htev"))
                            for sl in range(4):
                                l = (sl + 1) % 4
                                last = nc.tensor.matmul(
                                    yps[sl], hT[l][:, tc * 128:(tc + 1) * 128],
                                    blsb[0:16, sl * OS_W:(sl + 1) * OS_W],
                                    start=True, stop=True)
                            vy = S.inc(last, "ymm")
                            S.wait(nc.vector, "ymm", vy)
                            S.wait(nc.vector, "mld", vml)
                            S.wait(nc.vector, "vself", S.val("vself"))
                            S.wait(nc.vector, "comb", S.val("comb"))
                            S.inc(nc.vector.tensor_copy(accb, yps[3]), "vself")
                            for sl in range(3):
                                S.wait(nc.vector, "vself", S.val("vself"))
                                S.inc(nc.vector.copy_predicated(
                                    accb,
                                    mtb[:, (pm * 3 + sl) * OS_W:(pm * 3 + sl + 1) * OS_W],
                                    yps[sl]), "vself")
                            S.wait(nc.vector, "vself", S.val("vself"))
                            S.wait(nc.vector, "womm", vbase)
                            if it >= 2:
                                S.wait(nc.vector, "wosta" if pw == 0 else "wostb",
                                       (b_wosta if pw == 0 else b_wostb)
                                       + (it // 2) * 16)
                            vv = S.inc(nc.vector.tensor_tensor(ost[pw], accb,
                                                               wps[pw], ALU.add),
                                       "comb")
                        else:
                            S.wait(nc.vector, "womm", vbase)
                            if it >= 2:
                                S.wait(nc.vector, "wosta" if pw == 0 else "wostb",
                                       (b_wosta if pw == 0 else b_wostb)
                                       + (it // 2) * 16)
                            S.inc(nc.vector.tensor_copy(ost[pw], wps[pw]), "woev")
                        if lora:
                            S.wait(nc.sync, "comb", S.val("comb"))
                        else:
                            S.wait(nc.sync, "woev", S.val("woev"))
                        S.inc(nc.sync.dma_start(
                            dst_dram[tc * 128:(tc + 1) * 128,
                                     os_ * OS_W:(os_ + 1) * OS_W], ost[pw]),
                            "wosta" if pw == 0 else "wostb", 16)
                        it += 1
                S.wait(nc.sync, "wosta", S.val("wosta"))
                S.wait(nc.sync, "wostb", S.val("wostb"))

        out_proj(wo_in, attnsp, "o")
        barrier()

        with (
            nc.sbuf_tensor("hT0", [16, NTC * 128], BF16) as hT0_g,
            nc.sbuf_tensor("hT1", [16, NTC * 128], BF16) as hT1_g,
            nc.sbuf_tensor("hT2", [16, NTC * 128], BF16) as hT2_g,
            nc.sbuf_tensor("hT3", [16, NTC * 128], BF16) as hT3_g,
        ):
            hT = [hT0_g.ap(), hT1_g.ap(), hT2_g.ap(), hT3_g.ap()]
            p5 = contextlib.ExitStack()
            _p = p5.enter_context
            x2b = _p(nc.sbuf_tensor("x2b", [128, D], F32)).ap()
            xnb2 = _p(nc.sbuf_tensor("xnb2", [128, D], BF16)).ap()
            atsb = _p(nc.sbuf_tensor("atsb", [128, ND * 64], BF16)).ap()
            h8b = _p(nc.sbuf_tensor("h8b", [128, 64], BF16)).ap()
            trpb = [_p(nc.psum_tensor("trp0b", [128, 512], BF16)).ap(),
                    _p(nc.psum_tensor("trp1b", [128, 512], BF16)).ap()]
            hps = _p(nc.psum_tensor("hps", [128, 64], F32)).ap()
            trpC = _p(nc.psum_tensor("trpC", [128, 512], BF16)).ap()
            vat = S.inc(nc.gpsimd.dma_start(atsb, at_in), "hld", 16)
            prev_res = 0
            at_bf = SS[:, 0:D // 2].bitcast(BF16)
            for tc in range(NTC):
                at_t, xt_t = at_bf, SS[:, D:2 * D]
                if tc >= 1:
                    S.wait(nc.sync, "lnd", prev_res)
                    S.wait(nc.sync, "outst", tc * 16)
                S.inc(nc.sync.dma_start(at_t, attnsp[tc * 128:(tc + 1) * 128, :]),
                      "xld", 16)
                v1 = S.inc(nc.sync.dma_start(xt_t, x_in[tc * 128:(tc + 1) * 128, :]),
                           "xld", 16)
                S.wait(nc.vector, "xld", v1)
                if tc >= 1:
                    S.wait(nc.vector, "outst", tc * 16)
                rl = ln_tile(at_t, x2b)
                S.wait(nc.vector, "lnd", rl)
                v = S.inc(nc.vector.tensor_tensor(x2b, x2b, xt_t, ALU.add), "lnd")
                S.wait(nc.sync, "lnd", v)
                S.inc(nc.sync.dma_start(x2sp[tc * 128:(tc + 1) * 128, :], x2b),
                      "outst", 16)
                if tc >= 1:
                    S.wait(nc.vector, "trev", S.val("trev"))
                prev_res = ln_tile(x2b, xnb2)
                S.wait(nc.tensor, "lnd", prev_res)
                transpose_to_bigt(xnb2, tc, trpb)
                S.wait(nc.tensor, "trev", S.val("trev"))
                if tc == 0:
                    S.wait(nc.tensor, "hld", vat)
                if tc >= 1:
                    S.wait(nc.tensor, "hact", tc)
                for dd in range(ND):
                    last = nc.tensor.matmul(
                        hps, BIGT[:, dd * NT + tc * 128: dd * NT + (tc + 1) * 128],
                        atsb[:, dd * 64:(dd + 1) * 64],
                        start=(dd == 0), stop=(dd == ND - 1))
                v = S.inc(last, "hmm")
                S.wait(nc.scalar, "hmm", v)
                if tc >= 1:
                    S.wait(nc.scalar, "htr", tc)
                v = S.inc(nc.scalar.activation(h8b, hps, AF.Copy, scale=8.0),
                          "hact")
                S.wait(nc.tensor, "hact", v)
                if tc >= 1:
                    S.wait(nc.tensor, "htev", tc)
                for l in range(4):
                    last = nc.tensor.transpose(trpC[0:16, l * 128:(l + 1) * 128],
                                               h8b[:, l * 16:(l + 1) * 16], ident)
                v = S.inc(last, "htr")
                S.wait(nc.vector, "htr", v)
                for l in range(4):
                    last = nc.vector.tensor_copy(
                        hT[l][:, tc * 128:(tc + 1) * 128],
                        trpC[0:16, l * 128:(l + 1) * 128])
                S.inc(last, "htev")
            S.wait(nc.sync, "outst", S.val("outst"))
            S.wait(nc.tensor, "htev", S.val("htev"))
            p5.close()
            barrier()

            with (
                nc.sbuf_tensor("b0sbf", [1, D], BF16) as b0sb_g,
                nc.sbuf_tensor("blsb", [16, 4 * OS_W], BF16) as blsb_g,
                nc.sbuf_tensor("mtb", [128, 6 * OS_W], U8) as mtb_g,
                nc.sbuf_tensor("accb", [128, OS_W], F32) as accb_g,
                nc.psum_tensor("yps0", [128, OS_W], F32) as yps0_g,
                nc.psum_tensor("yps1", [128, OS_W], F32) as yps1_g,
                nc.psum_tensor("yps2", [128, OS_W], F32) as yps2_g,
                nc.psum_tensor("yps3", [128, OS_W], F32) as yps3_g,
            ):
                b0sb = b0sb_g.ap()
                S.inc(nc.gpsimd.dma_start(b0sb, b0_in), "b0ld", 16)
                out_proj(w0_in, ffnsp, "f", lora=True, hT=hT, b0sb=b0sb,
                         blsb=blsb_g.ap(),
                         mtb=mtb_g.ap(), accb=accb_g.ap(),
                         yps=[yps0_g.ap(), yps1_g.ap(), yps2_g.ap(), yps3_g.ap()],
                         wsl_ap=SS.bitcast(BF16))
        barrier()

        if True:
            p7 = contextlib.ExitStack()
            if ND * NT >= 65536:
                BF32 = BIGT.bitcast(F32)
                fts = [BIGT[:, i * 4096:(i + 1) * 4096] for i in range(4)]
                x2s = [BF32[:, 8192 + i * 4096: 8192 + (i + 1) * 4096]
                       for i in range(4)]
            else:
                fts = [p7.enter_context(nc.sbuf_tensor(f"ft{i}", [128, D], BF16)).ap()
                       for i in range(2)] * 2
                x2s = [p7.enter_context(nc.sbuf_tensor(f"x2{i}", [128, D], F32)).ap()
                       for i in range(2)] * 2
            outb = [SS[:, 0:D], SS[:, D:2 * D]]
            nslot = 4 if ND * NT >= 65536 else 2
            xsems = ["xlda", "xldb", "xldc", "xldd"][:nslot]
            lnd_fin = {}
            for tc in range(NTC):
                sl = tc % nslot
                par = tc % 2
                xsem = xsems[sl]
                ft, x2t = fts[sl], x2s[sl]
                if tc >= nslot:
                    S.wait(nc.sync, "lnd", lnd_fin[tc - nslot])
                    S.wait(nc.scalar, "lnd", lnd_fin[tc - nslot])
                S.inc(nc.scalar.dma_start(ft, ffnsp[tc * 128:(tc + 1) * 128, :]),
                      xsem, 16)
                v1 = S.inc(nc.sync.dma_start(x2t, x2sp[tc * 128:(tc + 1) * 128, :]),
                           xsem, 16)
                S.wait(nc.vector, xsem, v1)
                if tc >= 2:
                    S.wait(nc.vector, "fina" if par == 0 else "finb",
                           (tc // 2) * 16)
                rl = ln_tile(ft, outb[par])
                S.wait(nc.vector, "lnd", rl)
                lnd_fin[tc] = S.inc(
                    nc.vector.tensor_tensor(outb[par], outb[par], x2t, ALU.add),
                    "lnd")
                S.wait(nc.sync, "lnd", lnd_fin[tc])
                S.inc(nc.sync.dma_start(out_d[tc * 128:(tc + 1) * 128, :],
                                        outb[par]), "fina" if par == 0 else "finb", 16)
            S.wait(nc.sync, "fina", S.val("fina"))
            S.wait(nc.sync, "finb", S.val("finb"))
            p7.close()
        barrier()

    nc.finalize()
    return nc




def _threefry_uniform_key42(n):
    def rotl(x, d):
        return ((x << np.uint32(d)) | (x >> np.uint32(32 - d))).astype(np.uint32)

    def tf2x32(k0, k1, x0, x1):
        ks0, ks1 = np.uint32(k0), np.uint32(k1)
        ks2 = np.uint32(ks0 ^ ks1 ^ np.uint32(0x1BD11BDA))
        x0 = (x0 + ks0).astype(np.uint32)
        x1 = (x1 + ks1).astype(np.uint32)
        rots = [[13, 15, 26, 6], [17, 29, 16, 24]]
        ks = [ks1, ks2, ks0]
        for i in range(5):
            for r in rots[i % 2]:
                x0 = (x0 + x1).astype(np.uint32)
                x1 = rotl(x1, r)
                x1 = (x1 ^ x0).astype(np.uint32)
            x0 = (x0 + ks[i % 3]).astype(np.uint32)
            x1 = (x1 + ks[(i + 1) % 3] + np.uint32(i + 1)).astype(np.uint32)
        return x0, x1

    idx = np.arange(n, dtype=np.uint64)
    lo = (idx & np.uint64(0xFFFFFFFF)).astype(np.uint32)
    hi = (idx >> np.uint64(32)).astype(np.uint32)
    o0, o1 = tf2x32(0, 42, hi, lo)
    bits = o0 ^ o1
    return ((bits >> np.uint32(9)) | np.uint32(0x3F800000)).view(np.float32) - np.float32(1.0)


def _host_prep(Wq, Wo, W0, A, Bm, b0):
    WQH = np.ascontiguousarray(
        Wq.reshape(NH, 128, ND, 128).transpose(0, 3, 2, 1).reshape(NH, 128, D))
    nos = D // OS_W
    WOH = np.ascontiguousarray(
        Wo.reshape(nos, OS_W, ND, 128).transpose(0, 3, 2, 1).reshape(nos, 128, ND * OS_W))
    W0H = np.ascontiguousarray(
        W0.reshape(nos, OS_W, ND, 128).transpose(0, 3, 2, 1).reshape(nos, 128, ND * OS_W))
    ATH = np.ascontiguousarray(
        A.reshape(4, 16, ND, 128).transpose(3, 2, 0, 1).reshape(128, ND * 64))
    BLT = np.ascontiguousarray(Bm.transpose(0, 2, 1))
    b0h = np.ascontiguousarray(b0.reshape(1, D))

    Sq = 8192
    inv = (1.0 / (np.float32(10000.0) ** (np.arange(0, HD, 2, dtype=np.float32)
                                          / np.float32(HD)))).astype(np.float32)
    t = np.arange(Sq, dtype=np.float32)
    freqs = np.outer(t, inv)
    emb = np.concatenate([freqs, freqs], axis=-1)
    cos = np.cos(emb).astype(np.float32)
    sinf = np.sin(emb).astype(np.float32)
    sinf = sinf.copy()
    sinf[:, 0:64] *= -1.0

    u = _threefry_uniform_key42(2 * 8192 * D).reshape(16384, D)
    masks = []
    for l in range(1, 4):
        lo, hi = np.float32(0.25 * l), np.float32(0.25 * (l + 1))
        masks.append(((u >= lo) & (u < hi)).astype(np.uint8))

    ident = np.eye(128, dtype=BF)
    return WQH, WOH, W0H, ATH, BLT, b0h, cos, sinf, masks, ident


def make_in_maps(L0_attn_hidden_states, L0_attn_k_cache, L0_attn_v_cache, L0_attn_Wq,
                 L0_attn_Wo, L0_attn_block_table, L0_ffn_W0, L0_ffn_b0, L0_ffn_A,
                 L0_ffn_B, NT):
    WQH, WOH, W0H, ATH, BLT, b0h, cos, sinf, masks, ident = _host_prep(
        np.asarray(L0_attn_Wq, np.float32), np.asarray(L0_attn_Wo, np.float32),
        np.asarray(L0_ffn_W0, np.float32), np.asarray(L0_ffn_A, np.float32),
        np.asarray(L0_ffn_B, np.float32), np.asarray(L0_ffn_b0, np.float32))
    X = np.ascontiguousarray(
        np.asarray(L0_attn_hidden_states, np.float32).reshape(16384, D))
    kc = np.ascontiguousarray(np.asarray(L0_attn_k_cache, np.float32).transpose(1, 0, 2, 3))
    vc = np.ascontiguousarray(np.asarray(L0_attn_v_cache, np.float32).transpose(1, 0, 2, 3))
    bt = np.ascontiguousarray(np.asarray(L0_attn_block_table, np.int32))

    in_maps = []
    for c in range(N_CORES):
        r0 = c * NT
        p0 = r0 % 8192
        m = {
            "x": X[r0:r0 + NT], "wq": WQH, "wo": WOH, "w0": W0H,
            "kc": kc, "vc": vc, "bt": bt, "at": ATH, "bl": BLT, "b0": b0h,
            "cosT": np.ascontiguousarray(cos[p0:p0 + NT].T),
            "sinT": np.ascontiguousarray(sinf[p0:p0 + NT].T),
            "ident": ident,
        }
        for l in range(3):
            m[f"m{l + 1}"] = np.ascontiguousarray(masks[l][r0:r0 + NT])
        in_maps.append(m)
    return in_maps


def kernel(L0_attn_hidden_states, L0_attn_k_cache, L0_attn_v_cache, L0_attn_Wq,
           L0_attn_Wo, L0_attn_block_table, L0_ffn_W0, L0_ffn_b0, L0_ffn_A, L0_ffn_B):
    NT = 16384 // N_CORES
    in_maps = make_in_maps(L0_attn_hidden_states, L0_attn_k_cache, L0_attn_v_cache,
                           L0_attn_Wq, L0_attn_Wo, L0_attn_block_table, L0_ffn_W0,
                           L0_ffn_b0, L0_ffn_A, L0_ffn_B, NT)
    nc = build_program(NT)
    res = run_bass_kernel_spmd(nc, in_maps, core_ids=list(range(N_CORES)))
    out = np.concatenate([res.results[c]["out"] for c in range(N_CORES)], axis=0)
    return out.reshape(2, 8192, D).astype(np.float32)


# version 16
# speedup vs baseline: 1.1209x; 1.1209x over previous
import contextlib
import numpy as np
import ml_dtypes
import concourse.bass as bass
import concourse.mybir as mybir
from concourse.bass import IndirectOffsetOnAxis
from concourse.bass_utils import run_bass_kernel_spmd
import concourse.bass_utils as _bu

_orig_run_command = _bu.run_command

def _run_command_ldwopt(cmd, *a, **k):
    cmd = ["--enable-ldw-opt=true" if c == "--enable-ldw-opt=false" else c
           for c in cmd]
    return _orig_run_command(cmd, *a, **k)

_bu.run_command = _run_command_ldwopt

F32 = mybir.dt.float32
BF16 = mybir.dt.bfloat16
I32 = mybir.dt.int32
U8 = mybir.dt.uint8
AF = mybir.ActivationFunctionType
ALU = mybir.AluOpType
BF = ml_dtypes.bfloat16

N_CORES = 8
D = 4096
NH = 32
NKV = 8
HD = 128
NBLK = 256
BS = 16
EPS = 1e-5
SCALE = 1.0 / float(np.sqrt(128.0))
ND = D // 128
OS_W = 512


def chunks(total, step=512):
    out = []
    o = 0
    while o < total:
        w = min(step, total - o)
        out.append((o, w))
        o += w
    return out


class Sems:
    def __init__(self, nc, stack):
        self.nc, self.stack, self.s, self.c = nc, stack, {}, {}

    def new(self, *names):
        for n in names:
            self.s[n] = self.stack.enter_context(self.nc.semaphore(n))
            self.c[n] = 0

    def inc(self, inst, name, d=1):
        inst.then_inc(self.s[name], d)
        self.c[name] += d
        return self.c[name]

    def inc2(self, inst, n1, n2, d2=1):
        inst.then_inc(self.s[n1], 1)
        self.c[n1] += 1
        inst.then_inc(self.s[n2], d2)
        self.c[n2] += d2
        return self.c[n2]

    def val(self, name):
        return self.c[name]

    def wait(self, eng, name, v):
        if v > 0:
            eng.wait_ge(self.s[name], v)


class AttnCtx:
    pass


def attn_scores_kt(nc, S, c, u, ui, kt):
    kv, hh, th = u
    h = kv * 4 + hh
    parh = h % 2
    pk = kt % 2
    E = c.Earea[ui % 2]
    S.wait(nc.tensor, "expd", c.scps_gate[pk])
    for (no, nw) in chunks(c.TH):
        last = nc.tensor.matmul(
            c.scps[pk][:, no:no + nw],
            c.KT[kv % 2][:, kt * 128:(kt + 1) * 128],
            c.qth[parh][:, th * c.TH + no: th * c.TH + no + nw],
            start=True, stop=True)
    v2 = S.inc(last, "scmm")
    S.wait(nc.scalar, "scmm", v2)
    c.scps_gate[pk] = S.inc(
        nc.scalar.activation(E[:, kt * c.TH:(kt + 1) * c.TH],
                             c.scps[pk], AF.Exp, scale=SCALE), "expd")


def attn_ctx_tc(nc, S, c, u, ui, tc, ktr=(0, 32)):
    kv, hh, th = u
    E = c.Earea[ui % 2]
    CTC = c.TH // 128
    bk, slot = tc // 3, tc % 3
    if tc == 0 and ktr[0] == 0:
        S.wait(nc.tensor, "cxev", S.val("cxev"))
    for kt in range(*ktr):
        if tc == 0:
            S.wait(nc.tensor, "expd", c.expd_end[ui] - 31 + kt)
        last = nc.tensor.matmul(
            c.cxb[bk][:, slot * 129:slot * 129 + 129],
            E[:, kt * c.TH + tc * 128: kt * c.TH + (tc + 1) * 128],
            c.vp[(kv % 2) * 2 + kt % 2][:, (kt // 2) * 129:(kt // 2) * 129 + 129],
            start=(kt == 0), stop=(kt == 31))
    if ktr[1] < 32:
        return
    v3 = S.inc(last, "ctxg")
    if slot == 2 or tc == CTC - 1:
        S.wait(nc.vector, "ctxg", v3)
        for tcc in range(bk * 3, tc + 1):
            sl = tcc % 3
            S.wait(nc.vector, "vself", S.val("vself"))
            S.wait(nc.vector, "cxev", S.val("cxev"))
            S.inc(nc.vector.reciprocal(
                c.recb, c.cxb[bk][:, sl * 129 + 128: sl * 129 + 129]), "vself")
            S.wait(nc.vector, "vself", S.val("vself"))
            inst = nc.vector.tensor_scalar(
                c.cxn[:, tcc * 128:(tcc + 1) * 128],
                c.cxb[bk][:, sl * 129: sl * 129 + 128],
                c.recb, None, ALU.mult)
            S.inc(inst, "cxev" if tcc == tc else "vself")
    if tc == CTC - 1:
        c.ctxg_end[ui] = S.val("ctxg")


def attn_T(nc, S, c, u, ui):
    kv, hh, th = u
    h = kv * 4 + hh
    parh = h % 2
    CTC = c.TH // 128
    S.wait(nc.tensor, "cxev", S.val("cxev"))
    for g in range((CTC + 3) // 4):
        S.wait(nc.tensor, "trev", S.val("trev"))
        n_in_g = min(4, CTC - g * 4)
        for k in range(n_in_g):
            tc = g * 4 + k
            last = nc.tensor.transpose(c.trpA[:, k * 128:(k + 1) * 128],
                                       c.cxn[:, tc * 128:(tc + 1) * 128], c.ident)
        v4 = S.inc(last, "trp")
        S.wait(nc.vector, "trp", v4)
        if th == 0 and g == 0 and h >= 2:
            S.wait(nc.vector, "ctxsta" if parh == 0 else "ctxstb",
                   (h // 2) * 16)
        dst = bass.AP(c.cxT[parh].tensor,
                      c.cxT[parh].offset + th * c.TH + g * 4 * 128,
                      [[c.cxT[parh].ap[0][0], 128], [128, n_in_g], [1, 128]])
        S.inc(nc.vector.tensor_copy(
            dst, c.trpA.rearrange("p (a b) -> p a b", b=128)[:, 0:n_in_g, :]),
            "trev")
    if th == c.NTH - 1:
        S.wait(nc.sync, "trev", S.val("trev"))
        S.inc(nc.sync.dma_start(c.ctxsp[h], c.cxT[parh]),
              "ctxsta" if parh == 0 else "ctxstb", 16)


def attn_unit(nc, S, c, u, ui, prev):
    kv, hh, th = u
    h = kv * 4 + hh
    parh = h % 2
    qsem = "qlda" if parh == 0 else "qldb"
    if th == 0:
        if h >= 2:
            S.wait(nc.sync, "scmm", S.val("scmm"))
        c.vq[h] = S.inc(nc.sync.dma_start(c.qth[parh], c.qsp[h]), qsem, 16)
    S.wait(nc.tensor, qsem, c.vq[h])
    if hh == 0 and th == 0:
        S.wait(nc.tensor, "trev", S.val("trev"))
    if ui >= 2:
        S.wait(nc.scalar, "ctxg", c.ctxg_end[ui - 2])
    CTC = c.TH // 128
    for j in range(16):
        for kt in range(2 * j, 2 * j + 2):
            attn_scores_kt(nc, S, c, u, ui, kt)
            if kt == 31:
                c.expd_end[ui] = S.val("expd")
        if prev is not None and j // 2 < CTC:
            attn_ctx_tc(nc, S, c, prev, ui - 1, j // 2,
                        (0, 16) if j % 2 == 0 else (16, 32))
    if prev is not None:
        attn_T(nc, S, c, prev, ui - 1)


def build_program(NT=2048, debug=False):
    assert NT % 256 == 0
    NTC = NT // 128
    TH = min(1024, NT // 2)
    NTH = NT // TH
    CTC = TH // 128
    NS = NT // 512

    nc = bass.Bass("TRN2", target_bir_lowering=False)

    x_in = nc.dram_tensor("x", [NT, D], F32, kind="ExternalInput").ap()
    wq_in = nc.dram_tensor("wq", [NH, 128, D], F32, kind="ExternalInput").ap()
    wo_in = nc.dram_tensor("wo", [D // OS_W, 128, ND * OS_W], F32, kind="ExternalInput").ap()
    w0_in = nc.dram_tensor("w0", [D // OS_W, 128, ND * OS_W], F32, kind="ExternalInput").ap()
    kc_in = nc.dram_tensor("kc", [NKV, 512, BS, HD], F32, kind="ExternalInput").ap()
    vc_in = nc.dram_tensor("vc", [NKV, 512, BS, HD], F32, kind="ExternalInput").ap()
    bt_in = nc.dram_tensor("bt", [NBLK], I32, kind="ExternalInput").ap()
    at_in = nc.dram_tensor("at", [128, ND * 64], F32, kind="ExternalInput").ap()
    bl_in = nc.dram_tensor("bl", [4, 16, D], F32, kind="ExternalInput").ap()
    b0_in = nc.dram_tensor("b0", [1, D], F32, kind="ExternalInput").ap()
    cos_in = nc.dram_tensor("cosT", [128, NT], F32, kind="ExternalInput").ap()
    sin_in = nc.dram_tensor("sinT", [128, NT], F32, kind="ExternalInput").ap()
    m_in = [nc.dram_tensor(f"m{l}", [NT, D], U8, kind="ExternalInput").ap()
            for l in range(1, 4)]
    id_in = nc.dram_tensor("ident", [128, 128], BF16, kind="ExternalInput").ap()
    out_d = nc.dram_tensor("out", [NT, D], F32, kind="ExternalOutput").ap()

    sk = "ExternalOutput" if debug else "Internal"
    qsp = nc.dram_tensor("qsp", [NH, 128, NT], BF16, kind=sk).ap()
    ctxsp = nc.dram_tensor("ctxsp", [NH, 128, NT], BF16, kind=sk).ap()
    attnsp = nc.dram_tensor("attnsp", [NT, D], BF16, kind=sk).ap()
    x2sp = nc.dram_tensor("x2sp", [NT, D], BF16, kind=sk).ap()
    ffnsp = nc.dram_tensor("ffnsp", [NT, D], BF16, kind=sk).ap()

    stack = contextlib.ExitStack()
    with stack:
        S = Sems(nc, stack)
        S.new("bar", "cld", "xld", "lnd", "trp", "trev",
              "wql", "qmm", "rope", "qspill",
              "kvg", "qld", "scmm", "expd", "ctxg", "cxev", "ctxst",
              "ctxld", "wsl", "womm", "woev", "wost",
              "hld", "hmm", "hact", "htr", "htev",
              "bld", "ymm", "mld", "comb", "outst", "b0ld", "vone",
              "xlda", "xldb", "wqla", "wqlb", "qlda", "qldb",
              "ctxsta", "ctxstb", "qspa", "qspb", "wosta", "wostb",
              "wsl0", "wsl1", "wsl2", "wsl3", "vself", "sself", "fina", "finb",
              "kvga", "kvgb", "kvgv", "xldc", "xldd")

        BIGT = nc.alloc_sbuf_tensor("BIGT", [128, ND * NT], BF16).ap()
        SS = nc.alloc_sbuf_tensor("SS", [128, 2 * D], F32).ap()
        ident = nc.alloc_sbuf_tensor("identsb", [128, 128], BF16).ap()
        ones1 = nc.alloc_sbuf_tensor("ones1", [1, 128], BF16).ap()
        idxsb = nc.alloc_sbuf_tensor("idxsb", [128, 2], I32).ap()
        stats = nc.alloc_sbuf_tensor("stats", [128, 48], F32).ap()
        aggr = nc.alloc_sbuf_tensor("aggr", [128, 2], F32).ap()
        rstd = nc.alloc_sbuf_tensor("rstd", [128, 2], F32).ap()
        epsb = nc.alloc_sbuf_tensor("epsb", [128, 1], F32).ap()

        barv = [0]

        def barrier():
            engs = [nc.tensor, nc.vector, nc.scalar, nc.gpsimd, nc.sync]
            for e in engs:
                for name, cnt in S.c.items():
                    if cnt > 0 and name != "bar":
                        e.wait_ge(S.s[name], cnt)
            for e in engs:
                e.sem_inc(S.s["bar"], 1)
            barv[0] += 5
            for e in engs:
                e.wait_ge(S.s["bar"], barv[0])

        S.inc(nc.sync.dma_start(ident, id_in), "cld", 16)
        S.inc(nc.sync.dma_start(idxsb[:, 0:1],
                                bt_in[0:128].rearrange("(p a) -> p a", a=1)), "cld", 16)
        S.inc(nc.sync.dma_start(idxsb[:, 1:2],
                                bt_in[128:256].rearrange("(p a) -> p a", a=1)), "cld", 16)
        S.inc(nc.vector.memset(ones1, 1.0), "vself")
        S.inc(nc.vector.memset(epsb, EPS), "vself")
        S.wait(nc.vector, "cld", S.val("cld"))
        barrier()

        def ln_tile(src_ap, dst_ap):
            S.wait(nc.vector, "vself", S.val("vself"))
            S.wait(nc.vector, "lnd", S.val("lnd"))
            for i in range(8):
                S.inc(nc.vector.bn_stats(stats[:, 6 * i:6 * i + 6],
                                         src_ap[:, 512 * i:512 * (i + 1)]), "vself")
            S.wait(nc.vector, "vself", S.val("vself"))
            v = S.inc(nc.vector.bn_aggr(aggr, stats), "lnd")
            S.wait(nc.scalar, "lnd", v)
            S.wait(nc.scalar, "sself", S.val("sself"))
            S.inc(nc.scalar.activation(rstd[:, 0:1], aggr[:, 1:2], AF.Ln, bias=epsb),
                  "sself")
            S.wait(nc.scalar, "sself", S.val("sself"))
            v = S.inc(nc.scalar.activation(rstd[:, 1:2], rstd[:, 0:1], AF.Exp,
                                           scale=-0.5), "lnd")
            S.wait(nc.vector, "lnd", v)
            S.wait(nc.vector, "vself", S.val("vself"))
            return S.inc(nc.vector.tensor_scalar(dst_ap, src_ap, aggr[:, 0:1],
                                                 rstd[:, 1:2], ALU.subtract,
                                                 ALU.mult), "lnd")

        def transpose_to_bigt(src_bf, tc, trp):
            for g in range(8):
                gp = g % 2
                if S.val("trp") >= 2:
                    S.wait(nc.tensor, "trev", S.val("trev") - 1)
                for k in range(4):
                    dd = g * 4 + k
                    last = nc.tensor.transpose(trp[gp][:, k * 128:(k + 1) * 128],
                                               src_bf[:, dd * 128:(dd + 1) * 128], ident)
                v = S.inc(last, "trp")
                S.wait(nc.vector, "trp", v)
                dst = bass.AP(BIGT.tensor, BIGT.offset + (g * 4) * NT + tc * 128,
                              [[BIGT.ap[0][0], 128], [NT, 4], [1, 128]])
                S.inc(nc.vector.tensor_copy(
                    dst, trp[gp].rearrange("p (a b) -> p a b", b=128)), "trev")

        with (
            nc.sbuf_tensor("xnb", [128, D], BF16) as xnb_g,
            nc.psum_tensor("trp0", [128, 512], BF16) as trp0_g,
            nc.psum_tensor("trp1", [128, 512], BF16) as trp1_g,
        ):
            xnb = xnb_g.ap()
            trp = [trp0_g.ap(), trp1_g.ap()]
            lnd_of_tc = {}
            for tc in range(NTC):
                par = tc % 2
                xsem = "xlda" if par == 0 else "xldb"
                xt = SS[:, par * D:(par + 1) * D]
                if tc >= 2:
                    S.wait(nc.sync, "lnd", lnd_of_tc[tc - 2])
                v = S.inc(nc.sync.dma_start(xt, x_in[tc * 128:(tc + 1) * 128, :]),
                          xsem, 16)
                S.wait(nc.vector, xsem, v)
                if tc >= 1:
                    S.wait(nc.vector, "trev", S.val("trev"))
                lnd_of_tc[tc] = ln_tile(xt, xnb)
                S.wait(nc.tensor, "lnd", lnd_of_tc[tc])
                transpose_to_bigt(xnb, tc, trp)
        barrier()

        with (
            nc.sbuf_tensor("wqs0", [128, D], BF16) as wqs0_g,
            nc.sbuf_tensor("wqs1", [128, D], BF16) as wqs1_g,
            nc.sbuf_tensor("qst0", [128, NT], BF16) as qst0_g,
            nc.sbuf_tensor("qst1", [128, NT], BF16) as qst1_g,
            nc.psum_tensor("qps0", [128, NT], F32) as qps0_g,
            nc.psum_tensor("qps1", [128, NT], F32) as qps1_g,
        ):
            wqs = [wqs0_g.ap(), wqs1_g.ap()]
            qst = [qst0_g.ap(), qst1_g.ap()]
            qps = [qps0_g.ap(), qps1_g.ap()]
            cosT, sinT = SS[:, 0:NT], SS[:, NT:2 * NT]
            t1, t2 = SS[:, 2 * NT:3 * NT], SS[:, 3 * NT:4 * NT]
            S.inc(nc.sync.dma_start(cosT, cos_in), "xld", 16)
            vtab = S.inc(nc.sync.dma_start(sinT, sin_in), "xld", 16)
            for h in range(NH):
                par = h % 2
                wsem = "wqla" if par == 0 else "wqlb"
                if h >= 2:
                    S.wait(nc.gpsimd, "qmm", h - 1)
                v = S.inc(nc.gpsimd.dma_start(wqs[par], wq_in[h]), wsem, 16)
                S.wait(nc.tensor, wsem, v)
                if h >= 2:
                    S.wait(nc.tensor, "rope", (h - 1) * 4)
                for dd in range(ND):
                    lhsT = wqs[par][:, dd * 128:(dd + 1) * 128]
                    for (no, nw) in chunks(NT):
                        last = nc.tensor.matmul(
                            qps[par][:, no:no + nw], lhsT,
                            BIGT[:, dd * NT + no: dd * NT + no + nw],
                            start=(dd == 0), stop=(dd == ND - 1))
                v = S.inc(last, "qmm")
                S.wait(nc.vector, "qmm", v)
                S.wait(nc.vector, "xld", vtab)
                if h >= 2:
                    S.wait(nc.vector, "qspa" if par == 0 else "qspb",
                           (h // 2) * 16)
                S.wait(nc.vector, "vself", S.val("vself"))
                S.wait(nc.vector, "rope", S.val("rope"))
                S.inc(nc.vector.tensor_tensor(t1, qps[par], cosT, ALU.mult), "vself")
                S.inc(nc.vector.tensor_tensor(t2[0:64, :], qps[par][64:128, :],
                                              sinT[0:64, :], ALU.mult), "vself")
                S.inc(nc.vector.tensor_tensor(t2[64:128, :], qps[par][0:64, :],
                                              sinT[64:128, :], ALU.mult), "vself")
                S.wait(nc.vector, "vself", S.val("vself"))
                v = S.inc(nc.vector.tensor_tensor(qst[par], t1, t2, ALU.add),
                          "rope", 4)
                S.wait(nc.sync, "rope", v)
                S.inc(nc.sync.dma_start(qsp[h], qst[par]),
                      "qspa" if par == 0 else "qspb", 16)
            S.wait(nc.sync, "qspa", S.val("qspa"))
            S.wait(nc.sync, "qspb", S.val("qspb"))
        barrier()

        if True:
            aes = contextlib.ExitStack()
            _a = aes.enter_context
            SSB = SS.bitcast(BF16)
            kb = [SSB[:, 0:2048], SSB[:, 2048:4096]]
            vraw = SSB[:, 4096:6144]
            vp = [SSB[:, 6144 + i * 2064: 6144 + (i + 1) * 2064] for i in range(4)]
            KT = [_a(nc.sbuf_tensor(f"KT{i}", [128, 32 * 128], BF16)).ap()
                  for i in range(2)]
            qth = [_a(nc.sbuf_tensor(f"qth{i}", [128, NT], BF16)).ap()
                   for i in range(2)]
            cxn = _a(nc.sbuf_tensor("cxn", [128, TH], BF16)).ap()
            cxT = [_a(nc.sbuf_tensor(f"cxT{i}", [128, NT], BF16)).ap()
                   for i in range(2)]
            recb = _a(nc.sbuf_tensor("recb", [128, 1], F32)).ap()
            scps = [_a(nc.psum_tensor(f"scps{i}", [128, TH], F32)).ap()
                    for i in range(2)]
            cxb = [_a(nc.psum_tensor(f"cxb{i}", [128, 512], F32)).ap()
                   for i in range(3)]
            trpA = _a(nc.psum_tensor("trpA", [128, 512], BF16)).ap()

            c = AttnCtx()
            c.TH, c.NTH = TH, NTH
            c.Earea = [BIGT[:, 0:32 * TH], BIGT[:, 32 * TH:64 * TH]]
            c.KT, c.qth, c.cxn, c.cxT, c.recb = KT, qth, cxn, cxT, recb
            c.scps, c.cxb, c.trpA, c.vp = scps, cxb, trpA, vp
            c.ident, c.qsp, c.ctxsp = ident, qsp, ctxsp
            c.vq, c.expd_end, c.ctxg_end = {}, {}, {}
            c.scps_gate = [0, 0]
            c.trp_setup_end = {}
            c.scmm_kv_end = {}
            c.ctxg_kv_end = {}

            def attn_kv_setup(kv):
                kvp = kv % 2
                if kv > 0:
                    S.wait(nc.gpsimd, "trp", c.trp_setup_end[kv - 1])
                ksrc = bass.AP(kc_in.tensor, 0,
                               [[BS * HD, 512 * NKV], [HD, BS], [1, HD]])
                vsrc = bass.AP(vc_in.tensor, 0,
                               [[BS * HD, 512 * NKV], [HD, BS], [1, HD]])
                kbv = []
                for H in range(2):
                    kbv.append(S.inc(nc.gpsimd.indirect_dma_start(
                        kb[H], None, ksrc,
                        IndirectOffsetOnAxis(idxsb[:, H:H + 1], 0),
                        element_offset=kv * 512 * BS * HD),
                        "kvga" if H == 0 else "kvgb", 16))
                if kv > 1:
                    S.wait(nc.vector, "ctxg", c.ctxg_kv_end[kv - 2])
                for H in range(2):
                    S.wait(nc.gpsimd, "vone", S.val("vone"))
                    vv = S.inc(nc.gpsimd.indirect_dma_start(
                        vraw, None, vsrc,
                        IndirectOffsetOnAxis(idxsb[:, H:H + 1], 0),
                        element_offset=kv * 512 * BS * HD), "kvgv", 16)
                    S.wait(nc.vector, "kvgv", vv)
                    if kv < 2:
                        S.inc(nc.vector.memset(
                            vp[kvp * 2 + H].rearrange("p (s e) -> p s e", e=129)
                            [:, :, 128:129], 1.0), "vone", 1)
                    S.inc(nc.vector.tensor_copy(
                        vp[kvp * 2 + H].rearrange("p (s e) -> p s e", e=129)
                        [:, :, 0:128],
                        vraw.rearrange("p (s d) -> p s d", d=128)), "vone", 1)
                S.wait(nc.tensor, "kvga", kbv[0])
                S.wait(nc.tensor, "kvgb", kbv[1])
                S.wait(nc.tensor, "vone", S.val("vone"))
                for g in range(8):
                    if S.val("trp") >= 1:
                        S.wait(nc.tensor, "trev", S.val("trev"))
                    for k in range(4):
                        kt = g * 4 + k
                        s_, H_ = kt // 2, kt % 2
                        last = nc.tensor.transpose(
                            trpA[:, k * 128:(k + 1) * 128],
                            kb[H_][:, s_ * 128:(s_ + 1) * 128], ident)
                    v = S.inc(last, "trp")
                    S.wait(nc.vector, "trp", v)
                    if kv > 1 and g == 0:
                        S.wait(nc.vector, "scmm", c.scmm_kv_end[kv - 2])
                    S.inc(nc.vector.tensor_copy(KT[kvp][:, g * 512:(g + 1) * 512],
                                                trpA), "trev")
                c.trp_setup_end[kv] = S.val("trp")

            units = [(kv, hh, th) for kv in range(NKV) for hh in range(4)
                     for th in range(NTH)]
            prev = None
            for ui, u in enumerate(units):
                if u[1] == 0 and u[2] == 0:
                    if u[0] > 0:
                        c.scmm_kv_end[u[0] - 1] = S.val("scmm")
                        if ui >= 2:
                            c.ctxg_kv_end[u[0] - 1] = None
                    attn_kv_setup(u[0])
                attn_A_dummy = None
                attn_unit(nc, S, c, u, ui, prev)
                if prev is not None and prev[1] == 3 and prev[2] == NTH - 1:
                    c.ctxg_kv_end[prev[0]] = c.ctxg_end[ui - 1]
                prev = u
            lastui = len(units) - 1
            for tc in range(CTC):
                attn_ctx_tc(nc, S, c, prev, lastui, tc)
            attn_T(nc, S, c, prev, lastui)
            S.wait(nc.sync, "ctxsta", S.val("ctxsta"))
            S.wait(nc.sync, "ctxstb", S.val("ctxstb"))
            aes.close()
        barrier()

        for h in range(NH):
            S.inc(nc.sync.dma_start(BIGT[:, h * NT:(h + 1) * NT], ctxsp[h]),
                  "ctxld", 16)
        S.wait(nc.tensor, "ctxld", S.val("ctxld"))

        def out_proj(w_dram, dst_dram, tag, lora=False, hT=None, b0sb=None,
                     blsb=None, mtb=None, accb=None, yps=None, wsl_ap=None):
            with (
                contextlib.nullcontext() if wsl_ap is not None else
                nc.sbuf_tensor(f"wsl{tag}", [128, ND * OS_W], BF16) as wsl_g,
                nc.sbuf_tensor(f"ost{tag}0", [128, OS_W], BF16) as ost0_g,
                nc.sbuf_tensor(f"ost{tag}1", [128, OS_W], BF16) as ost1_g,
                nc.psum_tensor(f"wps{tag}0", [128, OS_W], F32) as wps0_g,
                nc.psum_tensor(f"wps{tag}1", [128, OS_W], F32) as wps1_g,
            ):
                wsl = wsl_ap if wsl_ap is not None else wsl_g.ap()
                ost = [ost0_g.ap(), ost1_g.ap()]
                wps = [wps0_g.ap(), wps1_g.ap()]
                nos = D // OS_W
                it = 0
                b_woev = S.val("woev")
                b_comb = S.val("comb")
                b_wosta = S.val("wosta")
                b_wostb = S.val("wostb")
                for os_ in range(nos):
                    if os_ > 0:
                        S.wait(nc.gpsimd, "womm", S.val("womm"))
                    QW = ND // 4 * OS_W
                    for q in range(4):
                        S.inc(nc.gpsimd.dma_start(
                            wsl[:, q * QW:(q + 1) * QW],
                            w_dram[os_][:, q * QW:(q + 1) * QW]), f"wsl{q}", 16)
                    if lora:
                        S.wait(nc.gpsimd, "ymm", S.val("ymm"))
                        for l in range(3):
                            S.inc(nc.gpsimd.dma_start(
                                blsb[0:16, l * OS_W:(l + 1) * OS_W],
                                bl_in[l + 1][:, os_ * OS_W:(os_ + 1) * OS_W]),
                                "bld", 16)
                        S.inc(nc.gpsimd.dma_start(
                            blsb[0:16, 3 * OS_W:4 * OS_W],
                            bl_in[0][:, os_ * OS_W:(os_ + 1) * OS_W]), "bld", 16)
                    for tc in range(NTC):
                        pw = it % 2
                        pm = it % 2
                        if lora:
                            S.wait(nc.sync, "comb", max(0, S.val("comb") - 1))
                            for l in range(3):
                                S.inc(nc.sync.dma_start(
                                    mtb[:, (pm * 3 + l) * OS_W:(pm * 3 + l + 1) * OS_W],
                                    m_in[l][tc * 128:(tc + 1) * 128,
                                            os_ * OS_W:(os_ + 1) * OS_W]), "mld", 16)
                            vml = S.val("mld")
                        if it >= 2:
                            S.wait(nc.tensor, "woev" if not lora else "comb",
                                   (b_woev if not lora else b_comb) + it - 1)
                        for dd in range(ND):
                            if tc == 0 and dd % (ND // 4) == 0:
                                S.wait(nc.tensor, f"wsl{dd // (ND // 4)}",
                                       S.val(f"wsl{dd // (ND // 4)}"))
                            last = nc.tensor.matmul(
                                wps[pw],
                                BIGT[:, dd * NT + tc * 128: dd * NT + (tc + 1) * 128],
                                wsl[:, dd * OS_W:(dd + 1) * OS_W],
                                start=(dd == 0), stop=(not lora) and (dd == ND - 1))
                        if lora:
                            S.wait(nc.tensor, "b0ld", 16)
                            last = nc.tensor.matmul(
                                wps[pw], ones1,
                                b0sb[0:1, os_ * OS_W:(os_ + 1) * OS_W],
                                start=False, stop=True)
                        vbase = S.inc(last, "womm")
                        if lora:
                            if it >= 1:
                                S.wait(nc.tensor, "comb", b_comb + it)
                            S.wait(nc.tensor, "bld", S.val("bld"))
                            S.wait(nc.tensor, "htev", S.val("htev"))
                            for sl in range(4):
                                l = (sl + 1) % 4
                                last = nc.tensor.matmul(
                                    yps[sl], hT[l][:, tc * 128:(tc + 1) * 128],
                                    blsb[0:16, sl * OS_W:(sl + 1) * OS_W],
                                    start=True, stop=True)
                            vy = S.inc(last, "ymm")
                            S.wait(nc.vector, "ymm", vy)
                            S.wait(nc.vector, "mld", vml)
                            S.wait(nc.vector, "vself", S.val("vself"))
                            S.wait(nc.vector, "comb", S.val("comb"))
                            S.inc(nc.vector.tensor_copy(accb, yps[3]), "vself")
                            for sl in range(3):
                                S.wait(nc.vector, "vself", S.val("vself"))
                                S.inc(nc.vector.copy_predicated(
                                    accb,
                                    mtb[:, (pm * 3 + sl) * OS_W:(pm * 3 + sl + 1) * OS_W],
                                    yps[sl]), "vself")
                            S.wait(nc.vector, "vself", S.val("vself"))
                            S.wait(nc.vector, "womm", vbase)
                            if it >= 2:
                                S.wait(nc.vector, "wosta" if pw == 0 else "wostb",
                                       (b_wosta if pw == 0 else b_wostb)
                                       + (it // 2) * 16)
                            vv = S.inc(nc.vector.tensor_tensor(ost[pw], accb,
                                                               wps[pw], ALU.add),
                                       "comb")
                        else:
                            S.wait(nc.vector, "womm", vbase)
                            if it >= 2:
                                S.wait(nc.vector, "wosta" if pw == 0 else "wostb",
                                       (b_wosta if pw == 0 else b_wostb)
                                       + (it // 2) * 16)
                            S.inc(nc.vector.tensor_copy(ost[pw], wps[pw]), "woev")
                        if lora:
                            S.wait(nc.sync, "comb", S.val("comb"))
                        else:
                            S.wait(nc.sync, "woev", S.val("woev"))
                        S.inc(nc.sync.dma_start(
                            dst_dram[tc * 128:(tc + 1) * 128,
                                     os_ * OS_W:(os_ + 1) * OS_W], ost[pw]),
                            "wosta" if pw == 0 else "wostb", 16)
                        it += 1
                S.wait(nc.sync, "wosta", S.val("wosta"))
                S.wait(nc.sync, "wostb", S.val("wostb"))

        out_proj(wo_in, attnsp, "o")
        barrier()

        with (
            nc.sbuf_tensor("hT0", [16, NTC * 128], BF16) as hT0_g,
            nc.sbuf_tensor("hT1", [16, NTC * 128], BF16) as hT1_g,
            nc.sbuf_tensor("hT2", [16, NTC * 128], BF16) as hT2_g,
            nc.sbuf_tensor("hT3", [16, NTC * 128], BF16) as hT3_g,
        ):
            hT = [hT0_g.ap(), hT1_g.ap(), hT2_g.ap(), hT3_g.ap()]
            p5 = contextlib.ExitStack()
            _p = p5.enter_context
            x2b = _p(nc.sbuf_tensor("x2b", [128, D], F32)).ap()
            xnb2 = _p(nc.sbuf_tensor("xnb2", [128, D], BF16)).ap()
            atsb = _p(nc.sbuf_tensor("atsb", [128, ND * 64], BF16)).ap()
            h8b = _p(nc.sbuf_tensor("h8b", [128, 64], BF16)).ap()
            trpb = [_p(nc.psum_tensor("trp0b", [128, 512], BF16)).ap(),
                    _p(nc.psum_tensor("trp1b", [128, 512], BF16)).ap()]
            hps = _p(nc.psum_tensor("hps", [128, 64], F32)).ap()
            trpC = _p(nc.psum_tensor("trpC", [128, 512], BF16)).ap()
            vat = S.inc(nc.gpsimd.dma_start(atsb, at_in), "hld", 16)
            prev_res = 0
            at_bf = SS[:, 0:D // 2].bitcast(BF16)
            for tc in range(NTC):
                at_t, xt_t = at_bf, SS[:, D:2 * D]
                if tc >= 1:
                    S.wait(nc.sync, "lnd", prev_res)
                    S.wait(nc.sync, "outst", tc * 16)
                S.inc(nc.sync.dma_start(at_t, attnsp[tc * 128:(tc + 1) * 128, :]),
                      "xld", 16)
                v1 = S.inc(nc.sync.dma_start(xt_t, x_in[tc * 128:(tc + 1) * 128, :]),
                           "xld", 16)
                S.wait(nc.vector, "xld", v1)
                if tc >= 1:
                    S.wait(nc.vector, "outst", tc * 16)
                rl = ln_tile(at_t, x2b)
                S.wait(nc.vector, "lnd", rl)
                v = S.inc(nc.vector.tensor_tensor(x2b, x2b, xt_t, ALU.add), "lnd")
                S.wait(nc.gpsimd, "lnd", v)
                S.inc(nc.gpsimd.dma_start(x2sp[tc * 128:(tc + 1) * 128, :], x2b),
                      "outst", 16)
                if tc >= 1:
                    S.wait(nc.vector, "trev", S.val("trev"))
                prev_res = ln_tile(x2b, xnb2)
                S.wait(nc.tensor, "lnd", prev_res)
                transpose_to_bigt(xnb2, tc, trpb)
                S.wait(nc.tensor, "trev", S.val("trev"))
                if tc == 0:
                    S.wait(nc.tensor, "hld", vat)
                if tc >= 1:
                    S.wait(nc.tensor, "hact", tc)
                for dd in range(ND):
                    last = nc.tensor.matmul(
                        hps, BIGT[:, dd * NT + tc * 128: dd * NT + (tc + 1) * 128],
                        atsb[:, dd * 64:(dd + 1) * 64],
                        start=(dd == 0), stop=(dd == ND - 1))
                v = S.inc(last, "hmm")
                S.wait(nc.scalar, "hmm", v)
                if tc >= 1:
                    S.wait(nc.scalar, "htr", tc)
                v = S.inc(nc.scalar.activation(h8b, hps, AF.Copy, scale=8.0),
                          "hact")
                S.wait(nc.tensor, "hact", v)
                if tc >= 1:
                    S.wait(nc.tensor, "htev", tc)
                for l in range(4):
                    last = nc.tensor.transpose(trpC[0:16, l * 128:(l + 1) * 128],
                                               h8b[:, l * 16:(l + 1) * 16], ident)
                v = S.inc(last, "htr")
                S.wait(nc.vector, "htr", v)
                for l in range(4):
                    last = nc.vector.tensor_copy(
                        hT[l][:, tc * 128:(tc + 1) * 128],
                        trpC[0:16, l * 128:(l + 1) * 128])
                S.inc(last, "htev")
            S.wait(nc.sync, "outst", S.val("outst"))
            S.wait(nc.tensor, "htev", S.val("htev"))
            p5.close()
            barrier()

            with (
                nc.sbuf_tensor("b0sbf", [1, D], BF16) as b0sb_g,
                nc.sbuf_tensor("blsb", [16, 4 * OS_W], BF16) as blsb_g,
                nc.sbuf_tensor("mtb", [128, 6 * OS_W], U8) as mtb_g,
                nc.sbuf_tensor("accb", [128, OS_W], F32) as accb_g,
                nc.psum_tensor("yps0", [128, OS_W], F32) as yps0_g,
                nc.psum_tensor("yps1", [128, OS_W], F32) as yps1_g,
                nc.psum_tensor("yps2", [128, OS_W], F32) as yps2_g,
                nc.psum_tensor("yps3", [128, OS_W], F32) as yps3_g,
            ):
                b0sb = b0sb_g.ap()
                S.inc(nc.gpsimd.dma_start(b0sb, b0_in), "b0ld", 16)
                out_proj(w0_in, ffnsp, "f", lora=True, hT=hT, b0sb=b0sb,
                         blsb=blsb_g.ap(),
                         mtb=mtb_g.ap(), accb=accb_g.ap(),
                         yps=[yps0_g.ap(), yps1_g.ap(), yps2_g.ap(), yps3_g.ap()],
                         wsl_ap=SS.bitcast(BF16))
        barrier()

        if True:
            p7 = contextlib.ExitStack()
            if ND * NT >= 65536:
                fts = [BIGT[:, i * 4096:(i + 1) * 4096] for i in range(4)]
                x2s = [BIGT[:, 16384 + i * 4096: 16384 + (i + 1) * 4096]
                       for i in range(4)]
            else:
                fts = [p7.enter_context(nc.sbuf_tensor(f"ft{i}", [128, D], BF16)).ap()
                       for i in range(2)] * 2
                x2s = [p7.enter_context(nc.sbuf_tensor(f"x2{i}", [128, D], BF16)).ap()
                       for i in range(2)] * 2
            outb = [SS[:, 0:D], SS[:, D:2 * D]]
            nslot = 4 if ND * NT >= 65536 else 2
            xsems = ["xlda", "xldb", "xldc", "xldd"][:nslot]
            lnd_fin = {}
            for tc in range(NTC):
                sl = tc % nslot
                par = tc % 2
                xsem = xsems[sl]
                ft, x2t = fts[sl], x2s[sl]
                if tc >= nslot:
                    S.wait(nc.sync, "lnd", lnd_fin[tc - nslot])
                    S.wait(nc.scalar, "lnd", lnd_fin[tc - nslot])
                S.inc(nc.scalar.dma_start(ft, ffnsp[tc * 128:(tc + 1) * 128, :]),
                      xsem, 16)
                v1 = S.inc(nc.sync.dma_start(x2t, x2sp[tc * 128:(tc + 1) * 128, :]),
                           xsem, 16)
                S.wait(nc.vector, xsem, v1)
                if tc >= 2:
                    S.wait(nc.vector, "fina" if par == 0 else "finb",
                           (tc // 2) * 16)
                rl = ln_tile(ft, outb[par])
                S.wait(nc.vector, "lnd", rl)
                lnd_fin[tc] = S.inc(
                    nc.vector.tensor_tensor(outb[par], outb[par], x2t, ALU.add),
                    "lnd")
                S.wait(nc.sync, "lnd", lnd_fin[tc])
                S.inc(nc.sync.dma_start(out_d[tc * 128:(tc + 1) * 128, :],
                                        outb[par]), "fina" if par == 0 else "finb", 16)
            S.wait(nc.sync, "fina", S.val("fina"))
            S.wait(nc.sync, "finb", S.val("finb"))
            p7.close()
        barrier()

    nc.finalize()
    return nc




def _threefry_uniform_key42(n):
    def rotl(x, d):
        return ((x << np.uint32(d)) | (x >> np.uint32(32 - d))).astype(np.uint32)

    def tf2x32(k0, k1, x0, x1):
        ks0, ks1 = np.uint32(k0), np.uint32(k1)
        ks2 = np.uint32(ks0 ^ ks1 ^ np.uint32(0x1BD11BDA))
        x0 = (x0 + ks0).astype(np.uint32)
        x1 = (x1 + ks1).astype(np.uint32)
        rots = [[13, 15, 26, 6], [17, 29, 16, 24]]
        ks = [ks1, ks2, ks0]
        for i in range(5):
            for r in rots[i % 2]:
                x0 = (x0 + x1).astype(np.uint32)
                x1 = rotl(x1, r)
                x1 = (x1 ^ x0).astype(np.uint32)
            x0 = (x0 + ks[i % 3]).astype(np.uint32)
            x1 = (x1 + ks[(i + 1) % 3] + np.uint32(i + 1)).astype(np.uint32)
        return x0, x1

    idx = np.arange(n, dtype=np.uint64)
    lo = (idx & np.uint64(0xFFFFFFFF)).astype(np.uint32)
    hi = (idx >> np.uint64(32)).astype(np.uint32)
    o0, o1 = tf2x32(0, 42, hi, lo)
    bits = o0 ^ o1
    return ((bits >> np.uint32(9)) | np.uint32(0x3F800000)).view(np.float32) - np.float32(1.0)


def _host_prep(Wq, Wo, W0, A, Bm, b0):
    WQH = np.ascontiguousarray(
        Wq.reshape(NH, 128, ND, 128).transpose(0, 3, 2, 1).reshape(NH, 128, D))
    nos = D // OS_W
    WOH = np.ascontiguousarray(
        Wo.reshape(nos, OS_W, ND, 128).transpose(0, 3, 2, 1).reshape(nos, 128, ND * OS_W))
    W0H = np.ascontiguousarray(
        W0.reshape(nos, OS_W, ND, 128).transpose(0, 3, 2, 1).reshape(nos, 128, ND * OS_W))
    ATH = np.ascontiguousarray(
        A.reshape(4, 16, ND, 128).transpose(3, 2, 0, 1).reshape(128, ND * 64))
    BLT = np.ascontiguousarray(Bm.transpose(0, 2, 1))
    b0h = np.ascontiguousarray(b0.reshape(1, D))

    Sq = 8192
    inv = (1.0 / (np.float32(10000.0) ** (np.arange(0, HD, 2, dtype=np.float32)
                                          / np.float32(HD)))).astype(np.float32)
    t = np.arange(Sq, dtype=np.float32)
    freqs = np.outer(t, inv)
    emb = np.concatenate([freqs, freqs], axis=-1)
    cos = np.cos(emb).astype(np.float32)
    sinf = np.sin(emb).astype(np.float32)
    sinf = sinf.copy()
    sinf[:, 0:64] *= -1.0

    u = _threefry_uniform_key42(2 * 8192 * D).reshape(16384, D)
    masks = []
    for l in range(1, 4):
        lo, hi = np.float32(0.25 * l), np.float32(0.25 * (l + 1))
        masks.append(((u >= lo) & (u < hi)).astype(np.uint8))

    ident = np.eye(128, dtype=BF)
    return WQH, WOH, W0H, ATH, BLT, b0h, cos, sinf, masks, ident


def make_in_maps(L0_attn_hidden_states, L0_attn_k_cache, L0_attn_v_cache, L0_attn_Wq,
                 L0_attn_Wo, L0_attn_block_table, L0_ffn_W0, L0_ffn_b0, L0_ffn_A,
                 L0_ffn_B, NT):
    WQH, WOH, W0H, ATH, BLT, b0h, cos, sinf, masks, ident = _host_prep(
        np.asarray(L0_attn_Wq, np.float32), np.asarray(L0_attn_Wo, np.float32),
        np.asarray(L0_ffn_W0, np.float32), np.asarray(L0_ffn_A, np.float32),
        np.asarray(L0_ffn_B, np.float32), np.asarray(L0_ffn_b0, np.float32))
    X = np.ascontiguousarray(
        np.asarray(L0_attn_hidden_states, np.float32).reshape(16384, D))
    kc = np.ascontiguousarray(np.asarray(L0_attn_k_cache, np.float32).transpose(1, 0, 2, 3))
    vc = np.ascontiguousarray(np.asarray(L0_attn_v_cache, np.float32).transpose(1, 0, 2, 3))
    bt = np.ascontiguousarray(np.asarray(L0_attn_block_table, np.int32))

    in_maps = []
    for c in range(N_CORES):
        r0 = c * NT
        p0 = r0 % 8192
        m = {
            "x": X[r0:r0 + NT], "wq": WQH, "wo": WOH, "w0": W0H,
            "kc": kc, "vc": vc, "bt": bt, "at": ATH, "bl": BLT, "b0": b0h,
            "cosT": np.ascontiguousarray(cos[p0:p0 + NT].T),
            "sinT": np.ascontiguousarray(sinf[p0:p0 + NT].T),
            "ident": ident,
        }
        for l in range(3):
            m[f"m{l + 1}"] = np.ascontiguousarray(masks[l][r0:r0 + NT])
        in_maps.append(m)
    return in_maps


def kernel(L0_attn_hidden_states, L0_attn_k_cache, L0_attn_v_cache, L0_attn_Wq,
           L0_attn_Wo, L0_attn_block_table, L0_ffn_W0, L0_ffn_b0, L0_ffn_A, L0_ffn_B):
    NT = 16384 // N_CORES
    in_maps = make_in_maps(L0_attn_hidden_states, L0_attn_k_cache, L0_attn_v_cache,
                           L0_attn_Wq, L0_attn_Wo, L0_attn_block_table, L0_ffn_W0,
                           L0_ffn_b0, L0_ffn_A, L0_ffn_B, NT)
    nc = build_program(NT)
    res = run_bass_kernel_spmd(nc, in_maps, core_ids=list(range(N_CORES)))
    out = np.concatenate([res.results[c]["out"] for c in range(N_CORES)], axis=0)
    return out.reshape(2, 8192, D).astype(np.float32)


# version 19
# speedup vs baseline: 1.1478x; 1.0239x over previous
import contextlib
import numpy as np
import ml_dtypes
import concourse.bass as bass
import concourse.mybir as mybir
from concourse.bass import IndirectOffsetOnAxis
from concourse.bass_utils import run_bass_kernel_spmd
import concourse.bass_utils as _bu

_orig_run_command = _bu.run_command

def _run_command_ldwopt(cmd, *a, **k):
    cmd = ["--enable-ldw-opt=true" if c == "--enable-ldw-opt=false" else c
           for c in cmd]
    return _orig_run_command(cmd, *a, **k)

_bu.run_command = _run_command_ldwopt

F32 = mybir.dt.float32
BF16 = mybir.dt.bfloat16
I32 = mybir.dt.int32
U8 = mybir.dt.uint8
AF = mybir.ActivationFunctionType
ALU = mybir.AluOpType
BF = ml_dtypes.bfloat16

N_CORES = 8
D = 4096
NH = 32
NKV = 8
HD = 128
NBLK = 256
BS = 16
EPS = 1e-5
SCALE = 1.0 / float(np.sqrt(128.0))
ND = D // 128
OS_W = 512


def chunks(total, step=512):
    out = []
    o = 0
    while o < total:
        w = min(step, total - o)
        out.append((o, w))
        o += w
    return out


class Sems:
    def __init__(self, nc, stack):
        self.nc, self.stack, self.s, self.c = nc, stack, {}, {}

    def new(self, *names):
        for n in names:
            self.s[n] = self.stack.enter_context(self.nc.semaphore(n))
            self.c[n] = 0

    def inc(self, inst, name, d=1):
        inst.then_inc(self.s[name], d)
        self.c[name] += d
        return self.c[name]

    def inc2(self, inst, n1, n2, d2=1):
        inst.then_inc(self.s[n1], 1)
        self.c[n1] += 1
        inst.then_inc(self.s[n2], d2)
        self.c[n2] += d2
        return self.c[n2]

    def val(self, name):
        return self.c[name]

    def wait(self, eng, name, v):
        if v > 0:
            eng.wait_ge(self.s[name], v)


class AttnCtx:
    pass


def attn_scores_kt(nc, S, c, u, ui, kt):
    kv, hh, th = u
    h = kv * 4 + hh
    parh = h % 2
    pk = kt % 2
    E = c.Earea[ui % 2]
    S.wait(nc.tensor, "expd", c.scps_gate[pk])
    for (no, nw) in chunks(c.TH):
        last = nc.tensor.matmul(
            c.scps[pk][:, no:no + nw],
            c.KT[kv % 2][:, kt * 128:(kt + 1) * 128],
            c.qth[parh][:, th * c.TH + no: th * c.TH + no + nw],
            start=True, stop=True)
    v2 = S.inc(last, "scmm")
    S.wait(nc.scalar, "scmm", v2)
    c.scps_gate[pk] = S.inc(
        nc.scalar.activation(E[:, kt * c.TH:(kt + 1) * c.TH],
                             c.scps[pk], AF.Exp, scale=SCALE), "expd")


def attn_ctx_tc(nc, S, c, u, ui, tc, ktr=(0, 32)):
    kv, hh, th = u
    E = c.Earea[ui % 2]
    CTC = c.TH // 128
    bk, slot = tc // 3, tc % 3
    if tc == 0 and ktr[0] == 0:
        S.wait(nc.tensor, "cxev", S.val("cxev"))
    for kt in range(*ktr):
        if tc == 0:
            S.wait(nc.tensor, "expd", c.expd_end[ui] - 31 + kt)
        last = nc.tensor.matmul(
            c.cxb[bk][:, slot * 129:slot * 129 + 129],
            E[:, kt * c.TH + tc * 128: kt * c.TH + (tc + 1) * 128],
            c.vp[(kv % 2) * 2 + kt % 2][:, (kt // 2) * 129:(kt // 2) * 129 + 129],
            start=(kt == 0), stop=(kt == 31))
    if ktr[1] < 32:
        return
    v3 = S.inc(last, "ctxg")
    if slot == 2 or tc == CTC - 1:
        S.wait(nc.vector, "ctxg", v3)
        for tcc in range(bk * 3, tc + 1):
            sl = tcc % 3
            S.wait(nc.vector, "vself", S.val("vself"))
            S.wait(nc.vector, "cxev", S.val("cxev"))
            S.inc(nc.vector.reciprocal(
                c.recb, c.cxb[bk][:, sl * 129 + 128: sl * 129 + 129]), "vself")
            S.wait(nc.vector, "vself", S.val("vself"))
            inst = nc.vector.tensor_scalar(
                c.cxn[:, tcc * 128:(tcc + 1) * 128],
                c.cxb[bk][:, sl * 129: sl * 129 + 128],
                c.recb, None, ALU.mult)
            S.inc(inst, "cxev" if tcc == tc else "vself")
    if tc == CTC - 1:
        c.ctxg_end[ui] = S.val("ctxg")


def attn_T(nc, S, c, u, ui):
    kv, hh, th = u
    h = kv * 4 + hh
    parh = h % 2
    CTC = c.TH // 128
    S.wait(nc.tensor, "cxev", S.val("cxev"))
    for g in range((CTC + 3) // 4):
        S.wait(nc.tensor, "trev", S.val("trev"))
        n_in_g = min(4, CTC - g * 4)
        for k in range(n_in_g):
            tc = g * 4 + k
            last = nc.tensor.transpose(c.trpA[:, k * 128:(k + 1) * 128],
                                       c.cxn[:, tc * 128:(tc + 1) * 128], c.ident)
        v4 = S.inc(last, "trp")
        S.wait(nc.vector, "trp", v4)
        if th == 0 and g == 0 and h >= 2:
            S.wait(nc.vector, "ctxsta" if parh == 0 else "ctxstb",
                   (h // 2) * 16)
        dst = bass.AP(c.cxT[parh].tensor,
                      c.cxT[parh].offset + th * c.TH + g * 4 * 128,
                      [[c.cxT[parh].ap[0][0], 128], [128, n_in_g], [1, 128]])
        S.inc(nc.vector.tensor_copy(
            dst, c.trpA.rearrange("p (a b) -> p a b", b=128)[:, 0:n_in_g, :]),
            "trev")
    if th == c.NTH - 1:
        S.wait(nc.sync, "trev", S.val("trev"))
        S.inc(nc.sync.dma_start(c.ctxsp[h], c.cxT[parh]),
              "ctxsta" if parh == 0 else "ctxstb", 16)


def attn_unit(nc, S, c, u, ui, prev):
    kv, hh, th = u
    h = kv * 4 + hh
    parh = h % 2
    qsem = "qlda" if parh == 0 else "qldb"
    if th == 0:
        if h >= 2:
            S.wait(nc.sync, "scmm", S.val("scmm"))
        c.vq[h] = S.inc(nc.sync.dma_start(c.qth[parh], c.qsp[h]), qsem, 16)
    S.wait(nc.tensor, qsem, c.vq[h])
    if hh == 0 and th == 0:
        S.wait(nc.tensor, "trev", S.val("trev"))
    if ui >= 2:
        S.wait(nc.scalar, "ctxg", c.ctxg_end[ui - 2])
    CTC = c.TH // 128
    for j in range(16):
        for kt in range(2 * j, 2 * j + 2):
            attn_scores_kt(nc, S, c, u, ui, kt)
            if kt == 31:
                c.expd_end[ui] = S.val("expd")
        if prev is not None and j // 2 < CTC:
            attn_ctx_tc(nc, S, c, prev, ui - 1, j // 2,
                        (0, 16) if j % 2 == 0 else (16, 32))
    if prev is not None:
        attn_T(nc, S, c, prev, ui - 1)


def build_program(NT=2048, debug=False):
    assert NT % 256 == 0
    NTC = NT // 128
    TH = min(1024, NT // 2)
    NTH = NT // TH
    CTC = TH // 128
    NS = NT // 512

    nc = bass.Bass("TRN2", target_bir_lowering=False)

    x_in = nc.dram_tensor("x", [NT, D], F32, kind="ExternalInput").ap()
    wq_in = nc.dram_tensor("wq", [NH, 128, D], F32, kind="ExternalInput").ap()
    wo_in = nc.dram_tensor("wo", [D // OS_W, 128, ND * OS_W], F32, kind="ExternalInput").ap()
    w0_in = nc.dram_tensor("w0", [D // OS_W, 128, ND * OS_W], F32, kind="ExternalInput").ap()
    kc_in = nc.dram_tensor("kc", [NKV, 512, BS, HD], F32, kind="ExternalInput").ap()
    vc_in = nc.dram_tensor("vc", [NKV, 512, BS, HD], F32, kind="ExternalInput").ap()
    bt_in = nc.dram_tensor("bt", [NBLK], I32, kind="ExternalInput").ap()
    at_in = nc.dram_tensor("at", [128, ND * 64], F32, kind="ExternalInput").ap()
    bl_in = nc.dram_tensor("bl", [4, 16, D], F32, kind="ExternalInput").ap()
    b0_in = nc.dram_tensor("b0", [1, D], F32, kind="ExternalInput").ap()
    cos_in = nc.dram_tensor("cosT", [128, NT], F32, kind="ExternalInput").ap()
    sin_in = nc.dram_tensor("sinT", [128, NT], F32, kind="ExternalInput").ap()
    m_in = [nc.dram_tensor(f"m{l}", [NT, D], U8, kind="ExternalInput").ap()
            for l in range(1, 4)]
    id_in = nc.dram_tensor("ident", [128, 128], BF16, kind="ExternalInput").ap()
    out_d = nc.dram_tensor("out", [NT, D], F32, kind="ExternalOutput").ap()

    sk = "ExternalOutput" if debug else "Internal"
    qsp = nc.dram_tensor("qsp", [NH, 128, NT], BF16, kind=sk).ap()
    ctxsp = nc.dram_tensor("ctxsp", [NH, 128, NT], BF16, kind=sk).ap()
    attnsp = nc.dram_tensor("attnsp", [NT, D], BF16, kind=sk).ap()
    x2sp = nc.dram_tensor("x2sp", [NT, D], BF16, kind=sk).ap()
    ffnsp = nc.dram_tensor("ffnsp", [NT, D], BF16, kind=sk).ap()

    stack = contextlib.ExitStack()
    with stack:
        S = Sems(nc, stack)
        S.new("bar", "cld", "xld", "lnd", "trp", "trev",
              "wql", "qmm", "rope", "qspill",
              "kvg", "qld", "scmm", "expd", "ctxg", "cxev", "ctxst",
              "ctxld", "wsl", "womm", "woev", "wost",
              "hld", "hmm", "hact", "htr", "htev",
              "bld", "ymm", "mld", "comb", "outst", "b0ld", "vone",
              "xlda", "xldb", "wqla", "wqlb", "qlda", "qldb",
              "ctxsta", "ctxstb", "qspa", "qspb", "wosta", "wostb",
              "wsl0", "wsl1", "wsl2", "wsl3", "vself", "sself", "fina", "finb",
              "kvga", "kvgb", "kvgv", "xldc", "xldd", "fstat", "p7v")

        BIGT = nc.alloc_sbuf_tensor("BIGT", [128, ND * NT], BF16).ap()
        SS = nc.alloc_sbuf_tensor("SS", [128, 2 * D], F32).ap()
        ident = nc.alloc_sbuf_tensor("identsb", [128, 128], BF16).ap()
        ones1 = nc.alloc_sbuf_tensor("ones1", [1, 128], BF16).ap()
        idxsb = nc.alloc_sbuf_tensor("idxsb", [128, 2], I32).ap()
        stats = nc.alloc_sbuf_tensor("stats", [128, 48], F32).ap()
        aggr = nc.alloc_sbuf_tensor("aggr", [128, 2], F32).ap()
        rstd = nc.alloc_sbuf_tensor("rstd", [128, 2], F32).ap()
        epsb = nc.alloc_sbuf_tensor("epsb", [128, 1], F32).ap()
        aggT = nc.alloc_sbuf_tensor("aggT", [128, 2 * (NT // 128)], F32).ap()

        barv = [0]

        def barrier():
            engs = [nc.tensor, nc.vector, nc.scalar, nc.gpsimd, nc.sync]
            for e in engs:
                for name, cnt in S.c.items():
                    if cnt > 0 and name != "bar":
                        e.wait_ge(S.s[name], cnt)
            for e in engs:
                e.sem_inc(S.s["bar"], 1)
            barv[0] += 5
            for e in engs:
                e.wait_ge(S.s["bar"], barv[0])

        S.inc(nc.sync.dma_start(ident, id_in), "cld", 16)
        S.inc(nc.sync.dma_start(idxsb[:, 0:1],
                                bt_in[0:128].rearrange("(p a) -> p a", a=1)), "cld", 16)
        S.inc(nc.sync.dma_start(idxsb[:, 1:2],
                                bt_in[128:256].rearrange("(p a) -> p a", a=1)), "cld", 16)
        S.inc(nc.vector.memset(ones1, 1.0), "vself")
        S.inc(nc.vector.memset(epsb, EPS), "vself")
        S.wait(nc.vector, "cld", S.val("cld"))
        barrier()

        def ln_tile(src_ap, dst_ap):
            S.wait(nc.vector, "vself", S.val("vself"))
            S.wait(nc.vector, "lnd", S.val("lnd"))
            for i in range(8):
                S.inc(nc.vector.bn_stats(stats[:, 6 * i:6 * i + 6],
                                         src_ap[:, 512 * i:512 * (i + 1)]), "vself")
            S.wait(nc.vector, "vself", S.val("vself"))
            v = S.inc(nc.vector.bn_aggr(aggr, stats), "lnd")
            S.wait(nc.scalar, "lnd", v)
            S.wait(nc.scalar, "sself", S.val("sself"))
            S.inc(nc.scalar.activation(rstd[:, 0:1], aggr[:, 1:2], AF.Ln, bias=epsb),
                  "sself")
            S.wait(nc.scalar, "sself", S.val("sself"))
            v = S.inc(nc.scalar.activation(rstd[:, 1:2], rstd[:, 0:1], AF.Exp,
                                           scale=-0.5), "lnd")
            S.wait(nc.vector, "lnd", v)
            S.wait(nc.vector, "vself", S.val("vself"))
            return S.inc(nc.vector.tensor_scalar(dst_ap, src_ap, aggr[:, 0:1],
                                                 rstd[:, 1:2], ALU.subtract,
                                                 ALU.mult), "lnd")

        def transpose_to_bigt(src_bf, tc, trp):
            for g in range(8):
                gp = g % 2
                if S.val("trp") >= 2:
                    S.wait(nc.tensor, "trev", S.val("trev") - 1)
                for k in range(4):
                    dd = g * 4 + k
                    last = nc.tensor.transpose(trp[gp][:, k * 128:(k + 1) * 128],
                                               src_bf[:, dd * 128:(dd + 1) * 128], ident)
                v = S.inc(last, "trp")
                S.wait(nc.vector, "trp", v)
                dst = bass.AP(BIGT.tensor, BIGT.offset + (g * 4) * NT + tc * 128,
                              [[BIGT.ap[0][0], 128], [NT, 4], [1, 128]])
                S.inc(nc.vector.tensor_copy(
                    dst, trp[gp].rearrange("p (a b) -> p a b", b=128)), "trev")

        with (
            nc.sbuf_tensor("xnb", [128, D], BF16) as xnb_g,
            nc.psum_tensor("trp0", [128, 512], BF16) as trp0_g,
            nc.psum_tensor("trp1", [128, 512], BF16) as trp1_g,
        ):
            xnb = xnb_g.ap()
            trp = [trp0_g.ap(), trp1_g.ap()]
            lnd_of_tc = {}
            for tc in range(NTC):
                par = tc % 2
                xsem = "xlda" if par == 0 else "xldb"
                xt = SS[:, par * D:(par + 1) * D]
                if tc >= 2:
                    S.wait(nc.sync, "lnd", lnd_of_tc[tc - 2])
                v = S.inc(nc.sync.dma_start(xt, x_in[tc * 128:(tc + 1) * 128, :]),
                          xsem, 16)
                S.wait(nc.vector, xsem, v)
                if tc >= 1:
                    S.wait(nc.vector, "trev", S.val("trev"))
                lnd_of_tc[tc] = ln_tile(xt, xnb)
                S.wait(nc.tensor, "lnd", lnd_of_tc[tc])
                transpose_to_bigt(xnb, tc, trp)
        barrier()

        with (
            nc.sbuf_tensor("wqs0", [128, D], BF16) as wqs0_g,
            nc.sbuf_tensor("wqs1", [128, D], BF16) as wqs1_g,
            nc.sbuf_tensor("qst0", [128, NT], BF16) as qst0_g,
            nc.sbuf_tensor("qst1", [128, NT], BF16) as qst1_g,
            nc.psum_tensor("qps0", [128, NT], F32) as qps0_g,
            nc.psum_tensor("qps1", [128, NT], F32) as qps1_g,
        ):
            wqs = [wqs0_g.ap(), wqs1_g.ap()]
            qst = [qst0_g.ap(), qst1_g.ap()]
            qps = [qps0_g.ap(), qps1_g.ap()]
            cosT, sinT = SS[:, 0:NT], SS[:, NT:2 * NT]
            t1, t2 = SS[:, 2 * NT:3 * NT], SS[:, 3 * NT:4 * NT]
            S.inc(nc.sync.dma_start(cosT, cos_in), "xld", 16)
            vtab = S.inc(nc.sync.dma_start(sinT, sin_in), "xld", 16)
            for h in range(NH):
                par = h % 2
                wsem = "wqla" if par == 0 else "wqlb"
                if h >= 2:
                    S.wait(nc.gpsimd, "qmm", h - 1)
                v = S.inc(nc.gpsimd.dma_start(wqs[par], wq_in[h]), wsem, 16)
                S.wait(nc.tensor, wsem, v)
                if h >= 2:
                    S.wait(nc.tensor, "rope", (h - 1) * 4)
                for dd in range(ND):
                    lhsT = wqs[par][:, dd * 128:(dd + 1) * 128]
                    for (no, nw) in chunks(NT):
                        last = nc.tensor.matmul(
                            qps[par][:, no:no + nw], lhsT,
                            BIGT[:, dd * NT + no: dd * NT + no + nw],
                            start=(dd == 0), stop=(dd == ND - 1))
                v = S.inc(last, "qmm")
                S.wait(nc.vector, "qmm", v)
                S.wait(nc.vector, "xld", vtab)
                if h >= 2:
                    S.wait(nc.vector, "qspa" if par == 0 else "qspb",
                           (h // 2) * 16)
                S.wait(nc.vector, "vself", S.val("vself"))
                S.wait(nc.vector, "rope", S.val("rope"))
                S.inc(nc.vector.tensor_tensor(t1, qps[par], cosT, ALU.mult), "vself")
                S.inc(nc.vector.tensor_tensor(t2[0:64, :], qps[par][64:128, :],
                                              sinT[0:64, :], ALU.mult), "vself")
                S.inc(nc.vector.tensor_tensor(t2[64:128, :], qps[par][0:64, :],
                                              sinT[64:128, :], ALU.mult), "vself")
                S.wait(nc.vector, "vself", S.val("vself"))
                v = S.inc(nc.vector.tensor_tensor(qst[par], t1, t2, ALU.add),
                          "rope", 4)
                S.wait(nc.sync, "rope", v)
                S.inc(nc.sync.dma_start(qsp[h], qst[par]),
                      "qspa" if par == 0 else "qspb", 16)
            S.wait(nc.sync, "qspa", S.val("qspa"))
            S.wait(nc.sync, "qspb", S.val("qspb"))
        barrier()

        if True:
            aes = contextlib.ExitStack()
            _a = aes.enter_context
            SSB = SS.bitcast(BF16)
            kb = [SSB[:, 0:2048], SSB[:, 2048:4096]]
            vraw = SSB[:, 4096:6144]
            vp = [SSB[:, 6144 + i * 2064: 6144 + (i + 1) * 2064] for i in range(4)]
            KT = [_a(nc.sbuf_tensor(f"KT{i}", [128, 32 * 128], BF16)).ap()
                  for i in range(2)]
            qth = [_a(nc.sbuf_tensor(f"qth{i}", [128, NT], BF16)).ap()
                   for i in range(2)]
            cxn = _a(nc.sbuf_tensor("cxn", [128, TH], BF16)).ap()
            cxT = [_a(nc.sbuf_tensor(f"cxT{i}", [128, NT], BF16)).ap()
                   for i in range(2)]
            recb = _a(nc.sbuf_tensor("recb", [128, 1], F32)).ap()
            scps = [_a(nc.psum_tensor(f"scps{i}", [128, TH], F32)).ap()
                    for i in range(2)]
            cxb = [_a(nc.psum_tensor(f"cxb{i}", [128, 512], F32)).ap()
                   for i in range(3)]
            trpA = _a(nc.psum_tensor("trpA", [128, 512], BF16)).ap()

            c = AttnCtx()
            c.TH, c.NTH = TH, NTH
            c.Earea = [BIGT[:, 0:32 * TH], BIGT[:, 32 * TH:64 * TH]]
            c.KT, c.qth, c.cxn, c.cxT, c.recb = KT, qth, cxn, cxT, recb
            c.scps, c.cxb, c.trpA, c.vp = scps, cxb, trpA, vp
            c.ident, c.qsp, c.ctxsp = ident, qsp, ctxsp
            c.vq, c.expd_end, c.ctxg_end = {}, {}, {}
            c.scps_gate = [0, 0]
            c.trp_setup_end = {}
            c.scmm_kv_end = {}
            c.ctxg_kv_end = {}

            def attn_kv_setup(kv):
                kvp = kv % 2
                if kv > 0:
                    S.wait(nc.gpsimd, "trp", c.trp_setup_end[kv - 1])
                ksrc = bass.AP(kc_in.tensor, 0,
                               [[BS * HD, 512 * NKV], [HD, BS], [1, HD]])
                vsrc = bass.AP(vc_in.tensor, 0,
                               [[BS * HD, 512 * NKV], [HD, BS], [1, HD]])
                kbv = []
                for H in range(2):
                    kbv.append(S.inc(nc.gpsimd.indirect_dma_start(
                        kb[H], None, ksrc,
                        IndirectOffsetOnAxis(idxsb[:, H:H + 1], 0),
                        element_offset=kv * 512 * BS * HD),
                        "kvga" if H == 0 else "kvgb", 16))
                if kv > 1:
                    S.wait(nc.vector, "ctxg", c.ctxg_kv_end[kv - 2])
                for H in range(2):
                    S.wait(nc.gpsimd, "vone", S.val("vone"))
                    vv = S.inc(nc.gpsimd.indirect_dma_start(
                        vraw, None, vsrc,
                        IndirectOffsetOnAxis(idxsb[:, H:H + 1], 0),
                        element_offset=kv * 512 * BS * HD), "kvgv", 16)
                    S.wait(nc.vector, "kvgv", vv)
                    if kv < 2:
                        S.inc(nc.vector.memset(
                            vp[kvp * 2 + H].rearrange("p (s e) -> p s e", e=129)
                            [:, :, 128:129], 1.0), "vone", 1)
                    S.inc(nc.vector.tensor_copy(
                        vp[kvp * 2 + H].rearrange("p (s e) -> p s e", e=129)
                        [:, :, 0:128],
                        vraw.rearrange("p (s d) -> p s d", d=128)), "vone", 1)
                S.wait(nc.tensor, "kvga", kbv[0])
                S.wait(nc.tensor, "kvgb", kbv[1])
                S.wait(nc.tensor, "vone", S.val("vone"))
                for g in range(8):
                    if S.val("trp") >= 1:
                        S.wait(nc.tensor, "trev", S.val("trev"))
                    for k in range(4):
                        kt = g * 4 + k
                        s_, H_ = kt // 2, kt % 2
                        last = nc.tensor.transpose(
                            trpA[:, k * 128:(k + 1) * 128],
                            kb[H_][:, s_ * 128:(s_ + 1) * 128], ident)
                    v = S.inc(last, "trp")
                    S.wait(nc.vector, "trp", v)
                    if kv > 1 and g == 0:
                        S.wait(nc.vector, "scmm", c.scmm_kv_end[kv - 2])
                    S.inc(nc.vector.tensor_copy(KT[kvp][:, g * 512:(g + 1) * 512],
                                                trpA), "trev")
                c.trp_setup_end[kv] = S.val("trp")

            units = [(kv, hh, th) for kv in range(NKV) for hh in range(4)
                     for th in range(NTH)]
            prev = None
            for ui, u in enumerate(units):
                if u[1] == 0 and u[2] == 0:
                    if u[0] > 0:
                        c.scmm_kv_end[u[0] - 1] = S.val("scmm")
                        if ui >= 2:
                            c.ctxg_kv_end[u[0] - 1] = None
                    attn_kv_setup(u[0])
                attn_A_dummy = None
                attn_unit(nc, S, c, u, ui, prev)
                if prev is not None and prev[1] == 3 and prev[2] == NTH - 1:
                    c.ctxg_kv_end[prev[0]] = c.ctxg_end[ui - 1]
                prev = u
            lastui = len(units) - 1
            for tc in range(CTC):
                attn_ctx_tc(nc, S, c, prev, lastui, tc)
            attn_T(nc, S, c, prev, lastui)
            S.wait(nc.sync, "ctxsta", S.val("ctxsta"))
            S.wait(nc.sync, "ctxstb", S.val("ctxstb"))
            aes.close()
        barrier()

        for h in range(NH):
            S.inc(nc.sync.dma_start(BIGT[:, h * NT:(h + 1) * NT], ctxsp[h]),
                  "ctxld", 16)
        S.wait(nc.tensor, "ctxld", S.val("ctxld"))

        def out_proj(w_dram, dst_dram, tag, lora=False, hT=None, b0sb=None,
                     blsb=None, mtb=None, accb=None, yps=None, wsl_ap=None,
                     stat_buf=None, stat_sem=None, agg_buf=None):
            with (
                contextlib.nullcontext() if wsl_ap is not None else
                nc.sbuf_tensor(f"wsl{tag}", [128, ND * OS_W], BF16) as wsl_g,
                nc.sbuf_tensor(f"ost{tag}0", [128, OS_W], BF16) as ost0_g,
                nc.sbuf_tensor(f"ost{tag}1", [128, OS_W], BF16) as ost1_g,
                nc.psum_tensor(f"wps{tag}0", [128, OS_W], F32) as wps0_g,
                nc.psum_tensor(f"wps{tag}1", [128, OS_W], F32) as wps1_g,
            ):
                wsl = wsl_ap if wsl_ap is not None else wsl_g.ap()
                ost = [ost0_g.ap(), ost1_g.ap()]
                wps = [wps0_g.ap(), wps1_g.ap()]
                nos = D // OS_W
                it = 0
                b_woev = S.val("woev")
                b_comb = S.val("comb")
                b_wosta = S.val("wosta")
                b_wostb = S.val("wostb")
                for os_ in range(nos):
                    if os_ > 0:
                        S.wait(nc.gpsimd, "womm", S.val("womm"))
                    QW = ND // 4 * OS_W
                    for q in range(4):
                        S.inc(nc.gpsimd.dma_start(
                            wsl[:, q * QW:(q + 1) * QW],
                            w_dram[os_][:, q * QW:(q + 1) * QW]), f"wsl{q}", 16)
                    if lora:
                        S.wait(nc.gpsimd, "ymm", S.val("ymm"))
                        for l in range(3):
                            S.inc(nc.gpsimd.dma_start(
                                blsb[0:16, l * OS_W:(l + 1) * OS_W],
                                bl_in[l + 1][:, os_ * OS_W:(os_ + 1) * OS_W]),
                                "bld", 16)
                        S.inc(nc.gpsimd.dma_start(
                            blsb[0:16, 3 * OS_W:4 * OS_W],
                            bl_in[0][:, os_ * OS_W:(os_ + 1) * OS_W]), "bld", 16)
                    for tc in range(NTC):
                        pw = it % 2
                        pm = it % 2
                        if lora:
                            S.wait(nc.sync, "comb", max(0, S.val("comb") - 1))
                            for l in range(3):
                                S.inc(nc.sync.dma_start(
                                    mtb[:, (pm * 3 + l) * OS_W:(pm * 3 + l + 1) * OS_W],
                                    m_in[l][tc * 128:(tc + 1) * 128,
                                            os_ * OS_W:(os_ + 1) * OS_W]), "mld", 16)
                            vml = S.val("mld")
                        if it >= 2:
                            S.wait(nc.tensor, "woev" if not lora else "comb",
                                   (b_woev if not lora else b_comb) + it - 1)
                        for dd in range(ND):
                            if tc == 0 and dd % (ND // 4) == 0:
                                S.wait(nc.tensor, f"wsl{dd // (ND // 4)}",
                                       S.val(f"wsl{dd // (ND // 4)}"))
                            last = nc.tensor.matmul(
                                wps[pw],
                                BIGT[:, dd * NT + tc * 128: dd * NT + (tc + 1) * 128],
                                wsl[:, dd * OS_W:(dd + 1) * OS_W],
                                start=(dd == 0), stop=(not lora) and (dd == ND - 1))
                        if lora:
                            S.wait(nc.tensor, "b0ld", 16)
                            last = nc.tensor.matmul(
                                wps[pw], ones1,
                                b0sb[0:1, os_ * OS_W:(os_ + 1) * OS_W],
                                start=False, stop=True)
                        vbase = S.inc(last, "womm")
                        if lora:
                            if it >= 1:
                                S.wait(nc.tensor, "comb", b_comb + it)
                            S.wait(nc.tensor, "bld", S.val("bld"))
                            S.wait(nc.tensor, "htev", S.val("htev"))
                            for sl in range(4):
                                l = (sl + 1) % 4
                                last = nc.tensor.matmul(
                                    yps[sl], hT[l][:, tc * 128:(tc + 1) * 128],
                                    blsb[0:16, sl * OS_W:(sl + 1) * OS_W],
                                    start=True, stop=True)
                            vy = S.inc(last, "ymm")
                            S.wait(nc.vector, "ymm", vy)
                            S.wait(nc.vector, "mld", vml)
                            S.wait(nc.vector, "vself", S.val("vself"))
                            S.wait(nc.vector, "comb", S.val("comb"))
                            S.inc(nc.vector.tensor_copy(accb, yps[3]), "vself")
                            for sl in range(3):
                                S.wait(nc.vector, "vself", S.val("vself"))
                                S.inc(nc.vector.copy_predicated(
                                    accb,
                                    mtb[:, (pm * 3 + sl) * OS_W:(pm * 3 + sl + 1) * OS_W],
                                    yps[sl]), "vself")
                            S.wait(nc.vector, "vself", S.val("vself"))
                            S.wait(nc.vector, "womm", vbase)
                            if it >= 2:
                                S.wait(nc.vector, "wosta" if pw == 0 else "wostb",
                                       (b_wosta if pw == 0 else b_wostb)
                                       + (it // 2) * 16)
                            vv = S.inc(nc.vector.tensor_tensor(ost[pw], accb,
                                                               wps[pw], ALU.add),
                                       "comb")
                            if stat_buf is not None:
                                S.wait(nc.vector, "comb", vv)
                                vs = S.inc(nc.vector.bn_stats(
                                    stat_buf[:, tc * 48 + os_ * 6:
                                             tc * 48 + os_ * 6 + 6], ost[pw]),
                                    stat_sem)
                                if os_ == nos - 1:
                                    S.wait(nc.vector, stat_sem, vs)
                                    S.inc(nc.vector.bn_aggr(
                                        agg_buf[:, 2 * tc:2 * tc + 2],
                                        stat_buf[:, tc * 48:(tc + 1) * 48]),
                                        stat_sem)
                        else:
                            S.wait(nc.vector, "womm", vbase)
                            if it >= 2:
                                S.wait(nc.vector, "wosta" if pw == 0 else "wostb",
                                       (b_wosta if pw == 0 else b_wostb)
                                       + (it // 2) * 16)
                            ve = S.inc(nc.vector.tensor_copy(ost[pw], wps[pw]),
                                       "woev")
                            if stat_buf is not None:
                                S.wait(nc.vector, "woev", ve)
                                vs = S.inc(nc.vector.bn_stats(
                                    stat_buf[:, tc * 48 + os_ * 6:
                                             tc * 48 + os_ * 6 + 6], ost[pw]),
                                    stat_sem)
                                if os_ == nos - 1:
                                    S.wait(nc.vector, stat_sem, vs)
                                    S.inc(nc.vector.bn_aggr(
                                        agg_buf[:, 2 * tc:2 * tc + 2],
                                        stat_buf[:, tc * 48:(tc + 1) * 48]),
                                        stat_sem)
                        if lora:
                            S.wait(nc.sync, "comb", S.val("comb"))
                        else:
                            S.wait(nc.sync, "woev", S.val("woev"))
                        S.inc(nc.sync.dma_start(
                            dst_dram[tc * 128:(tc + 1) * 128,
                                     os_ * OS_W:(os_ + 1) * OS_W], ost[pw]),
                            "wosta" if pw == 0 else "wostb", 16)
                        it += 1
                S.wait(nc.sync, "wosta", S.val("wosta"))
                S.wait(nc.sync, "wostb", S.val("wostb"))

        out_proj(wo_in, attnsp, "o")
        barrier()

        with (
            nc.sbuf_tensor("hT0", [16, NTC * 128], BF16) as hT0_g,
            nc.sbuf_tensor("hT1", [16, NTC * 128], BF16) as hT1_g,
            nc.sbuf_tensor("hT2", [16, NTC * 128], BF16) as hT2_g,
            nc.sbuf_tensor("hT3", [16, NTC * 128], BF16) as hT3_g,
        ):
            hT = [hT0_g.ap(), hT1_g.ap(), hT2_g.ap(), hT3_g.ap()]
            p5 = contextlib.ExitStack()
            _p = p5.enter_context
            x2b = _p(nc.sbuf_tensor("x2b", [128, D], F32)).ap()
            xnb2 = _p(nc.sbuf_tensor("xnb2", [128, D], BF16)).ap()
            atsb = _p(nc.sbuf_tensor("atsb", [128, ND * 64], BF16)).ap()
            h8b = _p(nc.sbuf_tensor("h8b", [128, 64], BF16)).ap()
            trpb = [_p(nc.psum_tensor("trp0b", [128, 512], BF16)).ap(),
                    _p(nc.psum_tensor("trp1b", [128, 512], BF16)).ap()]
            hps = _p(nc.psum_tensor("hps", [128, 64], F32)).ap()
            trpC = _p(nc.psum_tensor("trpC", [128, 512], BF16)).ap()
            vat = S.inc(nc.gpsimd.dma_start(atsb, at_in), "hld", 16)
            prev_res = 0
            at_bf = SS[:, 0:D // 2].bitcast(BF16)
            for tc in range(NTC):
                at_t, xt_t = at_bf, SS[:, D:2 * D]
                if tc >= 1:
                    S.wait(nc.sync, "lnd", prev_res)
                    S.wait(nc.sync, "outst", tc * 16)
                S.inc(nc.sync.dma_start(at_t, attnsp[tc * 128:(tc + 1) * 128, :]),
                      "xld", 16)
                v1 = S.inc(nc.sync.dma_start(xt_t, x_in[tc * 128:(tc + 1) * 128, :]),
                           "xld", 16)
                S.wait(nc.vector, "xld", v1)
                if tc >= 1:
                    S.wait(nc.vector, "outst", tc * 16)
                rl = ln_tile(at_t, x2b)
                S.wait(nc.vector, "lnd", rl)
                v = S.inc(nc.vector.tensor_tensor(x2b, x2b, xt_t, ALU.add), "lnd")
                S.wait(nc.gpsimd, "lnd", v)
                S.inc(nc.gpsimd.dma_start(x2sp[tc * 128:(tc + 1) * 128, :], x2b),
                      "outst", 16)
                if tc >= 1:
                    S.wait(nc.vector, "trev", S.val("trev"))
                prev_res = ln_tile(x2b, xnb2)
                S.wait(nc.tensor, "lnd", prev_res)
                transpose_to_bigt(xnb2, tc, trpb)
                S.wait(nc.tensor, "trev", S.val("trev"))
                if tc == 0:
                    S.wait(nc.tensor, "hld", vat)
                if tc >= 1:
                    S.wait(nc.tensor, "hact", tc)
                for dd in range(ND):
                    last = nc.tensor.matmul(
                        hps, BIGT[:, dd * NT + tc * 128: dd * NT + (tc + 1) * 128],
                        atsb[:, dd * 64:(dd + 1) * 64],
                        start=(dd == 0), stop=(dd == ND - 1))
                v = S.inc(last, "hmm")
                S.wait(nc.scalar, "hmm", v)
                if tc >= 1:
                    S.wait(nc.scalar, "htr", tc)
                v = S.inc(nc.scalar.activation(h8b, hps, AF.Copy, scale=8.0),
                          "hact")
                S.wait(nc.tensor, "hact", v)
                if tc >= 1:
                    S.wait(nc.tensor, "htev", tc)
                for l in range(4):
                    last = nc.tensor.transpose(trpC[0:16, l * 128:(l + 1) * 128],
                                               h8b[:, l * 16:(l + 1) * 16], ident)
                v = S.inc(last, "htr")
                S.wait(nc.vector, "htr", v)
                for l in range(4):
                    last = nc.vector.tensor_copy(
                        hT[l][:, tc * 128:(tc + 1) * 128],
                        trpC[0:16, l * 128:(l + 1) * 128])
                S.inc(last, "htev")
            S.wait(nc.sync, "outst", S.val("outst"))
            S.wait(nc.tensor, "htev", S.val("htev"))
            p5.close()
            barrier()

            with (
                nc.sbuf_tensor("b0sbf", [1, D], BF16) as b0sb_g,
                nc.sbuf_tensor("blsb", [16, 4 * OS_W], BF16) as blsb_g,
                nc.sbuf_tensor("mtb", [128, 6 * OS_W], U8) as mtb_g,
                nc.sbuf_tensor("accb", [128, OS_W], F32) as accb_g,
                nc.psum_tensor("yps0", [128, OS_W], F32) as yps0_g,
                nc.psum_tensor("yps1", [128, OS_W], F32) as yps1_g,
                nc.psum_tensor("yps2", [128, OS_W], F32) as yps2_g,
                nc.psum_tensor("yps3", [128, OS_W], F32) as yps3_g,
                nc.psum_tensor("stats_f", [128, NTC * 48], F32) as stats_f_g,
            ):
                b0sb = b0sb_g.ap()
                S.inc(nc.gpsimd.dma_start(b0sb, b0_in), "b0ld", 16)
                out_proj(w0_in, ffnsp, "f", lora=True, hT=hT, b0sb=b0sb,
                         blsb=blsb_g.ap(),
                         mtb=mtb_g.ap(), accb=accb_g.ap(),
                         yps=[yps0_g.ap(), yps1_g.ap(), yps2_g.ap(), yps3_g.ap()],
                         wsl_ap=SS.bitcast(BF16),
                         stat_buf=stats_f_g.ap(), stat_sem="fstat",
                         agg_buf=aggT)
        barrier()

        if True:
            p7 = contextlib.ExitStack()
            SSB7 = SS.bitcast(BF16)
            fts = [SSB7[:, 0:D], SSB7[:, D:2 * D]]
            x2s = [SSB7[:, 2 * D:3 * D], SSB7[:, 3 * D:4 * D]]
            rsf = p7.enter_context(nc.sbuf_tensor("rsf", [128, 4], F32)).ap()
            nmrb = p7.enter_context(nc.sbuf_tensor("nmrb", [128, 2], F32)).ap()
            obuf = p7.enter_context(nc.sbuf_tensor("obuf", [128, D], F32)).ap()
            p7ps = [obuf[:, k * 512:(k + 1) * 512] for k in range(8)]
            bsem = ["fina", "qspa", "ctxsta", "wosta",
                    "finb", "qspb", "ctxstb", "wostb"]
            qeng = [nc.sync] * 4 + [nc.scalar] * 4
            bbase = {s: S.val(s) for s in bsem}
            p7base = S.val("p7v")
            NCH = D // 512
            ln_val, nm_val, ex_val = {}, {}, {}
            for tc in range(NTC):
                p = tc % 2
                ft, x2t = fts[p], x2s[p]
                fsem = "xlda" if p == 0 else "xldb"
                xsem = "xldc" if p == 0 else "xldd"
                slot_free = p7base + NCH * (tc - 1)
                if tc >= 2:
                    S.wait(nc.scalar, "p7v", slot_free)
                vft = S.inc(nc.scalar.dma_start(
                    ft, ffnsp[tc * 128:(tc + 1) * 128, :]), fsem, 16)
                if tc >= 2:
                    S.wait(nc.sync, "p7v", slot_free)
                vx2 = S.inc(nc.sync.dma_start(
                    x2t, x2sp[tc * 128:(tc + 1) * 128, :]), xsem, 16)
                if tc >= 2:
                    S.wait(nc.scalar, "p7v", slot_free)
                S.wait(nc.scalar, "sself", S.val("sself"))
                ln_val[tc] = S.inc(nc.scalar.activation(
                    rsf[:, 2 * p:2 * p + 1], aggT[:, 2 * tc + 1:2 * tc + 2],
                    AF.Ln, bias=epsb), "sself")
                S.wait(nc.scalar, "sself", ln_val[tc])
                ex_val[tc] = S.inc(nc.scalar.activation(
                    rsf[:, 2 * p + 1:2 * p + 2], rsf[:, 2 * p:2 * p + 1],
                    AF.Exp, scale=-0.5), "lnd")
                S.wait(nc.vector, "lnd", ex_val[tc])
                if tc >= 2:
                    S.wait(nc.vector, "p7v", slot_free)
                nm_val[tc] = S.inc(nc.vector.tensor_scalar(
                    nmrb[:, p:p + 1], aggT[:, 2 * tc:2 * tc + 1],
                    rsf[:, 2 * p + 1:2 * p + 2], -1.0, ALU.mult, ALU.mult),
                    "lnd")
                S.wait(nc.scalar, "lnd", nm_val[tc])
                S.wait(nc.scalar, fsem, vft)
                S.wait(nc.vector, xsem, vx2)
                for k in range(NCH):
                    if tc >= 1:
                        S.wait(nc.scalar, bsem[k], bbase[bsem[k]] + 16 * tc)
                    vact = S.inc(nc.scalar.activation(
                        p7ps[k], ft[:, k * 512:(k + 1) * 512], AF.Identity,
                        bias=nmrb[:, p:p + 1],
                        scale=rsf[:, 2 * p + 1:2 * p + 2]), "sself")
                    S.wait(nc.vector, "sself", vact)
                    vaf = S.inc(nc.vector.tensor_tensor(
                        p7ps[k], p7ps[k],
                        x2t[:, k * 512:(k + 1) * 512], ALU.add), "p7v")
                    S.wait(qeng[k], "p7v", vaf)
                    S.inc(qeng[k].dma_start(
                        out_d[tc * 128:(tc + 1) * 128, k * 512:(k + 1) * 512],
                        p7ps[k]), bsem[k], 16)
            for s in bsem:
                S.wait(nc.sync, s, S.val(s))
            p7.close()
        barrier()

    nc.finalize()
    return nc




def _threefry_uniform_key42(n):
    def rotl(x, d):
        return ((x << np.uint32(d)) | (x >> np.uint32(32 - d))).astype(np.uint32)

    def tf2x32(k0, k1, x0, x1):
        ks0, ks1 = np.uint32(k0), np.uint32(k1)
        ks2 = np.uint32(ks0 ^ ks1 ^ np.uint32(0x1BD11BDA))
        x0 = (x0 + ks0).astype(np.uint32)
        x1 = (x1 + ks1).astype(np.uint32)
        rots = [[13, 15, 26, 6], [17, 29, 16, 24]]
        ks = [ks1, ks2, ks0]
        for i in range(5):
            for r in rots[i % 2]:
                x0 = (x0 + x1).astype(np.uint32)
                x1 = rotl(x1, r)
                x1 = (x1 ^ x0).astype(np.uint32)
            x0 = (x0 + ks[i % 3]).astype(np.uint32)
            x1 = (x1 + ks[(i + 1) % 3] + np.uint32(i + 1)).astype(np.uint32)
        return x0, x1

    idx = np.arange(n, dtype=np.uint64)
    lo = (idx & np.uint64(0xFFFFFFFF)).astype(np.uint32)
    hi = (idx >> np.uint64(32)).astype(np.uint32)
    o0, o1 = tf2x32(0, 42, hi, lo)
    bits = o0 ^ o1
    return ((bits >> np.uint32(9)) | np.uint32(0x3F800000)).view(np.float32) - np.float32(1.0)


def _host_prep(Wq, Wo, W0, A, Bm, b0):
    WQH = np.ascontiguousarray(
        Wq.reshape(NH, 128, ND, 128).transpose(0, 3, 2, 1).reshape(NH, 128, D))
    nos = D // OS_W
    WOH = np.ascontiguousarray(
        Wo.reshape(nos, OS_W, ND, 128).transpose(0, 3, 2, 1).reshape(nos, 128, ND * OS_W))
    W0H = np.ascontiguousarray(
        W0.reshape(nos, OS_W, ND, 128).transpose(0, 3, 2, 1).reshape(nos, 128, ND * OS_W))
    ATH = np.ascontiguousarray(
        A.reshape(4, 16, ND, 128).transpose(3, 2, 0, 1).reshape(128, ND * 64))
    BLT = np.ascontiguousarray(Bm.transpose(0, 2, 1))
    b0h = np.ascontiguousarray(b0.reshape(1, D))

    Sq = 8192
    inv = (1.0 / (np.float32(10000.0) ** (np.arange(0, HD, 2, dtype=np.float32)
                                          / np.float32(HD)))).astype(np.float32)
    t = np.arange(Sq, dtype=np.float32)
    freqs = np.outer(t, inv)
    emb = np.concatenate([freqs, freqs], axis=-1)
    cos = np.cos(emb).astype(np.float32)
    sinf = np.sin(emb).astype(np.float32)
    sinf = sinf.copy()
    sinf[:, 0:64] *= -1.0

    u = _threefry_uniform_key42(2 * 8192 * D).reshape(16384, D)
    masks = []
    for l in range(1, 4):
        lo, hi = np.float32(0.25 * l), np.float32(0.25 * (l + 1))
        masks.append(((u >= lo) & (u < hi)).astype(np.uint8))

    ident = np.eye(128, dtype=BF)
    return WQH, WOH, W0H, ATH, BLT, b0h, cos, sinf, masks, ident


def make_in_maps(L0_attn_hidden_states, L0_attn_k_cache, L0_attn_v_cache, L0_attn_Wq,
                 L0_attn_Wo, L0_attn_block_table, L0_ffn_W0, L0_ffn_b0, L0_ffn_A,
                 L0_ffn_B, NT):
    WQH, WOH, W0H, ATH, BLT, b0h, cos, sinf, masks, ident = _host_prep(
        np.asarray(L0_attn_Wq, np.float32), np.asarray(L0_attn_Wo, np.float32),
        np.asarray(L0_ffn_W0, np.float32), np.asarray(L0_ffn_A, np.float32),
        np.asarray(L0_ffn_B, np.float32), np.asarray(L0_ffn_b0, np.float32))
    X = np.ascontiguousarray(
        np.asarray(L0_attn_hidden_states, np.float32).reshape(16384, D))
    kc = np.ascontiguousarray(np.asarray(L0_attn_k_cache, np.float32).transpose(1, 0, 2, 3))
    vc = np.ascontiguousarray(np.asarray(L0_attn_v_cache, np.float32).transpose(1, 0, 2, 3))
    bt = np.ascontiguousarray(np.asarray(L0_attn_block_table, np.int32))

    in_maps = []
    for c in range(N_CORES):
        r0 = c * NT
        p0 = r0 % 8192
        m = {
            "x": X[r0:r0 + NT], "wq": WQH, "wo": WOH, "w0": W0H,
            "kc": kc, "vc": vc, "bt": bt, "at": ATH, "bl": BLT, "b0": b0h,
            "cosT": np.ascontiguousarray(cos[p0:p0 + NT].T),
            "sinT": np.ascontiguousarray(sinf[p0:p0 + NT].T),
            "ident": ident,
        }
        for l in range(3):
            m[f"m{l + 1}"] = np.ascontiguousarray(masks[l][r0:r0 + NT])
        in_maps.append(m)
    return in_maps


def kernel(L0_attn_hidden_states, L0_attn_k_cache, L0_attn_v_cache, L0_attn_Wq,
           L0_attn_Wo, L0_attn_block_table, L0_ffn_W0, L0_ffn_b0, L0_ffn_A, L0_ffn_B):
    NT = 16384 // N_CORES
    in_maps = make_in_maps(L0_attn_hidden_states, L0_attn_k_cache, L0_attn_v_cache,
                           L0_attn_Wq, L0_attn_Wo, L0_attn_block_table, L0_ffn_W0,
                           L0_ffn_b0, L0_ffn_A, L0_ffn_B, NT)
    nc = build_program(NT)
    res = run_bass_kernel_spmd(nc, in_maps, core_ids=list(range(N_CORES)))
    out = np.concatenate([res.results[c]["out"] for c in range(N_CORES)], axis=0)
    return out.reshape(2, 8192, D).astype(np.float32)


# version 20
# speedup vs baseline: 1.1522x; 1.0039x over previous
import contextlib
import numpy as np
import ml_dtypes
import concourse.bass as bass
import concourse.mybir as mybir
from concourse.bass import IndirectOffsetOnAxis
from concourse.bass_utils import run_bass_kernel_spmd
import concourse.bass_utils as _bu

_orig_run_command = _bu.run_command

def _run_command_ldwopt(cmd, *a, **k):
    cmd = ["--enable-ldw-opt=true" if c == "--enable-ldw-opt=false" else c
           for c in cmd]
    return _orig_run_command(cmd, *a, **k)

_bu.run_command = _run_command_ldwopt

F32 = mybir.dt.float32
BF16 = mybir.dt.bfloat16
I32 = mybir.dt.int32
U8 = mybir.dt.uint8
AF = mybir.ActivationFunctionType
ALU = mybir.AluOpType
BF = ml_dtypes.bfloat16

N_CORES = 8
D = 4096
NH = 32
NKV = 8
HD = 128
NBLK = 256
BS = 16
EPS = 1e-5
SCALE = 1.0 / float(np.sqrt(128.0))
ND = D // 128
OS_W = 512


def chunks(total, step=512):
    out = []
    o = 0
    while o < total:
        w = min(step, total - o)
        out.append((o, w))
        o += w
    return out


class Sems:
    def __init__(self, nc, stack):
        self.nc, self.stack, self.s, self.c = nc, stack, {}, {}

    def new(self, *names):
        for n in names:
            self.s[n] = self.stack.enter_context(self.nc.semaphore(n))
            self.c[n] = 0

    def inc(self, inst, name, d=1):
        inst.then_inc(self.s[name], d)
        self.c[name] += d
        return self.c[name]

    def inc2(self, inst, n1, n2, d2=1):
        inst.then_inc(self.s[n1], 1)
        self.c[n1] += 1
        inst.then_inc(self.s[n2], d2)
        self.c[n2] += d2
        return self.c[n2]

    def val(self, name):
        return self.c[name]

    def wait(self, eng, name, v):
        if v > 0:
            eng.wait_ge(self.s[name], v)


class AttnCtx:
    pass


def attn_scores_kt(nc, S, c, u, ui, kt):
    kv, hh, th = u
    h = kv * 4 + hh
    parh = h % 2
    pk = kt % 2
    E = c.Earea[ui % 2]
    S.wait(nc.tensor, "expd", c.scps_gate[pk])
    for (no, nw) in chunks(c.TH):
        last = nc.tensor.matmul(
            c.scps[pk][:, no:no + nw],
            c.KT[kv % 2][:, kt * 128:(kt + 1) * 128],
            c.qth[parh][:, th * c.TH + no: th * c.TH + no + nw],
            start=True, stop=True)
    v2 = S.inc(last, "scmm")
    S.wait(nc.scalar, "scmm", v2)
    c.scps_gate[pk] = S.inc(
        nc.scalar.activation(E[:, kt * c.TH:(kt + 1) * c.TH],
                             c.scps[pk], AF.Exp, scale=SCALE), "expd")


def attn_ctx_tc(nc, S, c, u, ui, tc, ktr=(0, 32)):
    kv, hh, th = u
    E = c.Earea[ui % 2]
    CTC = c.TH // 128
    bk, slot = tc // 3, tc % 3
    if tc == 0 and ktr[0] == 0:
        S.wait(nc.tensor, "cxev", S.val("cxev"))
    for kt in range(*ktr):
        if tc == 0:
            S.wait(nc.tensor, "expd", c.expd_end[ui] - 31 + kt)
        last = nc.tensor.matmul(
            c.cxb[bk][:, slot * 129:slot * 129 + 129],
            E[:, kt * c.TH + tc * 128: kt * c.TH + (tc + 1) * 128],
            c.vp[(kv % 2) * 2 + kt % 2][:, (kt // 2) * 129:(kt // 2) * 129 + 129],
            start=(kt == 0), stop=(kt == 31))
    if ktr[1] < 32:
        return
    v3 = S.inc(last, "ctxg")
    if slot == 2 or tc == CTC - 1:
        S.wait(nc.vector, "ctxg", v3)
        for tcc in range(bk * 3, tc + 1):
            sl = tcc % 3
            S.wait(nc.vector, "vself", S.val("vself"))
            S.wait(nc.vector, "cxev", S.val("cxev"))
            S.inc(nc.vector.reciprocal(
                c.recb, c.cxb[bk][:, sl * 129 + 128: sl * 129 + 129]), "vself")
            S.wait(nc.vector, "vself", S.val("vself"))
            inst = nc.vector.tensor_scalar(
                c.cxn[:, tcc * 128:(tcc + 1) * 128],
                c.cxb[bk][:, sl * 129: sl * 129 + 128],
                c.recb, None, ALU.mult)
            S.inc(inst, "cxev" if tcc == tc else "vself")
    if tc == CTC - 1:
        c.ctxg_end[ui] = S.val("ctxg")


def attn_T(nc, S, c, u, ui):
    kv, hh, th = u
    h = kv * 4 + hh
    parh = h % 2
    CTC = c.TH // 128
    S.wait(nc.tensor, "cxev", S.val("cxev"))
    for g in range((CTC + 3) // 4):
        S.wait(nc.tensor, "trev", S.val("trev"))
        n_in_g = min(4, CTC - g * 4)
        for k in range(n_in_g):
            tc = g * 4 + k
            last = nc.tensor.transpose(c.trpA[:, k * 128:(k + 1) * 128],
                                       c.cxn[:, tc * 128:(tc + 1) * 128], c.ident)
        v4 = S.inc(last, "trp")
        S.wait(nc.vector, "trp", v4)
        if th == 0 and g == 0 and h >= 2:
            S.wait(nc.vector, "ctxsta" if parh == 0 else "ctxstb",
                   (h // 2) * 16)
        dst = bass.AP(c.cxT[parh].tensor,
                      c.cxT[parh].offset + th * c.TH + g * 4 * 128,
                      [[c.cxT[parh].ap[0][0], 128], [128, n_in_g], [1, 128]])
        S.inc(nc.vector.tensor_copy(
            dst, c.trpA.rearrange("p (a b) -> p a b", b=128)[:, 0:n_in_g, :]),
            "trev")
    if th == c.NTH - 1:
        S.wait(nc.sync, "trev", S.val("trev"))
        S.inc(nc.sync.dma_start(c.ctxsp[h], c.cxT[parh]),
              "ctxsta" if parh == 0 else "ctxstb", 16)


def attn_unit(nc, S, c, u, ui, prev):
    kv, hh, th = u
    h = kv * 4 + hh
    parh = h % 2
    qsem = "qlda" if parh == 0 else "qldb"
    if th == 0:
        if h >= 2:
            S.wait(nc.sync, "scmm", S.val("scmm"))
        c.vq[h] = S.inc(nc.sync.dma_start(c.qth[parh], c.qsp[h]), qsem, 16)
    S.wait(nc.tensor, qsem, c.vq[h])
    if hh == 0 and th == 0:
        S.wait(nc.tensor, "trev", S.val("trev"))
    if ui >= 2:
        S.wait(nc.scalar, "ctxg", c.ctxg_end[ui - 2])
    CTC = c.TH // 128
    for j in range(16):
        for kt in range(2 * j, 2 * j + 2):
            attn_scores_kt(nc, S, c, u, ui, kt)
            if kt == 31:
                c.expd_end[ui] = S.val("expd")
        if prev is not None and j // 2 < CTC:
            attn_ctx_tc(nc, S, c, prev, ui - 1, j // 2,
                        (0, 16) if j % 2 == 0 else (16, 32))
    if prev is not None:
        attn_T(nc, S, c, prev, ui - 1)


def build_program(NT=2048, debug=False):
    assert NT % 256 == 0
    NTC = NT // 128
    TH = min(1024, NT // 2)
    NTH = NT // TH
    CTC = TH // 128
    NS = NT // 512

    nc = bass.Bass("TRN2", target_bir_lowering=False)

    x_in = nc.dram_tensor("x", [NT, D], F32, kind="ExternalInput").ap()
    wq_in = nc.dram_tensor("wq", [NH, 128, D], F32, kind="ExternalInput").ap()
    wo_in = nc.dram_tensor("wo", [D // OS_W, 128, ND * OS_W], F32, kind="ExternalInput").ap()
    w0_in = nc.dram_tensor("w0", [D // OS_W, 128, ND * OS_W], F32, kind="ExternalInput").ap()
    kc_in = nc.dram_tensor("kc", [NKV, 512, BS, HD], F32, kind="ExternalInput").ap()
    vc_in = nc.dram_tensor("vc", [NKV, 512, BS, HD], F32, kind="ExternalInput").ap()
    bt_in = nc.dram_tensor("bt", [NBLK], I32, kind="ExternalInput").ap()
    at_in = nc.dram_tensor("at", [128, ND * 64], F32, kind="ExternalInput").ap()
    bl_in = nc.dram_tensor("bl", [4, 16, D], F32, kind="ExternalInput").ap()
    b0_in = nc.dram_tensor("b0", [1, D], F32, kind="ExternalInput").ap()
    cos_in = nc.dram_tensor("cosT", [128, NT], F32, kind="ExternalInput").ap()
    sin_in = nc.dram_tensor("sinT", [128, NT], F32, kind="ExternalInput").ap()
    m_in = [nc.dram_tensor(f"m{l}", [NT, D], U8, kind="ExternalInput").ap()
            for l in range(1, 4)]
    id_in = nc.dram_tensor("ident", [128, 128], BF16, kind="ExternalInput").ap()
    out_d = nc.dram_tensor("out", [NT, D], F32, kind="ExternalOutput").ap()

    sk = "ExternalOutput" if debug else "Internal"
    qsp = nc.dram_tensor("qsp", [NH, 128, NT], BF16, kind=sk).ap()
    ctxsp = nc.dram_tensor("ctxsp", [NH, 128, NT], BF16, kind=sk).ap()
    attnsp = nc.dram_tensor("attnsp", [NT, D], BF16, kind=sk).ap()
    x2sp = nc.dram_tensor("x2sp", [NT, D], BF16, kind=sk).ap()
    ffnsp = nc.dram_tensor("ffnsp", [NT, D], BF16, kind=sk).ap()

    stack = contextlib.ExitStack()
    with stack:
        S = Sems(nc, stack)
        S.new("bar", "cld", "xld", "lnd", "trp", "trev",
              "wql", "qmm", "rope", "qspill",
              "kvg", "qld", "scmm", "expd", "ctxg", "cxev", "ctxst",
              "ctxld", "wsl", "womm", "woev", "wost",
              "hld", "hmm", "hact", "htr", "htev",
              "bld", "ymm", "mld", "comb", "outst", "b0ld", "vone",
              "xlda", "xldb", "wqla", "wqlb", "qlda", "qldb",
              "ctxsta", "ctxstb", "qspa", "qspb", "wosta", "wostb",
              "wsl0", "wsl1", "wsl2", "wsl3", "vself", "sself", "fina", "finb",
              "kvga", "kvgb", "kvgv", "xldc", "xldd", "fstat", "p7v", "trevs")

        BIGT = nc.alloc_sbuf_tensor("BIGT", [128, ND * NT], BF16).ap()
        SS = nc.alloc_sbuf_tensor("SS", [128, 2 * D], F32).ap()
        ident = nc.alloc_sbuf_tensor("identsb", [128, 128], BF16).ap()
        ones1 = nc.alloc_sbuf_tensor("ones1", [1, 128], BF16).ap()
        idxsb = nc.alloc_sbuf_tensor("idxsb", [128, 2], I32).ap()
        stats = nc.alloc_sbuf_tensor("stats", [128, 48], F32).ap()
        aggr = nc.alloc_sbuf_tensor("aggr", [128, 2], F32).ap()
        rstd = nc.alloc_sbuf_tensor("rstd", [128, 3], F32).ap()
        epsb = nc.alloc_sbuf_tensor("epsb", [128, 1], F32).ap()
        aggT = nc.alloc_sbuf_tensor("aggT", [128, 2 * (NT // 128)], F32).ap()
        aggW = nc.alloc_sbuf_tensor("aggW", [128, 2 * (NT // 128)], F32).ap()

        barv = [0]

        def barrier():
            engs = [nc.tensor, nc.vector, nc.scalar, nc.gpsimd, nc.sync]
            for e in engs:
                for name, cnt in S.c.items():
                    if cnt > 0 and name != "bar":
                        e.wait_ge(S.s[name], cnt)
            for e in engs:
                e.sem_inc(S.s["bar"], 1)
            barv[0] += 5
            for e in engs:
                e.wait_ge(S.s["bar"], barv[0])

        S.inc(nc.sync.dma_start(ident, id_in), "cld", 16)
        S.inc(nc.sync.dma_start(idxsb[:, 0:1],
                                bt_in[0:128].rearrange("(p a) -> p a", a=1)), "cld", 16)
        S.inc(nc.sync.dma_start(idxsb[:, 1:2],
                                bt_in[128:256].rearrange("(p a) -> p a", a=1)), "cld", 16)
        S.inc(nc.vector.memset(ones1, 1.0), "vself")
        S.inc(nc.vector.memset(epsb, EPS), "vself")
        S.wait(nc.vector, "cld", S.val("cld"))
        barrier()

        def ln_tile(src_ap, dst_ap):
            S.wait(nc.vector, "vself", S.val("vself"))
            S.wait(nc.vector, "lnd", S.val("lnd"))
            S.wait(nc.vector, "sself", S.val("sself"))
            for i in range(8):
                S.inc(nc.vector.bn_stats(stats[:, 6 * i:6 * i + 6],
                                         src_ap[:, 512 * i:512 * (i + 1)]), "vself")
            S.wait(nc.vector, "vself", S.val("vself"))
            va = S.inc(nc.vector.bn_aggr(aggr, stats), "lnd")
            S.wait(nc.scalar, "lnd", va)
            S.wait(nc.scalar, "sself", S.val("sself"))
            S.inc(nc.scalar.activation(rstd[:, 0:1], aggr[:, 1:2], AF.Ln, bias=epsb),
                  "sself")
            S.wait(nc.scalar, "sself", S.val("sself"))
            ve = S.inc(nc.scalar.activation(rstd[:, 1:2], rstd[:, 0:1], AF.Exp,
                                            scale=-0.5), "lnd")
            S.wait(nc.vector, "lnd", ve)
            vn = S.inc(nc.vector.tensor_scalar(rstd[:, 2:3], aggr[:, 0:1],
                                               rstd[:, 1:2], -1.0, ALU.mult,
                                               ALU.mult), "lnd")
            S.wait(nc.scalar, "lnd", vn)
            return S.inc(nc.scalar.activation(dst_ap, src_ap, AF.Identity,
                                              bias=rstd[:, 2:3],
                                              scale=rstd[:, 1:2]), "lnd")

        tr_last = {0: None, 1: None}

        def transpose_to_bigt(src_bf, tc, trp, split_evac=False):
            for g in range(8):
                gp = g % 2
                if tr_last[gp] is not None:
                    S.wait(nc.tensor, *tr_last[gp])
                for k in range(4):
                    dd = g * 4 + k
                    last = nc.tensor.transpose(trp[gp][:, k * 128:(k + 1) * 128],
                                               src_bf[:, dd * 128:(dd + 1) * 128], ident)
                v = S.inc(last, "trp")
                dst = bass.AP(BIGT.tensor, BIGT.offset + (g * 4) * NT + tc * 128,
                              [[BIGT.ap[0][0], 128], [NT, 4], [1, 128]])
                src3 = trp[gp].rearrange("p (a b) -> p a b", b=128)
                if split_evac and g % 2 == 1:
                    S.wait(nc.scalar, "trp", v)
                    ev = S.inc(nc.scalar.copy(dst, src3), "trevs")
                    tr_last[gp] = ("trevs", ev)
                else:
                    S.wait(nc.vector, "trp", v)
                    ev = S.inc(nc.vector.tensor_copy(dst, src3), "trev")
                    tr_last[gp] = ("trev", ev)

        with (
            nc.sbuf_tensor("xnb", [128, D], BF16) as xnb_g,
            nc.psum_tensor("trp0", [128, 512], BF16) as trp0_g,
            nc.psum_tensor("trp1", [128, 512], BF16) as trp1_g,
        ):
            xnb = xnb_g.ap()
            trp = [trp0_g.ap(), trp1_g.ap()]
            lnd_of_tc = {}
            for tc in range(NTC):
                par = tc % 2
                xsem = "xlda" if par == 0 else "xldb"
                qe = nc.sync if par == 0 else nc.scalar
                xt = SS[:, par * D:(par + 1) * D]
                if tc >= 2:
                    S.wait(qe, "lnd", lnd_of_tc[tc - 2])
                v = S.inc(qe.dma_start(xt, x_in[tc * 128:(tc + 1) * 128, :]),
                          xsem, 16)
                S.wait(nc.vector, xsem, v)
                S.wait(nc.scalar, xsem, v)
                if tc >= 1:
                    S.wait(nc.scalar, "trev", S.val("trev"))
                    S.wait(nc.scalar, "trevs", S.val("trevs"))
                lnd_of_tc[tc] = ln_tile(xt, xnb)
                S.wait(nc.tensor, "lnd", lnd_of_tc[tc])
                transpose_to_bigt(xnb, tc, trp, split_evac=True)
        barrier()

        with (
            nc.sbuf_tensor("wqs0", [128, D], BF16) as wqs0_g,
            nc.sbuf_tensor("wqs1", [128, D], BF16) as wqs1_g,
            nc.sbuf_tensor("qst0", [128, NT], BF16) as qst0_g,
            nc.sbuf_tensor("qst1", [128, NT], BF16) as qst1_g,
            nc.psum_tensor("qps0", [128, NT], F32) as qps0_g,
            nc.psum_tensor("qps1", [128, NT], F32) as qps1_g,
        ):
            wqs = [wqs0_g.ap(), wqs1_g.ap()]
            qst = [qst0_g.ap(), qst1_g.ap()]
            qps = [qps0_g.ap(), qps1_g.ap()]
            cosT, sinT = SS[:, 0:NT], SS[:, NT:2 * NT]
            t1, t2 = SS[:, 2 * NT:3 * NT], SS[:, 3 * NT:4 * NT]
            S.inc(nc.sync.dma_start(cosT, cos_in), "xld", 16)
            vtab = S.inc(nc.sync.dma_start(sinT, sin_in), "xld", 16)
            for h in range(NH):
                par = h % 2
                wsem = "wqla" if par == 0 else "wqlb"
                if h >= 2:
                    S.wait(nc.gpsimd, "qmm", h - 1)
                v = S.inc(nc.gpsimd.dma_start(wqs[par], wq_in[h]), wsem, 16)
                S.wait(nc.tensor, wsem, v)
                if h >= 2:
                    S.wait(nc.tensor, "rope", (h - 1) * 4)
                for dd in range(ND):
                    lhsT = wqs[par][:, dd * 128:(dd + 1) * 128]
                    for (no, nw) in chunks(NT):
                        last = nc.tensor.matmul(
                            qps[par][:, no:no + nw], lhsT,
                            BIGT[:, dd * NT + no: dd * NT + no + nw],
                            start=(dd == 0), stop=(dd == ND - 1))
                v = S.inc(last, "qmm")
                S.wait(nc.vector, "qmm", v)
                S.wait(nc.vector, "xld", vtab)
                if h >= 2:
                    S.wait(nc.vector, "qspa" if par == 0 else "qspb",
                           (h // 2) * 16)
                S.wait(nc.vector, "vself", S.val("vself"))
                S.wait(nc.vector, "rope", S.val("rope"))
                S.inc(nc.vector.tensor_tensor(t1, qps[par], cosT, ALU.mult), "vself")
                S.inc(nc.vector.tensor_tensor(t2[0:64, :], qps[par][64:128, :],
                                              sinT[0:64, :], ALU.mult), "vself")
                S.inc(nc.vector.tensor_tensor(t2[64:128, :], qps[par][0:64, :],
                                              sinT[64:128, :], ALU.mult), "vself")
                S.wait(nc.vector, "vself", S.val("vself"))
                v = S.inc(nc.vector.tensor_tensor(qst[par], t1, t2, ALU.add),
                          "rope", 4)
                S.wait(nc.sync, "rope", v)
                S.inc(nc.sync.dma_start(qsp[h], qst[par]),
                      "qspa" if par == 0 else "qspb", 16)
            S.wait(nc.sync, "qspa", S.val("qspa"))
            S.wait(nc.sync, "qspb", S.val("qspb"))
        barrier()

        if True:
            aes = contextlib.ExitStack()
            _a = aes.enter_context
            SSB = SS.bitcast(BF16)
            kb = [SSB[:, 0:2048], SSB[:, 2048:4096]]
            vraw = SSB[:, 4096:6144]
            vp = [SSB[:, 6144 + i * 2064: 6144 + (i + 1) * 2064] for i in range(4)]
            KT = [_a(nc.sbuf_tensor(f"KT{i}", [128, 32 * 128], BF16)).ap()
                  for i in range(2)]
            qth = [_a(nc.sbuf_tensor(f"qth{i}", [128, NT], BF16)).ap()
                   for i in range(2)]
            cxn = _a(nc.sbuf_tensor("cxn", [128, TH], BF16)).ap()
            cxT = [_a(nc.sbuf_tensor(f"cxT{i}", [128, NT], BF16)).ap()
                   for i in range(2)]
            recb = _a(nc.sbuf_tensor("recb", [128, 1], F32)).ap()
            scps = [_a(nc.psum_tensor(f"scps{i}", [128, TH], F32)).ap()
                    for i in range(2)]
            cxb = [_a(nc.psum_tensor(f"cxb{i}", [128, 512], F32)).ap()
                   for i in range(3)]
            trpA = _a(nc.psum_tensor("trpA", [128, 512], BF16)).ap()

            c = AttnCtx()
            c.TH, c.NTH = TH, NTH
            c.Earea = [BIGT[:, 0:32 * TH], BIGT[:, 32 * TH:64 * TH]]
            c.KT, c.qth, c.cxn, c.cxT, c.recb = KT, qth, cxn, cxT, recb
            c.scps, c.cxb, c.trpA, c.vp = scps, cxb, trpA, vp
            c.ident, c.qsp, c.ctxsp = ident, qsp, ctxsp
            c.vq, c.expd_end, c.ctxg_end = {}, {}, {}
            c.scps_gate = [0, 0]
            c.trp_setup_end = {}
            c.scmm_kv_end = {}
            c.ctxg_kv_end = {}

            def attn_kv_setup(kv):
                kvp = kv % 2
                if kv > 0:
                    S.wait(nc.gpsimd, "trp", c.trp_setup_end[kv - 1])
                ksrc = bass.AP(kc_in.tensor, 0,
                               [[BS * HD, 512 * NKV], [HD, BS], [1, HD]])
                vsrc = bass.AP(vc_in.tensor, 0,
                               [[BS * HD, 512 * NKV], [HD, BS], [1, HD]])
                kbv = []
                for H in range(2):
                    kbv.append(S.inc(nc.gpsimd.indirect_dma_start(
                        kb[H], None, ksrc,
                        IndirectOffsetOnAxis(idxsb[:, H:H + 1], 0),
                        element_offset=kv * 512 * BS * HD),
                        "kvga" if H == 0 else "kvgb", 16))
                if kv > 1:
                    S.wait(nc.vector, "ctxg", c.ctxg_kv_end[kv - 2])
                for H in range(2):
                    S.wait(nc.gpsimd, "vone", S.val("vone"))
                    vv = S.inc(nc.gpsimd.indirect_dma_start(
                        vraw, None, vsrc,
                        IndirectOffsetOnAxis(idxsb[:, H:H + 1], 0),
                        element_offset=kv * 512 * BS * HD), "kvgv", 16)
                    S.wait(nc.vector, "kvgv", vv)
                    if kv < 2:
                        S.inc(nc.vector.memset(
                            vp[kvp * 2 + H].rearrange("p (s e) -> p s e", e=129)
                            [:, :, 128:129], 1.0), "vone", 1)
                    S.inc(nc.vector.tensor_copy(
                        vp[kvp * 2 + H].rearrange("p (s e) -> p s e", e=129)
                        [:, :, 0:128],
                        vraw.rearrange("p (s d) -> p s d", d=128)), "vone", 1)
                S.wait(nc.tensor, "kvga", kbv[0])
                S.wait(nc.tensor, "kvgb", kbv[1])
                S.wait(nc.tensor, "vone", S.val("vone"))
                for g in range(8):
                    if S.val("trp") >= 1:
                        S.wait(nc.tensor, "trev", S.val("trev"))
                    for k in range(4):
                        kt = g * 4 + k
                        s_, H_ = kt // 2, kt % 2
                        last = nc.tensor.transpose(
                            trpA[:, k * 128:(k + 1) * 128],
                            kb[H_][:, s_ * 128:(s_ + 1) * 128], ident)
                    v = S.inc(last, "trp")
                    S.wait(nc.vector, "trp", v)
                    if kv > 1 and g == 0:
                        S.wait(nc.vector, "scmm", c.scmm_kv_end[kv - 2])
                    S.inc(nc.vector.tensor_copy(KT[kvp][:, g * 512:(g + 1) * 512],
                                                trpA), "trev")
                c.trp_setup_end[kv] = S.val("trp")

            units = [(kv, hh, th) for kv in range(NKV) for hh in range(4)
                     for th in range(NTH)]
            prev = None
            for ui, u in enumerate(units):
                if u[1] == 0 and u[2] == 0:
                    if u[0] > 0:
                        c.scmm_kv_end[u[0] - 1] = S.val("scmm")
                        if ui >= 2:
                            c.ctxg_kv_end[u[0] - 1] = None
                    attn_kv_setup(u[0])
                attn_A_dummy = None
                attn_unit(nc, S, c, u, ui, prev)
                if prev is not None and prev[1] == 3 and prev[2] == NTH - 1:
                    c.ctxg_kv_end[prev[0]] = c.ctxg_end[ui - 1]
                prev = u
            lastui = len(units) - 1
            for tc in range(CTC):
                attn_ctx_tc(nc, S, c, prev, lastui, tc)
            attn_T(nc, S, c, prev, lastui)
            S.wait(nc.sync, "ctxsta", S.val("ctxsta"))
            S.wait(nc.sync, "ctxstb", S.val("ctxstb"))
            aes.close()
        barrier()

        for h in range(NH):
            S.inc(nc.sync.dma_start(BIGT[:, h * NT:(h + 1) * NT], ctxsp[h]),
                  "ctxld", 16)
        S.wait(nc.tensor, "ctxld", S.val("ctxld"))

        def out_proj(w_dram, dst_dram, tag, lora=False, hT=None, b0sb=None,
                     blsb=None, mtb=None, accb=None, yps=None, wsl_ap=None,
                     stat_buf=None, stat_sem=None, agg_buf=None):
            with (
                contextlib.nullcontext() if wsl_ap is not None else
                nc.sbuf_tensor(f"wsl{tag}", [128, ND * OS_W], BF16) as wsl_g,
                nc.sbuf_tensor(f"ost{tag}0", [128, OS_W], BF16) as ost0_g,
                nc.sbuf_tensor(f"ost{tag}1", [128, OS_W], BF16) as ost1_g,
                nc.psum_tensor(f"wps{tag}0", [128, OS_W], F32) as wps0_g,
                nc.psum_tensor(f"wps{tag}1", [128, OS_W], F32) as wps1_g,
            ):
                wsl = wsl_ap if wsl_ap is not None else wsl_g.ap()
                ost = [ost0_g.ap(), ost1_g.ap()]
                wps = [wps0_g.ap(), wps1_g.ap()]
                nos = D // OS_W
                it = 0
                b_woev = S.val("woev")
                b_comb = S.val("comb")
                b_wosta = S.val("wosta")
                b_wostb = S.val("wostb")
                for os_ in range(nos):
                    if os_ > 0:
                        S.wait(nc.gpsimd, "womm", S.val("womm"))
                    QW = ND // 4 * OS_W
                    for q in range(4):
                        S.inc(nc.gpsimd.dma_start(
                            wsl[:, q * QW:(q + 1) * QW],
                            w_dram[os_][:, q * QW:(q + 1) * QW]), f"wsl{q}", 16)
                    if lora:
                        S.wait(nc.gpsimd, "ymm", S.val("ymm"))
                        for l in range(3):
                            S.inc(nc.gpsimd.dma_start(
                                blsb[0:16, l * OS_W:(l + 1) * OS_W],
                                bl_in[l + 1][:, os_ * OS_W:(os_ + 1) * OS_W]),
                                "bld", 16)
                        S.inc(nc.gpsimd.dma_start(
                            blsb[0:16, 3 * OS_W:4 * OS_W],
                            bl_in[0][:, os_ * OS_W:(os_ + 1) * OS_W]), "bld", 16)
                    for tc in range(NTC):
                        pw = it % 2
                        pm = it % 2
                        if lora:
                            S.wait(nc.sync, "comb", max(0, S.val("comb") - 1))
                            for l in range(3):
                                S.inc(nc.sync.dma_start(
                                    mtb[:, (pm * 3 + l) * OS_W:(pm * 3 + l + 1) * OS_W],
                                    m_in[l][tc * 128:(tc + 1) * 128,
                                            os_ * OS_W:(os_ + 1) * OS_W]), "mld", 16)
                            vml = S.val("mld")
                        if it >= 2:
                            S.wait(nc.tensor, "woev" if not lora else "comb",
                                   (b_woev if not lora else b_comb) + it - 1)
                        for dd in range(ND):
                            if tc == 0 and dd % (ND // 4) == 0:
                                S.wait(nc.tensor, f"wsl{dd // (ND // 4)}",
                                       S.val(f"wsl{dd // (ND // 4)}"))
                            last = nc.tensor.matmul(
                                wps[pw],
                                BIGT[:, dd * NT + tc * 128: dd * NT + (tc + 1) * 128],
                                wsl[:, dd * OS_W:(dd + 1) * OS_W],
                                start=(dd == 0), stop=(not lora) and (dd == ND - 1))
                        if lora:
                            S.wait(nc.tensor, "b0ld", 16)
                            last = nc.tensor.matmul(
                                wps[pw], ones1,
                                b0sb[0:1, os_ * OS_W:(os_ + 1) * OS_W],
                                start=False, stop=True)
                        vbase = S.inc(last, "womm")
                        if lora:
                            if it >= 1:
                                S.wait(nc.tensor, "comb", b_comb + it)
                            S.wait(nc.tensor, "bld", S.val("bld"))
                            S.wait(nc.tensor, "htev", S.val("htev"))
                            for sl in range(4):
                                l = (sl + 1) % 4
                                last = nc.tensor.matmul(
                                    yps[sl], hT[l][:, tc * 128:(tc + 1) * 128],
                                    blsb[0:16, sl * OS_W:(sl + 1) * OS_W],
                                    start=True, stop=True)
                            vy = S.inc(last, "ymm")
                            S.wait(nc.vector, "ymm", vy)
                            S.wait(nc.vector, "mld", vml)
                            S.wait(nc.vector, "vself", S.val("vself"))
                            S.wait(nc.vector, "comb", S.val("comb"))
                            S.inc(nc.vector.tensor_copy(accb, yps[3]), "vself")
                            for sl in range(3):
                                S.wait(nc.vector, "vself", S.val("vself"))
                                S.inc(nc.vector.copy_predicated(
                                    accb,
                                    mtb[:, (pm * 3 + sl) * OS_W:(pm * 3 + sl + 1) * OS_W],
                                    yps[sl]), "vself")
                            S.wait(nc.vector, "vself", S.val("vself"))
                            S.wait(nc.vector, "womm", vbase)
                            if it >= 2:
                                S.wait(nc.vector, "wosta" if pw == 0 else "wostb",
                                       (b_wosta if pw == 0 else b_wostb)
                                       + (it // 2) * 16)
                            vv = S.inc(nc.vector.tensor_tensor(ost[pw], accb,
                                                               wps[pw], ALU.add),
                                       "comb")
                            if stat_buf is not None:
                                S.wait(nc.vector, "comb", vv)
                                vs = S.inc(nc.vector.bn_stats(
                                    stat_buf[:, tc * 48 + os_ * 6:
                                             tc * 48 + os_ * 6 + 6], ost[pw]),
                                    stat_sem)
                                if os_ == nos - 1:
                                    S.wait(nc.vector, stat_sem, vs)
                                    S.inc(nc.vector.bn_aggr(
                                        agg_buf[:, 2 * tc:2 * tc + 2],
                                        stat_buf[:, tc * 48:(tc + 1) * 48]),
                                        stat_sem)
                        else:
                            S.wait(nc.vector, "womm", vbase)
                            if it >= 2:
                                S.wait(nc.vector, "wosta" if pw == 0 else "wostb",
                                       (b_wosta if pw == 0 else b_wostb)
                                       + (it // 2) * 16)
                            ve = S.inc(nc.vector.tensor_copy(ost[pw], wps[pw]),
                                       "woev")
                            if stat_buf is not None:
                                S.wait(nc.vector, "woev", ve)
                                vs = S.inc(nc.vector.bn_stats(
                                    stat_buf[:, tc * 48 + os_ * 6:
                                             tc * 48 + os_ * 6 + 6], ost[pw]),
                                    stat_sem)
                                if os_ == nos - 1:
                                    S.wait(nc.vector, stat_sem, vs)
                                    S.inc(nc.vector.bn_aggr(
                                        agg_buf[:, 2 * tc:2 * tc + 2],
                                        stat_buf[:, tc * 48:(tc + 1) * 48]),
                                        stat_sem)
                        if lora:
                            S.wait(nc.sync, "comb", S.val("comb"))
                        else:
                            S.wait(nc.sync, "woev", S.val("woev"))
                        S.inc(nc.sync.dma_start(
                            dst_dram[tc * 128:(tc + 1) * 128,
                                     os_ * OS_W:(os_ + 1) * OS_W], ost[pw]),
                            "wosta" if pw == 0 else "wostb", 16)
                        it += 1
                S.wait(nc.sync, "wosta", S.val("wosta"))
                S.wait(nc.sync, "wostb", S.val("wostb"))

        with nc.psum_tensor("stats_w", [128, NTC * 48], F32) as stats_w_g:
            out_proj(wo_in, attnsp, "o", stat_buf=stats_w_g.ap(),
                     stat_sem="fstat", agg_buf=aggW)
        barrier()

        with (
            nc.sbuf_tensor("hT0", [16, NTC * 128], BF16) as hT0_g,
            nc.sbuf_tensor("hT1", [16, NTC * 128], BF16) as hT1_g,
            nc.sbuf_tensor("hT2", [16, NTC * 128], BF16) as hT2_g,
            nc.sbuf_tensor("hT3", [16, NTC * 128], BF16) as hT3_g,
        ):
            hT = [hT0_g.ap(), hT1_g.ap(), hT2_g.ap(), hT3_g.ap()]
            p5 = contextlib.ExitStack()
            _p = p5.enter_context
            x2b = [_p(nc.sbuf_tensor("x2b0", [128, D], BF16)).ap(),
                   _p(nc.sbuf_tensor("x2b1", [128, D], BF16)).ap()]
            xnb2 = _p(nc.sbuf_tensor("xnb2", [128, D], BF16)).ap()
            atsb = _p(nc.sbuf_tensor("atsb", [128, ND * 64], BF16)).ap()
            h8b = _p(nc.sbuf_tensor("h8b", [128, 64], BF16)).ap()
            trpb = [_p(nc.psum_tensor("trp0b", [128, 512], BF16)).ap(),
                    _p(nc.psum_tensor("trp1b", [128, 512], BF16)).ap()]
            hps = _p(nc.psum_tensor("hps", [128, 64], F32)).ap()
            trpC = _p(nc.psum_tensor("trpC", [128, 512], BF16)).ap()
            rsf5 = _p(nc.sbuf_tensor("rsf5", [128, 8], F32)).ap()
            a5 = _p(nc.sbuf_tensor("a5", [128, 4], F32)).ap()
            nm5 = _p(nc.sbuf_tensor("nm5", [128, 4], F32)).ap()
            vat = S.inc(nc.gpsimd.dma_start(atsb, at_in), "hld", 16)
            SSB5 = SS.bitcast(BF16)
            ats = [SSB5[:, 0:D], SSB5[:, D:2 * D]]
            xtf = SS[:, D:2 * D]
            outst_b = S.val("outst")
            n1v, ttv, n2v, nm1v, nm2v, l2v, bnv = {}, {}, {}, {}, {}, {}, {}
            trev5 = {}
            for tc in range(NTC):
                p = tc % 2
                asem = "xlda" if p == 0 else "xldb"
                at_t = ats[p]
                x2t = x2b[p]
                if tc >= 2:
                    S.wait(nc.scalar, "sself", n1v[tc - 2])
                va = S.inc(nc.scalar.dma_start(
                    at_t, attnsp[tc * 128:(tc + 1) * 128, :]), asem, 16)
                if tc >= 1:
                    S.wait(nc.sync, "lnd", ttv[tc - 1])
                vx = S.inc(nc.sync.dma_start(
                    xtf, x_in[tc * 128:(tc + 1) * 128, :]), "xldc", 16)
                S.wait(nc.scalar, "sself", S.val("sself"))
                if tc >= 2:
                    S.wait(nc.scalar, "lnd", nm1v[tc - 2])
                l1 = S.inc(nc.scalar.activation(
                    rsf5[:, 4 * p:4 * p + 1], aggW[:, 2 * tc + 1:2 * tc + 2],
                    AF.Ln, bias=epsb), "sself")
                S.wait(nc.scalar, "sself", l1)
                e1 = S.inc(nc.scalar.activation(
                    rsf5[:, 4 * p + 1:4 * p + 2], rsf5[:, 4 * p:4 * p + 1],
                    AF.Exp, scale=-0.5), "lnd")
                S.wait(nc.vector, "lnd", e1)
                if tc >= 2:
                    S.wait(nc.vector, "sself", n1v[tc - 2])
                nm1v[tc] = S.inc(nc.vector.tensor_scalar(
                    nm5[:, 2 * p:2 * p + 1], aggW[:, 2 * tc:2 * tc + 1],
                    rsf5[:, 4 * p + 1:4 * p + 2], -1.0, ALU.mult, ALU.mult),
                    "lnd")
                S.wait(nc.scalar, asem, va)
                S.wait(nc.scalar, "lnd", nm1v[tc])
                if tc >= 2:
                    S.wait(nc.scalar, "vself", bnv[tc - 2])
                    S.wait(nc.scalar, "outst", outst_b + 16 * (tc - 1))
                n1v[tc] = S.inc(nc.scalar.activation(
                    x2t, at_t, AF.Identity, bias=nm5[:, 2 * p:2 * p + 1],
                    scale=rsf5[:, 4 * p + 1:4 * p + 2]), "sself")
                S.wait(nc.vector, "sself", n1v[tc])
                S.wait(nc.vector, "xldc", vx)
                ttv[tc] = S.inc(nc.vector.tensor_tensor(x2t, x2t, xtf, ALU.add),
                                "lnd")
                S.wait(nc.gpsimd, "lnd", ttv[tc])
                S.inc(nc.gpsimd.dma_start(x2sp[tc * 128:(tc + 1) * 128, :], x2t),
                      "outst", 16)
                S.wait(nc.vector, "lnd", ttv[tc])
                S.wait(nc.vector, "vself", S.val("vself"))
                for i in range(8):
                    S.inc(nc.vector.bn_stats(stats[:, 6 * i:6 * i + 6],
                                             x2t[:, 512 * i:512 * (i + 1)]),
                          "vself")
                bnv[tc] = S.val("vself")
                S.wait(nc.vector, "vself", bnv[tc])
                if tc >= 2:
                    S.wait(nc.vector, "sself", l2v[tc - 2])
                ag2 = S.inc(nc.vector.bn_aggr(a5[:, 2 * p:2 * p + 2], stats),
                            "lnd")
                S.wait(nc.scalar, "lnd", ag2)
                S.wait(nc.scalar, "sself", S.val("sself"))
                if tc >= 2:
                    S.wait(nc.scalar, "lnd", nm2v[tc - 2])
                l2v[tc] = S.inc(nc.scalar.activation(
                    rsf5[:, 4 * p + 2:4 * p + 3], a5[:, 2 * p + 1:2 * p + 2],
                    AF.Ln, bias=epsb), "sself")
                S.wait(nc.scalar, "sself", l2v[tc])
                e2 = S.inc(nc.scalar.activation(
                    rsf5[:, 4 * p + 3:4 * p + 4], rsf5[:, 4 * p + 2:4 * p + 3],
                    AF.Exp, scale=-0.5), "lnd")
                S.wait(nc.vector, "lnd", e2)
                if tc >= 2:
                    S.wait(nc.vector, "sself", n2v[tc - 2])
                nm2v[tc] = S.inc(nc.vector.tensor_scalar(
                    nm5[:, 2 * p + 1:2 * p + 2], a5[:, 2 * p:2 * p + 1],
                    rsf5[:, 4 * p + 3:4 * p + 4], -1.0, ALU.mult, ALU.mult),
                    "lnd")
                S.wait(nc.scalar, "lnd", nm2v[tc])
                if tc >= 1:
                    S.wait(nc.scalar, "trev", trev5[tc - 1][0])
                    S.wait(nc.scalar, "trevs", trev5[tc - 1][1])
                n2v[tc] = S.inc(nc.scalar.activation(
                    xnb2, x2t, AF.Identity, bias=nm5[:, 2 * p + 1:2 * p + 2],
                    scale=rsf5[:, 4 * p + 3:4 * p + 4]), "sself")
                S.wait(nc.tensor, "sself", n2v[tc])
                transpose_to_bigt(xnb2, tc, trpb, split_evac=True)
                trev5[tc] = (S.val("trev"), S.val("trevs"))
                S.wait(nc.tensor, "trev", S.val("trev"))
                S.wait(nc.tensor, "trevs", S.val("trevs"))
                if tc == 0:
                    S.wait(nc.tensor, "hld", vat)
                if tc >= 1:
                    S.wait(nc.tensor, "hact", tc)
                for dd in range(ND):
                    last = nc.tensor.matmul(
                        hps, BIGT[:, dd * NT + tc * 128: dd * NT + (tc + 1) * 128],
                        atsb[:, dd * 64:(dd + 1) * 64],
                        start=(dd == 0), stop=(dd == ND - 1))
                v = S.inc(last, "hmm")
                S.wait(nc.scalar, "hmm", v)
                if tc >= 1:
                    S.wait(nc.scalar, "htr", tc)
                v = S.inc(nc.scalar.activation(h8b, hps, AF.Copy, scale=8.0),
                          "hact")
                S.wait(nc.tensor, "hact", v)
                if tc >= 1:
                    S.wait(nc.tensor, "htev", tc)
                for l in range(4):
                    last = nc.tensor.transpose(trpC[0:16, l * 128:(l + 1) * 128],
                                               h8b[:, l * 16:(l + 1) * 16], ident)
                v = S.inc(last, "htr")
                S.wait(nc.vector, "htr", v)
                for l in range(4):
                    last = nc.vector.tensor_copy(
                        hT[l][:, tc * 128:(tc + 1) * 128],
                        trpC[0:16, l * 128:(l + 1) * 128])
                S.inc(last, "htev")
            S.wait(nc.sync, "outst", S.val("outst"))
            S.wait(nc.tensor, "htev", S.val("htev"))
            p5.close()
            barrier()

            with (
                nc.sbuf_tensor("b0sbf", [1, D], BF16) as b0sb_g,
                nc.sbuf_tensor("blsb", [16, 4 * OS_W], BF16) as blsb_g,
                nc.sbuf_tensor("mtb", [128, 6 * OS_W], U8) as mtb_g,
                nc.sbuf_tensor("accb", [128, OS_W], F32) as accb_g,
                nc.psum_tensor("yps0", [128, OS_W], F32) as yps0_g,
                nc.psum_tensor("yps1", [128, OS_W], F32) as yps1_g,
                nc.psum_tensor("yps2", [128, OS_W], F32) as yps2_g,
                nc.psum_tensor("yps3", [128, OS_W], F32) as yps3_g,
                nc.psum_tensor("stats_f", [128, NTC * 48], F32) as stats_f_g,
            ):
                b0sb = b0sb_g.ap()
                S.inc(nc.gpsimd.dma_start(b0sb, b0_in), "b0ld", 16)
                out_proj(w0_in, ffnsp, "f", lora=True, hT=hT, b0sb=b0sb,
                         blsb=blsb_g.ap(),
                         mtb=mtb_g.ap(), accb=accb_g.ap(),
                         yps=[yps0_g.ap(), yps1_g.ap(), yps2_g.ap(), yps3_g.ap()],
                         wsl_ap=SS.bitcast(BF16),
                         stat_buf=stats_f_g.ap(), stat_sem="fstat",
                         agg_buf=aggT)
        barrier()

        if True:
            p7 = contextlib.ExitStack()
            SSB7 = SS.bitcast(BF16)
            fts = [SSB7[:, 0:D], SSB7[:, D:2 * D]]
            x2s = [SSB7[:, 2 * D:3 * D], SSB7[:, 3 * D:4 * D]]
            rsf = p7.enter_context(nc.sbuf_tensor("rsf", [128, 4], F32)).ap()
            nmrb = p7.enter_context(nc.sbuf_tensor("nmrb", [128, 2], F32)).ap()
            obuf = p7.enter_context(nc.sbuf_tensor("obuf", [128, D], F32)).ap()
            p7ps = [obuf[:, k * 512:(k + 1) * 512] for k in range(8)]
            bsem = ["fina", "qspa", "ctxsta", "wosta",
                    "finb", "qspb", "ctxstb", "wostb"]
            qeng = [nc.sync] * 4 + [nc.scalar] * 4
            bbase = {s: S.val(s) for s in bsem}
            p7base = S.val("p7v")
            NCH = D // 512
            ln_val, nm_val, ex_val = {}, {}, {}
            for tc in range(NTC):
                p = tc % 2
                ft, x2t = fts[p], x2s[p]
                fsem = "xlda" if p == 0 else "xldb"
                xsem = "xldc" if p == 0 else "xldd"
                slot_free = p7base + NCH * (tc - 1)
                if tc >= 2:
                    S.wait(nc.scalar, "p7v", slot_free)
                vft = S.inc(nc.scalar.dma_start(
                    ft, ffnsp[tc * 128:(tc + 1) * 128, :]), fsem, 16)
                if tc >= 2:
                    S.wait(nc.sync, "p7v", slot_free)
                vx2 = S.inc(nc.sync.dma_start(
                    x2t, x2sp[tc * 128:(tc + 1) * 128, :]), xsem, 16)
                if tc >= 2:
                    S.wait(nc.scalar, "p7v", slot_free)
                S.wait(nc.scalar, "sself", S.val("sself"))
                ln_val[tc] = S.inc(nc.scalar.activation(
                    rsf[:, 2 * p:2 * p + 1], aggT[:, 2 * tc + 1:2 * tc + 2],
                    AF.Ln, bias=epsb), "sself")
                S.wait(nc.scalar, "sself", ln_val[tc])
                ex_val[tc] = S.inc(nc.scalar.activation(
                    rsf[:, 2 * p + 1:2 * p + 2], rsf[:, 2 * p:2 * p + 1],
                    AF.Exp, scale=-0.5), "lnd")
                S.wait(nc.vector, "lnd", ex_val[tc])
                if tc >= 2:
                    S.wait(nc.vector, "p7v", slot_free)
                nm_val[tc] = S.inc(nc.vector.tensor_scalar(
                    nmrb[:, p:p + 1], aggT[:, 2 * tc:2 * tc + 1],
                    rsf[:, 2 * p + 1:2 * p + 2], -1.0, ALU.mult, ALU.mult),
                    "lnd")
                S.wait(nc.scalar, "lnd", nm_val[tc])
                S.wait(nc.scalar, fsem, vft)
                S.wait(nc.vector, xsem, vx2)
                for k in range(NCH):
                    if tc >= 1:
                        S.wait(nc.scalar, bsem[k], bbase[bsem[k]] + 16 * tc)
                    vact = S.inc(nc.scalar.activation(
                        p7ps[k], ft[:, k * 512:(k + 1) * 512], AF.Identity,
                        bias=nmrb[:, p:p + 1],
                        scale=rsf[:, 2 * p + 1:2 * p + 2]), "sself")
                    S.wait(nc.vector, "sself", vact)
                    vaf = S.inc(nc.vector.tensor_tensor(
                        p7ps[k], p7ps[k],
                        x2t[:, k * 512:(k + 1) * 512], ALU.add), "p7v")
                    S.wait(qeng[k], "p7v", vaf)
                    S.inc(qeng[k].dma_start(
                        out_d[tc * 128:(tc + 1) * 128, k * 512:(k + 1) * 512],
                        p7ps[k]), bsem[k], 16)
            for s in bsem:
                S.wait(nc.sync, s, S.val(s))
            p7.close()
        barrier()

    nc.finalize()
    return nc




def _threefry_uniform_key42(n):
    def rotl(x, d):
        return ((x << np.uint32(d)) | (x >> np.uint32(32 - d))).astype(np.uint32)

    def tf2x32(k0, k1, x0, x1):
        ks0, ks1 = np.uint32(k0), np.uint32(k1)
        ks2 = np.uint32(ks0 ^ ks1 ^ np.uint32(0x1BD11BDA))
        x0 = (x0 + ks0).astype(np.uint32)
        x1 = (x1 + ks1).astype(np.uint32)
        rots = [[13, 15, 26, 6], [17, 29, 16, 24]]
        ks = [ks1, ks2, ks0]
        for i in range(5):
            for r in rots[i % 2]:
                x0 = (x0 + x1).astype(np.uint32)
                x1 = rotl(x1, r)
                x1 = (x1 ^ x0).astype(np.uint32)
            x0 = (x0 + ks[i % 3]).astype(np.uint32)
            x1 = (x1 + ks[(i + 1) % 3] + np.uint32(i + 1)).astype(np.uint32)
        return x0, x1

    idx = np.arange(n, dtype=np.uint64)
    lo = (idx & np.uint64(0xFFFFFFFF)).astype(np.uint32)
    hi = (idx >> np.uint64(32)).astype(np.uint32)
    o0, o1 = tf2x32(0, 42, hi, lo)
    bits = o0 ^ o1
    return ((bits >> np.uint32(9)) | np.uint32(0x3F800000)).view(np.float32) - np.float32(1.0)


def _host_prep(Wq, Wo, W0, A, Bm, b0):
    WQH = np.ascontiguousarray(
        Wq.reshape(NH, 128, ND, 128).transpose(0, 3, 2, 1).reshape(NH, 128, D))
    nos = D // OS_W
    WOH = np.ascontiguousarray(
        Wo.reshape(nos, OS_W, ND, 128).transpose(0, 3, 2, 1).reshape(nos, 128, ND * OS_W))
    W0H = np.ascontiguousarray(
        W0.reshape(nos, OS_W, ND, 128).transpose(0, 3, 2, 1).reshape(nos, 128, ND * OS_W))
    ATH = np.ascontiguousarray(
        A.reshape(4, 16, ND, 128).transpose(3, 2, 0, 1).reshape(128, ND * 64))
    BLT = np.ascontiguousarray(Bm.transpose(0, 2, 1))
    b0h = np.ascontiguousarray(b0.reshape(1, D))

    Sq = 8192
    inv = (1.0 / (np.float32(10000.0) ** (np.arange(0, HD, 2, dtype=np.float32)
                                          / np.float32(HD)))).astype(np.float32)
    t = np.arange(Sq, dtype=np.float32)
    freqs = np.outer(t, inv)
    emb = np.concatenate([freqs, freqs], axis=-1)
    cos = np.cos(emb).astype(np.float32)
    sinf = np.sin(emb).astype(np.float32)
    sinf = sinf.copy()
    sinf[:, 0:64] *= -1.0

    u = _threefry_uniform_key42(2 * 8192 * D).reshape(16384, D)
    masks = []
    for l in range(1, 4):
        lo, hi = np.float32(0.25 * l), np.float32(0.25 * (l + 1))
        masks.append(((u >= lo) & (u < hi)).astype(np.uint8))

    ident = np.eye(128, dtype=BF)
    return WQH, WOH, W0H, ATH, BLT, b0h, cos, sinf, masks, ident


def make_in_maps(L0_attn_hidden_states, L0_attn_k_cache, L0_attn_v_cache, L0_attn_Wq,
                 L0_attn_Wo, L0_attn_block_table, L0_ffn_W0, L0_ffn_b0, L0_ffn_A,
                 L0_ffn_B, NT):
    WQH, WOH, W0H, ATH, BLT, b0h, cos, sinf, masks, ident = _host_prep(
        np.asarray(L0_attn_Wq, np.float32), np.asarray(L0_attn_Wo, np.float32),
        np.asarray(L0_ffn_W0, np.float32), np.asarray(L0_ffn_A, np.float32),
        np.asarray(L0_ffn_B, np.float32), np.asarray(L0_ffn_b0, np.float32))
    X = np.ascontiguousarray(
        np.asarray(L0_attn_hidden_states, np.float32).reshape(16384, D))
    kc = np.ascontiguousarray(np.asarray(L0_attn_k_cache, np.float32).transpose(1, 0, 2, 3))
    vc = np.ascontiguousarray(np.asarray(L0_attn_v_cache, np.float32).transpose(1, 0, 2, 3))
    bt = np.ascontiguousarray(np.asarray(L0_attn_block_table, np.int32))

    in_maps = []
    for c in range(N_CORES):
        r0 = c * NT
        p0 = r0 % 8192
        m = {
            "x": X[r0:r0 + NT], "wq": WQH, "wo": WOH, "w0": W0H,
            "kc": kc, "vc": vc, "bt": bt, "at": ATH, "bl": BLT, "b0": b0h,
            "cosT": np.ascontiguousarray(cos[p0:p0 + NT].T),
            "sinT": np.ascontiguousarray(sinf[p0:p0 + NT].T),
            "ident": ident,
        }
        for l in range(3):
            m[f"m{l + 1}"] = np.ascontiguousarray(masks[l][r0:r0 + NT])
        in_maps.append(m)
    return in_maps


def kernel(L0_attn_hidden_states, L0_attn_k_cache, L0_attn_v_cache, L0_attn_Wq,
           L0_attn_Wo, L0_attn_block_table, L0_ffn_W0, L0_ffn_b0, L0_ffn_A, L0_ffn_B):
    NT = 16384 // N_CORES
    in_maps = make_in_maps(L0_attn_hidden_states, L0_attn_k_cache, L0_attn_v_cache,
                           L0_attn_Wq, L0_attn_Wo, L0_attn_block_table, L0_ffn_W0,
                           L0_ffn_b0, L0_ffn_A, L0_ffn_B, NT)
    nc = build_program(NT)
    res = run_bass_kernel_spmd(nc, in_maps, core_ids=list(range(N_CORES)))
    out = np.concatenate([res.results[c]["out"] for c in range(N_CORES)], axis=0)
    return out.reshape(2, 8192, D).astype(np.float32)


# version 21 (final)
# speedup vs baseline: 1.1672x; 1.0130x over previous
import contextlib
import numpy as np
import ml_dtypes
import concourse.bass as bass
import concourse.mybir as mybir
from concourse.bass import IndirectOffsetOnAxis
from concourse.bass_utils import run_bass_kernel_spmd
import concourse.bass_utils as _bu

_orig_run_command = _bu.run_command

def _run_command_ldwopt(cmd, *a, **k):
    cmd = ["--enable-ldw-opt=true" if c == "--enable-ldw-opt=false" else c
           for c in cmd]
    return _orig_run_command(cmd, *a, **k)

_bu.run_command = _run_command_ldwopt

F32 = mybir.dt.float32
BF16 = mybir.dt.bfloat16
I32 = mybir.dt.int32
U8 = mybir.dt.uint8
AF = mybir.ActivationFunctionType
ALU = mybir.AluOpType
BF = ml_dtypes.bfloat16

N_CORES = 8
D = 4096
NH = 32
NKV = 8
HD = 128
NBLK = 256
BS = 16
EPS = 1e-5
SCALE = 1.0 / float(np.sqrt(128.0))
ND = D // 128
OS_W = 512


def chunks(total, step=512):
    out = []
    o = 0
    while o < total:
        w = min(step, total - o)
        out.append((o, w))
        o += w
    return out


class Sems:
    def __init__(self, nc, stack):
        self.nc, self.stack, self.s, self.c = nc, stack, {}, {}

    def new(self, *names):
        for n in names:
            self.s[n] = self.stack.enter_context(self.nc.semaphore(n))
            self.c[n] = 0

    def inc(self, inst, name, d=1):
        inst.then_inc(self.s[name], d)
        self.c[name] += d
        return self.c[name]

    def inc2(self, inst, n1, n2, d2=1):
        inst.then_inc(self.s[n1], 1)
        self.c[n1] += 1
        inst.then_inc(self.s[n2], d2)
        self.c[n2] += d2
        return self.c[n2]

    def val(self, name):
        return self.c[name]

    def wait(self, eng, name, v):
        if v > 0:
            eng.wait_ge(self.s[name], v)


class AttnCtx:
    pass


def attn_scores_kt(nc, S, c, u, ui, kt):
    kv, hh, th = u
    h = kv * 4 + hh
    parh = h % 2
    pk = kt % 2
    E = c.Earea[ui % 2]
    S.wait(nc.tensor, "expd", c.scps_gate[pk])
    for (no, nw) in chunks(c.TH):
        last = nc.tensor.matmul(
            c.scps[pk][:, no:no + nw],
            c.KT[kv % 2][:, kt * 128:(kt + 1) * 128],
            c.qth[parh][:, th * c.TH + no: th * c.TH + no + nw],
            start=True, stop=True)
    v2 = S.inc(last, "scmm")
    S.wait(nc.scalar, "scmm", v2)
    c.scps_gate[pk] = S.inc(
        nc.scalar.activation(E[:, kt * c.TH:(kt + 1) * c.TH],
                             c.scps[pk], AF.Exp, scale=SCALE), "expd")


def attn_ctx_tc(nc, S, c, u, ui, tc, ktr=(0, 32)):
    kv, hh, th = u
    E = c.Earea[ui % 2]
    CTC = c.TH // 128
    bk, slot = tc // 3, tc % 3
    if tc == 0 and ktr[0] == 0:
        S.wait(nc.tensor, "cxev", S.val("cxev"))
    for kt in range(*ktr):
        if tc == 0:
            S.wait(nc.tensor, "expd", c.expd_end[ui] - 31 + kt)
        last = nc.tensor.matmul(
            c.cxb[bk][:, slot * 129:slot * 129 + 129],
            E[:, kt * c.TH + tc * 128: kt * c.TH + (tc + 1) * 128],
            c.vp[(kv % 2) * 2 + kt % 2][:, (kt // 2) * 129:(kt // 2) * 129 + 129],
            start=(kt == 0), stop=(kt == 31))
    if ktr[1] < 32:
        return
    v3 = S.inc(last, "ctxg")
    if slot == 2 or tc == CTC - 1:
        S.wait(nc.vector, "ctxg", v3)
        for tcc in range(bk * 3, tc + 1):
            sl = tcc % 3
            S.wait(nc.vector, "vself", S.val("vself"))
            S.wait(nc.vector, "cxev", S.val("cxev"))
            S.inc(nc.vector.reciprocal(
                c.recb, c.cxb[bk][:, sl * 129 + 128: sl * 129 + 129]), "vself")
            S.wait(nc.vector, "vself", S.val("vself"))
            inst = nc.vector.tensor_scalar(
                c.cxn[:, tcc * 128:(tcc + 1) * 128],
                c.cxb[bk][:, sl * 129: sl * 129 + 128],
                c.recb, None, ALU.mult)
            S.inc(inst, "cxev" if tcc == tc else "vself")
    if tc == CTC - 1:
        c.ctxg_end[ui] = S.val("ctxg")


def attn_T(nc, S, c, u, ui):
    kv, hh, th = u
    h = kv * 4 + hh
    parh = h % 2
    CTC = c.TH // 128
    S.wait(nc.tensor, "cxev", S.val("cxev"))
    for g in range((CTC + 3) // 4):
        S.wait(nc.tensor, "trev", S.val("trev"))
        n_in_g = min(4, CTC - g * 4)
        for k in range(n_in_g):
            tc = g * 4 + k
            last = nc.tensor.transpose(c.trpA[:, k * 128:(k + 1) * 128],
                                       c.cxn[:, tc * 128:(tc + 1) * 128], c.ident)
        v4 = S.inc(last, "trp")
        S.wait(nc.vector, "trp", v4)
        if th == 0 and g == 0 and h >= 2:
            S.wait(nc.vector, "ctxsta" if parh == 0 else "ctxstb",
                   (h // 2) * 16)
        dst = bass.AP(c.cxT[parh].tensor,
                      c.cxT[parh].offset + th * c.TH + g * 4 * 128,
                      [[c.cxT[parh].ap[0][0], 128], [128, n_in_g], [1, 128]])
        S.inc(nc.vector.tensor_copy(
            dst, c.trpA.rearrange("p (a b) -> p a b", b=128)[:, 0:n_in_g, :]),
            "trev")
    if th == c.NTH - 1:
        S.wait(nc.sync, "trev", S.val("trev"))
        S.inc(nc.sync.dma_start(c.ctxsp[h], c.cxT[parh]),
              "ctxsta" if parh == 0 else "ctxstb", 16)


def attn_unit(nc, S, c, u, ui, prev):
    kv, hh, th = u
    h = kv * 4 + hh
    parh = h % 2
    qsem = "qlda" if parh == 0 else "qldb"
    if th == 0:
        if h >= 2:
            S.wait(nc.sync, "scmm", S.val("scmm"))
        c.vq[h] = S.inc(nc.sync.dma_start(c.qth[parh], c.qsp[h]), qsem, 16)
    S.wait(nc.tensor, qsem, c.vq[h])
    if hh == 0 and th == 0:
        S.wait(nc.tensor, "trev", S.val("trev"))
    if ui >= 2:
        S.wait(nc.scalar, "ctxg", c.ctxg_end[ui - 2])
    CTC = c.TH // 128
    for j in range(16):
        for kt in range(2 * j, 2 * j + 2):
            attn_scores_kt(nc, S, c, u, ui, kt)
            if kt == 31:
                c.expd_end[ui] = S.val("expd")
        if prev is not None and j // 2 < CTC:
            attn_ctx_tc(nc, S, c, prev, ui - 1, j // 2,
                        (0, 16) if j % 2 == 0 else (16, 32))
    if prev is not None:
        attn_T(nc, S, c, prev, ui - 1)


def build_program(NT=2048, debug=False):
    assert NT % 256 == 0
    NTC = NT // 128
    TH = min(1024, NT // 2)
    NTH = NT // TH
    CTC = TH // 128
    NS = NT // 512

    nc = bass.Bass("TRN2", target_bir_lowering=False)

    x_in = nc.dram_tensor("x", [NT, D], F32, kind="ExternalInput").ap()
    wq_in = nc.dram_tensor("wq", [NH, 128, D], F32, kind="ExternalInput").ap()
    wo_in = nc.dram_tensor("wo", [D // OS_W, 128, ND * OS_W], F32, kind="ExternalInput").ap()
    w0_in = nc.dram_tensor("w0", [D // OS_W, 128, ND * OS_W], F32, kind="ExternalInput").ap()
    kc_in = nc.dram_tensor("kc", [NKV, 512, BS, HD], F32, kind="ExternalInput").ap()
    vc_in = nc.dram_tensor("vc", [NKV, 512, BS, HD], F32, kind="ExternalInput").ap()
    bt_in = nc.dram_tensor("bt", [NBLK], I32, kind="ExternalInput").ap()
    at_in = nc.dram_tensor("at", [128, ND * 64], F32, kind="ExternalInput").ap()
    bl_in = nc.dram_tensor("bl", [4, 16, D], F32, kind="ExternalInput").ap()
    b0_in = nc.dram_tensor("b0", [1, D], F32, kind="ExternalInput").ap()
    cos_in = nc.dram_tensor("cosT", [128, NT], F32, kind="ExternalInput").ap()
    sin_in = nc.dram_tensor("sinT", [128, NT], F32, kind="ExternalInput").ap()
    m_in = [nc.dram_tensor(f"m{l}", [NT, D], U8, kind="ExternalInput").ap()
            for l in range(1, 4)]
    id_in = nc.dram_tensor("ident", [128, 128], BF16, kind="ExternalInput").ap()
    out_d = nc.dram_tensor("out", [NT, D], F32, kind="ExternalOutput").ap()

    sk = "ExternalOutput" if debug else "Internal"
    qsp = nc.dram_tensor("qsp", [NH, 128, NT], BF16, kind=sk).ap()
    ctxsp = nc.dram_tensor("ctxsp", [NH, 128, NT], BF16, kind=sk).ap()
    attnsp = nc.dram_tensor("attnsp", [NT, D], BF16, kind=sk).ap()
    x2sp = nc.dram_tensor("x2sp", [NT, D], BF16, kind=sk).ap()
    ffnsp = nc.dram_tensor("ffnsp", [NT, D], BF16, kind=sk).ap()

    stack = contextlib.ExitStack()
    with stack:
        S = Sems(nc, stack)
        S.new("bar", "cld", "xld", "lnd", "trp", "trev",
              "wql", "qmm", "rope", "qspill",
              "kvg", "qld", "scmm", "expd", "ctxg", "cxev", "ctxst",
              "ctxld", "wsl", "womm", "woev", "wost",
              "hld", "hmm", "hact", "htr", "htev",
              "bld", "ymm", "mld", "comb", "outst", "b0ld", "vone",
              "xlda", "xldb", "wqla", "wqlb", "qlda", "qldb",
              "ctxsta", "ctxstb", "qspa", "qspb", "wosta", "wostb",
              "wsl0", "wsl1", "wsl2", "wsl3", "vself", "sself", "fina", "finb",
              "kvga", "kvgb", "kvgv", "xldc", "xldd", "fstat", "p7v", "trevs")

        BIGT = nc.alloc_sbuf_tensor("BIGT", [128, ND * NT], BF16).ap()
        SS = nc.alloc_sbuf_tensor("SS", [128, 2 * D], F32).ap()
        ident = nc.alloc_sbuf_tensor("identsb", [128, 128], BF16).ap()
        ones1 = nc.alloc_sbuf_tensor("ones1", [1, 128], BF16).ap()
        idxsb = nc.alloc_sbuf_tensor("idxsb", [128, 2], I32).ap()
        stats = nc.alloc_sbuf_tensor("stats", [128, 48], F32).ap()
        aggr = nc.alloc_sbuf_tensor("aggr", [128, 2], F32).ap()
        rstd = nc.alloc_sbuf_tensor("rstd", [128, 3], F32).ap()
        epsb = nc.alloc_sbuf_tensor("epsb", [128, 1], F32).ap()
        aggT = nc.alloc_sbuf_tensor("aggT", [128, 2 * (NT // 128)], F32).ap()
        aggW = nc.alloc_sbuf_tensor("aggW", [128, 2 * (NT // 128)], F32).ap()

        barv = [0]

        def barrier():
            engs = [nc.tensor, nc.vector, nc.scalar, nc.gpsimd, nc.sync]
            for e in engs:
                for name, cnt in S.c.items():
                    if cnt > 0 and name != "bar":
                        e.wait_ge(S.s[name], cnt)
            for e in engs:
                e.sem_inc(S.s["bar"], 1)
            barv[0] += 5
            for e in engs:
                e.wait_ge(S.s["bar"], barv[0])

        S.inc(nc.sync.dma_start(ident, id_in), "cld", 16)
        S.inc(nc.sync.dma_start(idxsb[:, 0:1],
                                bt_in[0:128].rearrange("(p a) -> p a", a=1)), "cld", 16)
        S.inc(nc.sync.dma_start(idxsb[:, 1:2],
                                bt_in[128:256].rearrange("(p a) -> p a", a=1)), "cld", 16)
        S.inc(nc.vector.memset(ones1, 1.0), "vself")
        S.inc(nc.vector.memset(epsb, EPS), "vself")
        S.wait(nc.vector, "cld", S.val("cld"))
        barrier()

        _ln_prev = [0, 0, 0]

        def ln_tile(src_ap, dst_ap):
            S.wait(nc.vector, "vself", S.val("vself"))
            S.wait(nc.vector, "lnd", _ln_prev[2])
            for i in range(8):
                S.inc(nc.vector.bn_stats(stats[:, 6 * i:6 * i + 6],
                                         src_ap[:, 512 * i:512 * (i + 1)]), "vself")
            S.wait(nc.vector, "vself", S.val("vself"))
            S.wait(nc.vector, "sself", _ln_prev[0])
            S.wait(nc.vector, "lnd", _ln_prev[1])
            va = S.inc(nc.vector.bn_aggr(aggr, stats), "lnd")
            _ln_prev[2] = va
            S.wait(nc.scalar, "lnd", va)
            S.wait(nc.scalar, "sself", S.val("sself"))
            _ln_prev[0] = S.inc(
                nc.scalar.activation(rstd[:, 0:1], aggr[:, 1:2], AF.Ln, bias=epsb),
                "sself")
            S.wait(nc.scalar, "sself", S.val("sself"))
            ve = S.inc(nc.scalar.activation(rstd[:, 1:2], rstd[:, 0:1], AF.Exp,
                                            scale=-0.5), "lnd")
            S.wait(nc.vector, "lnd", ve)
            vn = S.inc(nc.vector.tensor_scalar(rstd[:, 2:3], aggr[:, 0:1],
                                               rstd[:, 1:2], -1.0, ALU.mult,
                                               ALU.mult), "lnd")
            _ln_prev[1] = vn
            S.wait(nc.scalar, "lnd", vn)
            return S.inc(nc.scalar.activation(dst_ap, src_ap, AF.Identity,
                                              bias=rstd[:, 2:3],
                                              scale=rstd[:, 1:2]), "qspill")

        tr_last = {0: None, 1: None}

        def transpose_to_bigt(src_bf, tc, trp, split_evac=False):
            for g in range(8):
                gp = g % 2
                if tr_last[gp] is not None:
                    S.wait(nc.tensor, *tr_last[gp])
                for k in range(4):
                    dd = g * 4 + k
                    last = nc.tensor.transpose(trp[gp][:, k * 128:(k + 1) * 128],
                                               src_bf[:, dd * 128:(dd + 1) * 128], ident)
                v = S.inc(last, "trp")
                dst = bass.AP(BIGT.tensor, BIGT.offset + (g * 4) * NT + tc * 128,
                              [[BIGT.ap[0][0], 128], [NT, 4], [1, 128]])
                src3 = trp[gp].rearrange("p (a b) -> p a b", b=128)
                if split_evac and g % 2 == 1:
                    S.wait(nc.scalar, "trp", v)
                    ev = S.inc(nc.scalar.copy(dst, src3), "trevs")
                    tr_last[gp] = ("trevs", ev)
                else:
                    S.wait(nc.vector, "trp", v)
                    ev = S.inc(nc.vector.tensor_copy(dst, src3), "trev")
                    tr_last[gp] = ("trev", ev)

        with (
            nc.sbuf_tensor("xnb", [128, D], BF16) as xnb_g,
            nc.psum_tensor("trp0", [128, 512], BF16) as trp0_g,
            nc.psum_tensor("trp1", [128, 512], BF16) as trp1_g,
        ):
            xnb = xnb_g.ap()
            trp = [trp0_g.ap(), trp1_g.ap()]
            lnd_of_tc = {}
            for tc in range(NTC):
                par = tc % 2
                xsem = "xlda" if par == 0 else "xldb"
                qe = nc.sync if par == 0 else nc.scalar
                xt = SS[:, par * D:(par + 1) * D]
                if tc >= 2:
                    S.wait(qe, "qspill", lnd_of_tc[tc - 2])
                v = S.inc(qe.dma_start(xt, x_in[tc * 128:(tc + 1) * 128, :]),
                          xsem, 16)
                S.wait(nc.vector, xsem, v)
                S.wait(nc.scalar, xsem, v)
                if tc >= 1:
                    S.wait(nc.scalar, "trev", S.val("trev"))
                lnd_of_tc[tc] = ln_tile(xt, xnb)
                S.wait(nc.tensor, "qspill", lnd_of_tc[tc])
                transpose_to_bigt(xnb, tc, trp)
        barrier()

        with (
            nc.sbuf_tensor("wqs0", [128, D], BF16) as wqs0_g,
            nc.sbuf_tensor("wqs1", [128, D], BF16) as wqs1_g,
            nc.sbuf_tensor("qst0", [128, NT], BF16) as qst0_g,
            nc.sbuf_tensor("qst1", [128, NT], BF16) as qst1_g,
            nc.psum_tensor("qps0", [128, NT], F32) as qps0_g,
            nc.psum_tensor("qps1", [128, NT], F32) as qps1_g,
        ):
            wqs = [wqs0_g.ap(), wqs1_g.ap()]
            qst = [qst0_g.ap(), qst1_g.ap()]
            qps = [qps0_g.ap(), qps1_g.ap()]
            cosT, sinT = SS[:, 0:NT], SS[:, NT:2 * NT]
            t1, t2 = SS[:, 2 * NT:3 * NT], SS[:, 3 * NT:4 * NT]
            S.inc(nc.sync.dma_start(cosT, cos_in), "xld", 16)
            vtab = S.inc(nc.sync.dma_start(sinT, sin_in), "xld", 16)
            for h in range(NH):
                par = h % 2
                wsem = "wqla" if par == 0 else "wqlb"
                if h >= 2:
                    S.wait(nc.gpsimd, "qmm", h - 1)
                v = S.inc(nc.gpsimd.dma_start(wqs[par], wq_in[h]), wsem, 16)
                S.wait(nc.tensor, wsem, v)
                if h >= 2:
                    S.wait(nc.tensor, "rope", (h - 1) * 4)
                for dd in range(ND):
                    lhsT = wqs[par][:, dd * 128:(dd + 1) * 128]
                    for (no, nw) in chunks(NT):
                        last = nc.tensor.matmul(
                            qps[par][:, no:no + nw], lhsT,
                            BIGT[:, dd * NT + no: dd * NT + no + nw],
                            start=(dd == 0), stop=(dd == ND - 1))
                v = S.inc(last, "qmm")
                S.wait(nc.vector, "qmm", v)
                S.wait(nc.vector, "xld", vtab)
                if h >= 2:
                    S.wait(nc.vector, "qspa" if par == 0 else "qspb",
                           (h // 2) * 16)
                S.wait(nc.vector, "vself", S.val("vself"))
                S.wait(nc.vector, "rope", S.val("rope"))
                S.inc(nc.vector.tensor_tensor(t1, qps[par], cosT, ALU.mult), "vself")
                S.inc(nc.vector.tensor_tensor(t2[0:64, :], qps[par][64:128, :],
                                              sinT[0:64, :], ALU.mult), "vself")
                S.inc(nc.vector.tensor_tensor(t2[64:128, :], qps[par][0:64, :],
                                              sinT[64:128, :], ALU.mult), "vself")
                S.wait(nc.vector, "vself", S.val("vself"))
                v = S.inc(nc.vector.tensor_tensor(qst[par], t1, t2, ALU.add),
                          "rope", 4)
                S.wait(nc.sync, "rope", v)
                S.inc(nc.sync.dma_start(qsp[h], qst[par]),
                      "qspa" if par == 0 else "qspb", 16)
            S.wait(nc.sync, "qspa", S.val("qspa"))
            S.wait(nc.sync, "qspb", S.val("qspb"))
        barrier()

        if True:
            aes = contextlib.ExitStack()
            _a = aes.enter_context
            SSB = SS.bitcast(BF16)
            kb = [SSB[:, 0:2048], SSB[:, 2048:4096]]
            vraw = SSB[:, 4096:6144]
            vp = [SSB[:, 6144 + i * 2064: 6144 + (i + 1) * 2064] for i in range(4)]
            KT = [_a(nc.sbuf_tensor(f"KT{i}", [128, 32 * 128], BF16)).ap()
                  for i in range(2)]
            qth = [_a(nc.sbuf_tensor(f"qth{i}", [128, NT], BF16)).ap()
                   for i in range(2)]
            cxn = _a(nc.sbuf_tensor("cxn", [128, TH], BF16)).ap()
            cxT = [_a(nc.sbuf_tensor(f"cxT{i}", [128, NT], BF16)).ap()
                   for i in range(2)]
            recb = _a(nc.sbuf_tensor("recb", [128, 1], F32)).ap()
            scps = [_a(nc.psum_tensor(f"scps{i}", [128, TH], F32)).ap()
                    for i in range(2)]
            cxb = [_a(nc.psum_tensor(f"cxb{i}", [128, 512], F32)).ap()
                   for i in range(3)]
            trpA = _a(nc.psum_tensor("trpA", [128, 512], BF16)).ap()

            c = AttnCtx()
            c.TH, c.NTH = TH, NTH
            c.Earea = [BIGT[:, 0:32 * TH], BIGT[:, 32 * TH:64 * TH]]
            c.KT, c.qth, c.cxn, c.cxT, c.recb = KT, qth, cxn, cxT, recb
            c.scps, c.cxb, c.trpA, c.vp = scps, cxb, trpA, vp
            c.ident, c.qsp, c.ctxsp = ident, qsp, ctxsp
            c.vq, c.expd_end, c.ctxg_end = {}, {}, {}
            c.scps_gate = [0, 0]
            c.trp_setup_end = {}
            c.scmm_kv_end = {}
            c.ctxg_kv_end = {}

            def attn_kv_setup(kv):
                kvp = kv % 2
                if kv > 0:
                    S.wait(nc.gpsimd, "trp", c.trp_setup_end[kv - 1])
                ksrc = bass.AP(kc_in.tensor, 0,
                               [[BS * HD, 512 * NKV], [HD, BS], [1, HD]])
                vsrc = bass.AP(vc_in.tensor, 0,
                               [[BS * HD, 512 * NKV], [HD, BS], [1, HD]])
                kbv = []
                for H in range(2):
                    kbv.append(S.inc(nc.gpsimd.indirect_dma_start(
                        kb[H], None, ksrc,
                        IndirectOffsetOnAxis(idxsb[:, H:H + 1], 0),
                        element_offset=kv * 512 * BS * HD),
                        "kvga" if H == 0 else "kvgb", 16))
                if kv > 1:
                    S.wait(nc.vector, "ctxg", c.ctxg_kv_end[kv - 2])
                for H in range(2):
                    S.wait(nc.gpsimd, "vone", S.val("vone"))
                    vv = S.inc(nc.gpsimd.indirect_dma_start(
                        vraw, None, vsrc,
                        IndirectOffsetOnAxis(idxsb[:, H:H + 1], 0),
                        element_offset=kv * 512 * BS * HD), "kvgv", 16)
                    S.wait(nc.vector, "kvgv", vv)
                    if kv < 2:
                        S.inc(nc.vector.memset(
                            vp[kvp * 2 + H].rearrange("p (s e) -> p s e", e=129)
                            [:, :, 128:129], 1.0), "vone", 1)
                    S.inc(nc.vector.tensor_copy(
                        vp[kvp * 2 + H].rearrange("p (s e) -> p s e", e=129)
                        [:, :, 0:128],
                        vraw.rearrange("p (s d) -> p s d", d=128)), "vone", 1)
                S.wait(nc.tensor, "kvga", kbv[0])
                S.wait(nc.tensor, "kvgb", kbv[1])
                S.wait(nc.tensor, "vone", S.val("vone"))
                for g in range(8):
                    if S.val("trp") >= 1:
                        S.wait(nc.tensor, "trev", S.val("trev"))
                    for k in range(4):
                        kt = g * 4 + k
                        s_, H_ = kt // 2, kt % 2
                        last = nc.tensor.transpose(
                            trpA[:, k * 128:(k + 1) * 128],
                            kb[H_][:, s_ * 128:(s_ + 1) * 128], ident)
                    v = S.inc(last, "trp")
                    S.wait(nc.vector, "trp", v)
                    if kv > 1 and g == 0:
                        S.wait(nc.vector, "scmm", c.scmm_kv_end[kv - 2])
                    S.inc(nc.vector.tensor_copy(KT[kvp][:, g * 512:(g + 1) * 512],
                                                trpA), "trev")
                c.trp_setup_end[kv] = S.val("trp")

            units = [(kv, hh, th) for kv in range(NKV) for hh in range(4)
                     for th in range(NTH)]
            prev = None
            for ui, u in enumerate(units):
                if u[1] == 0 and u[2] == 0:
                    if u[0] > 0:
                        c.scmm_kv_end[u[0] - 1] = S.val("scmm")
                        if ui >= 2:
                            c.ctxg_kv_end[u[0] - 1] = None
                    attn_kv_setup(u[0])
                attn_A_dummy = None
                attn_unit(nc, S, c, u, ui, prev)
                if prev is not None and prev[1] == 3 and prev[2] == NTH - 1:
                    c.ctxg_kv_end[prev[0]] = c.ctxg_end[ui - 1]
                prev = u
            lastui = len(units) - 1
            for tc in range(CTC):
                attn_ctx_tc(nc, S, c, prev, lastui, tc)
            attn_T(nc, S, c, prev, lastui)
            S.wait(nc.sync, "ctxsta", S.val("ctxsta"))
            S.wait(nc.sync, "ctxstb", S.val("ctxstb"))
            aes.close()
        barrier()

        for h in range(NH):
            S.inc(nc.sync.dma_start(BIGT[:, h * NT:(h + 1) * NT], ctxsp[h]),
                  "ctxld", 16)
        S.wait(nc.tensor, "ctxld", S.val("ctxld"))

        def out_proj(w_dram, dst_dram, tag, lora=False, hT=None, b0sb=None,
                     blsb=None, mtb=None, accb=None, yps=None, wsl_ap=None,
                     stat_buf=None, stat_sem=None, agg_buf=None):
            with (
                contextlib.nullcontext() if wsl_ap is not None else
                nc.sbuf_tensor(f"wsl{tag}", [128, ND * OS_W], BF16) as wsl_g,
                nc.sbuf_tensor(f"ost{tag}0", [128, OS_W], BF16) as ost0_g,
                nc.sbuf_tensor(f"ost{tag}1", [128, OS_W], BF16) as ost1_g,
                nc.psum_tensor(f"wps{tag}0", [128, OS_W], F32) as wps0_g,
                nc.psum_tensor(f"wps{tag}1", [128, OS_W], F32) as wps1_g,
            ):
                wsl = wsl_ap if wsl_ap is not None else wsl_g.ap()
                ost = [ost0_g.ap(), ost1_g.ap()]
                wps = [wps0_g.ap(), wps1_g.ap()]
                nos = D // OS_W
                it = 0
                b_woev = S.val("woev")
                b_comb = S.val("comb")
                b_wosta = S.val("wosta")
                b_wostb = S.val("wostb")
                for os_ in range(nos):
                    if os_ > 0:
                        S.wait(nc.gpsimd, "womm", S.val("womm"))
                    QW = ND // 4 * OS_W
                    for q in range(4):
                        S.inc(nc.gpsimd.dma_start(
                            wsl[:, q * QW:(q + 1) * QW],
                            w_dram[os_][:, q * QW:(q + 1) * QW]), f"wsl{q}", 16)
                    if lora:
                        S.wait(nc.gpsimd, "ymm", S.val("ymm"))
                        for l in range(3):
                            S.inc(nc.gpsimd.dma_start(
                                blsb[0:16, l * OS_W:(l + 1) * OS_W],
                                bl_in[l + 1][:, os_ * OS_W:(os_ + 1) * OS_W]),
                                "bld", 16)
                        S.inc(nc.gpsimd.dma_start(
                            blsb[0:16, 3 * OS_W:4 * OS_W],
                            bl_in[0][:, os_ * OS_W:(os_ + 1) * OS_W]), "bld", 16)
                    for tc in range(NTC):
                        pw = it % 2
                        pm = it % 2
                        if lora:
                            S.wait(nc.sync, "comb", max(0, S.val("comb") - 1))
                            for l in range(3):
                                S.inc(nc.sync.dma_start(
                                    mtb[:, (pm * 3 + l) * OS_W:(pm * 3 + l + 1) * OS_W],
                                    m_in[l][tc * 128:(tc + 1) * 128,
                                            os_ * OS_W:(os_ + 1) * OS_W]), "mld", 16)
                            vml = S.val("mld")
                        if it >= 2:
                            S.wait(nc.tensor, "woev" if not lora else "comb",
                                   (b_woev if not lora else b_comb) + it - 1)
                        for dd in range(ND):
                            if tc == 0 and dd % (ND // 4) == 0:
                                S.wait(nc.tensor, f"wsl{dd // (ND // 4)}",
                                       S.val(f"wsl{dd // (ND // 4)}"))
                            last = nc.tensor.matmul(
                                wps[pw],
                                BIGT[:, dd * NT + tc * 128: dd * NT + (tc + 1) * 128],
                                wsl[:, dd * OS_W:(dd + 1) * OS_W],
                                start=(dd == 0), stop=(not lora) and (dd == ND - 1))
                        if lora:
                            S.wait(nc.tensor, "b0ld", 16)
                            last = nc.tensor.matmul(
                                wps[pw], ones1,
                                b0sb[0:1, os_ * OS_W:(os_ + 1) * OS_W],
                                start=False, stop=True)
                        vbase = S.inc(last, "womm")
                        if lora:
                            if it >= 1:
                                S.wait(nc.tensor, "comb", b_comb + it)
                            S.wait(nc.tensor, "bld", S.val("bld"))
                            S.wait(nc.tensor, "htev", S.val("htev"))
                            for sl in range(4):
                                l = (sl + 1) % 4
                                last = nc.tensor.matmul(
                                    yps[sl], hT[l][:, tc * 128:(tc + 1) * 128],
                                    blsb[0:16, sl * OS_W:(sl + 1) * OS_W],
                                    start=True, stop=True)
                            vy = S.inc(last, "ymm")
                            S.wait(nc.vector, "ymm", vy)
                            S.wait(nc.vector, "mld", vml)
                            S.wait(nc.vector, "vself", S.val("vself"))
                            S.wait(nc.vector, "comb", S.val("comb"))
                            S.inc(nc.vector.tensor_copy(accb, yps[3]), "vself")
                            for sl in range(3):
                                S.wait(nc.vector, "vself", S.val("vself"))
                                S.inc(nc.vector.copy_predicated(
                                    accb,
                                    mtb[:, (pm * 3 + sl) * OS_W:(pm * 3 + sl + 1) * OS_W],
                                    yps[sl]), "vself")
                            S.wait(nc.vector, "vself", S.val("vself"))
                            S.wait(nc.vector, "womm", vbase)
                            if it >= 2:
                                S.wait(nc.vector, "wosta" if pw == 0 else "wostb",
                                       (b_wosta if pw == 0 else b_wostb)
                                       + (it // 2) * 16)
                            vv = S.inc(nc.vector.tensor_tensor(ost[pw], accb,
                                                               wps[pw], ALU.add),
                                       "comb")
                            if stat_buf is not None:
                                S.wait(nc.vector, "comb", vv)
                                vs = S.inc(nc.vector.bn_stats(
                                    stat_buf[:, tc * 48 + os_ * 6:
                                             tc * 48 + os_ * 6 + 6], ost[pw]),
                                    stat_sem)
                                if os_ == nos - 1:
                                    S.wait(nc.vector, stat_sem, vs)
                                    S.inc(nc.vector.bn_aggr(
                                        agg_buf[:, 2 * tc:2 * tc + 2],
                                        stat_buf[:, tc * 48:(tc + 1) * 48]),
                                        stat_sem)
                        else:
                            S.wait(nc.vector, "womm", vbase)
                            if it >= 2:
                                S.wait(nc.vector, "wosta" if pw == 0 else "wostb",
                                       (b_wosta if pw == 0 else b_wostb)
                                       + (it // 2) * 16)
                            if stat_buf is not None:
                                vs = S.inc(nc.vector.bn_stats(
                                    stat_buf[:, tc * 48 + os_ * 6:
                                             tc * 48 + os_ * 6 + 6], wps[pw]),
                                    stat_sem)
                                if os_ == nos - 1:
                                    S.wait(nc.vector, stat_sem, vs)
                                    S.inc(nc.vector.bn_aggr(
                                        agg_buf[:, 2 * tc:2 * tc + 2],
                                        stat_buf[:, tc * 48:(tc + 1) * 48]),
                                        stat_sem)
                            ve = S.inc(nc.vector.tensor_copy(ost[pw], wps[pw]),
                                       "woev")
                        if lora:
                            S.wait(nc.sync, "comb", S.val("comb"))
                        else:
                            S.wait(nc.sync, "woev", S.val("woev"))
                        S.inc(nc.sync.dma_start(
                            dst_dram[tc * 128:(tc + 1) * 128,
                                     os_ * OS_W:(os_ + 1) * OS_W], ost[pw]),
                            "wosta" if pw == 0 else "wostb", 16)
                        it += 1
                S.wait(nc.sync, "wosta", S.val("wosta"))
                S.wait(nc.sync, "wostb", S.val("wostb"))

        with nc.psum_tensor("stats_w", [128, NTC * 48], F32) as stats_w_g:
            out_proj(wo_in, attnsp, "o", stat_buf=stats_w_g.ap(),
                     stat_sem="fstat", agg_buf=aggW)
        barrier()

        with (
            nc.sbuf_tensor("hT0", [16, NTC * 128], BF16) as hT0_g,
            nc.sbuf_tensor("hT1", [16, NTC * 128], BF16) as hT1_g,
            nc.sbuf_tensor("hT2", [16, NTC * 128], BF16) as hT2_g,
            nc.sbuf_tensor("hT3", [16, NTC * 128], BF16) as hT3_g,
        ):
            hT = [hT0_g.ap(), hT1_g.ap(), hT2_g.ap(), hT3_g.ap()]
            p5 = contextlib.ExitStack()
            _p = p5.enter_context
            x2b = [_p(nc.sbuf_tensor("x2b0", [128, D], BF16)).ap(),
                   _p(nc.sbuf_tensor("x2b1", [128, D], BF16)).ap()]
            xnb2 = _p(nc.sbuf_tensor("xnb2", [128, D], BF16)).ap()
            atsb = _p(nc.sbuf_tensor("atsb", [128, ND * 64], BF16)).ap()
            h8b = _p(nc.sbuf_tensor("h8b", [128, 64], BF16)).ap()
            trpb = [_p(nc.psum_tensor("trp0b", [128, 512], BF16)).ap(),
                    _p(nc.psum_tensor("trp1b", [128, 512], BF16)).ap()]
            hps = _p(nc.psum_tensor("hps", [128, 64], F32)).ap()
            trpC = _p(nc.psum_tensor("trpC", [128, 512], BF16)).ap()
            rsf5 = _p(nc.sbuf_tensor("rsf5", [128, 8], F32)).ap()
            a5 = _p(nc.sbuf_tensor("a5", [128, 4], F32)).ap()
            nm5 = _p(nc.sbuf_tensor("nm5", [128, 4], F32)).ap()
            rs1b = _p(nc.sbuf_tensor("rs1b", [128, NTC], F32)).ap()
            nm1b = _p(nc.sbuf_tensor("nm1b", [128, NTC], F32)).ap()
            vat = S.inc(nc.gpsimd.dma_start(atsb, at_in), "hld", 16)
            SSB5 = SS.bitcast(BF16)
            ats = [SSB5[:, 0:D], SSB5[:, D:2 * D]]
            xtf = SS[:, D:2 * D]
            outst_b = S.val("outst")
            ex1 = 0
            for tc in range(NTC):
                S.wait(nc.scalar, "sself", S.val("sself"))
                S.inc(nc.scalar.activation(
                    rsf5[:, 0:1], aggW[:, 2 * tc + 1:2 * tc + 2],
                    AF.Ln, bias=epsb), "sself")
                S.wait(nc.scalar, "sself", S.val("sself"))
                ex1 = S.inc(nc.scalar.activation(
                    rs1b[:, tc:tc + 1], rsf5[:, 0:1], AF.Exp, scale=-0.5),
                    "sself")
            S.wait(nc.vector, "sself", ex1)
            agw_mu = aggW.rearrange("p (t w) -> p t w", w=2)[:, :, 0:1]
            vt1 = S.inc(nc.vector.tensor_tensor(
                nm1b.rearrange("p (t w) -> p t w", w=1), agw_mu,
                rs1b.rearrange("p (t w) -> p t w", w=1), ALU.mult), "lnd")
            S.wait(nc.vector, "lnd", vt1)
            vpre = S.inc(nc.vector.tensor_scalar(
                nm1b, nm1b, -1.0, 0.0, ALU.mult, ALU.add), "lnd")
            n1v, ttv, n2v, nm2v, l2v, bnv, ag2v = {}, {}, {}, {}, {}, {}, {}
            trev5, hmmv, hactv, htrv, htevv = {}, {}, {}, {}, {}
            for it in range(NTC + 2):
                tc, tp, tb = it, it - 1, it - 2
                if tb >= 0:
                    S.wait(nc.tensor, "trev", trev5[tb])
                    if tb == 0:
                        S.wait(nc.tensor, "hld", vat)
                    if tb >= 1:
                        S.wait(nc.tensor, "hact", hactv[tb - 1])
                    for dd in range(ND):
                        last = nc.tensor.matmul(
                            hps,
                            BIGT[:, dd * NT + tb * 128: dd * NT + (tb + 1) * 128],
                            atsb[:, dd * 64:(dd + 1) * 64],
                            start=(dd == 0), stop=(dd == ND - 1))
                    hmmv[tb] = S.inc(last, "hmm")
                    S.wait(nc.scalar, "hmm", hmmv[tb])
                    if tb >= 1:
                        S.wait(nc.scalar, "htr", htrv[tb - 1])
                    hactv[tb] = S.inc(
                        nc.scalar.activation(h8b, hps, AF.Copy, scale=8.0),
                        "hact")
                    S.wait(nc.tensor, "hact", hactv[tb])
                    if tb >= 1:
                        S.wait(nc.tensor, "htev", htevv[tb - 1])
                    for l in range(4):
                        last = nc.tensor.transpose(
                            trpC[0:16, l * 128:(l + 1) * 128],
                            h8b[:, l * 16:(l + 1) * 16], ident)
                    htrv[tb] = S.inc(last, "htr")
                if tc < NTC:
                    p = tc % 2
                    asem = "xlda" if p == 0 else "xldb"
                    at_t = ats[p]
                    x2t = x2b[p]
                    if tc >= 2:
                        S.wait(nc.scalar, "sself", n1v[tc - 2])
                    va = S.inc(nc.scalar.dma_start(
                        at_t, attnsp[tc * 128:(tc + 1) * 128, :]), asem, 16)
                    if tc >= 1:
                        S.wait(nc.sync, "xld", ttv[tc - 1])
                    vx = S.inc(nc.sync.dma_start(
                        xtf, x_in[tc * 128:(tc + 1) * 128, :]), "xldc", 16)
                    S.wait(nc.scalar, asem, va)
                    if tc == 0:
                        S.wait(nc.scalar, "lnd", vpre)
                    if tc >= 2:
                        S.wait(nc.scalar, "vself", bnv[tc - 2])
                        S.wait(nc.scalar, "outst",
                               outst_b + 16 * tc)
                    n1v[tc] = S.inc(nc.scalar.activation(
                        x2t, at_t, AF.Identity, bias=nm1b[:, tc:tc + 1],
                        scale=rs1b[:, tc:tc + 1]), "sself")
                    S.wait(nc.vector, "sself", n1v[tc])
                    S.wait(nc.vector, "xldc", vx)
                    ttv[tc] = S.inc(
                        nc.vector.tensor_tensor(x2t, x2t, xtf, ALU.add), "xld")
                    S.wait(nc.gpsimd, "xld", ttv[tc])
                    S.inc(nc.gpsimd.dma_start(
                        x2sp[tc * 128:(tc + 1) * 128, :], x2t), "outst", 16)
                    S.wait(nc.vector, "xld", ttv[tc])
                    S.wait(nc.vector, "vself", S.val("vself"))
                    if tc >= 1:
                        S.wait(nc.vector, "lnd", ag2v[tc - 1])
                    for i in range(8):
                        S.inc(nc.vector.bn_stats(stats[:, 6 * i:6 * i + 6],
                                                 x2t[:, 512 * i:512 * (i + 1)]),
                              "vself")
                    bnv[tc] = S.val("vself")
                    S.wait(nc.vector, "vself", bnv[tc])
                    if tc >= 2:
                        S.wait(nc.vector, "sself", l2v[tc - 2])
                        S.wait(nc.vector, "lnd", nm2v[tc - 2])
                    ag2v[tc] = S.inc(
                        nc.vector.bn_aggr(a5[:, 2 * p:2 * p + 2], stats), "lnd")
                if tp >= 0 and tp < NTC:
                    p = tp % 2
                    x2t = x2b[p]
                    S.wait(nc.scalar, "lnd", ag2v[tp])
                    S.wait(nc.scalar, "sself", S.val("sself"))
                    if tp >= 2:
                        S.wait(nc.scalar, "lnd", nm2v[tp - 2])
                    l2v[tp] = S.inc(nc.scalar.activation(
                        rsf5[:, 4 * p + 2:4 * p + 3], a5[:, 2 * p + 1:2 * p + 2],
                        AF.Ln, bias=epsb), "sself")
                    S.wait(nc.scalar, "sself", l2v[tp])
                    e2 = S.inc(nc.scalar.activation(
                        rsf5[:, 4 * p + 3:4 * p + 4], rsf5[:, 4 * p + 2:4 * p + 3],
                        AF.Exp, scale=-0.5), "sself")
                    S.wait(nc.vector, "sself", e2)
                    if tp >= 2:
                        S.wait(nc.vector, "sself", n2v[tp - 2])
                    nm2v[tp] = S.inc(nc.vector.tensor_scalar(
                        nm5[:, 2 * p + 1:2 * p + 2], a5[:, 2 * p:2 * p + 1],
                        rsf5[:, 4 * p + 3:4 * p + 4], -1.0, ALU.mult, ALU.mult),
                        "lnd")
                    S.wait(nc.scalar, "lnd", nm2v[tp])
                    S.wait(nc.scalar, "xld", ttv[tp])
                    if tp >= 1:
                        S.wait(nc.scalar, "trev", trev5[tp - 1])
                    n2v[tp] = S.inc(nc.scalar.activation(
                        xnb2, x2t, AF.Identity, bias=nm5[:, 2 * p + 1:2 * p + 2],
                        scale=rsf5[:, 4 * p + 3:4 * p + 4]), "sself")
                    S.wait(nc.tensor, "sself", n2v[tp])
                    transpose_to_bigt(xnb2, tp, trpb)
                    trev5[tp] = S.val("trev")
                if tb >= 0:
                    S.wait(nc.vector, "htr", htrv[tb])
                    for l in range(4):
                        last = nc.vector.tensor_copy(
                            hT[l][:, tb * 128:(tb + 1) * 128],
                            trpC[0:16, l * 128:(l + 1) * 128])
                    htevv[tb] = S.inc(last, "htev")
            S.wait(nc.sync, "outst", S.val("outst"))
            S.wait(nc.tensor, "htev", S.val("htev"))
            p5.close()
            barrier()

            with (
                nc.sbuf_tensor("b0sbf", [1, D], BF16) as b0sb_g,
                nc.sbuf_tensor("blsb", [16, 4 * OS_W], BF16) as blsb_g,
                nc.sbuf_tensor("mtb", [128, 6 * OS_W], U8) as mtb_g,
                nc.sbuf_tensor("accb", [128, OS_W], F32) as accb_g,
                nc.psum_tensor("yps0", [128, OS_W], F32) as yps0_g,
                nc.psum_tensor("yps1", [128, OS_W], F32) as yps1_g,
                nc.psum_tensor("yps2", [128, OS_W], F32) as yps2_g,
                nc.psum_tensor("yps3", [128, OS_W], F32) as yps3_g,
                nc.psum_tensor("stats_f", [128, NTC * 48], F32) as stats_f_g,
            ):
                b0sb = b0sb_g.ap()
                S.inc(nc.gpsimd.dma_start(b0sb, b0_in), "b0ld", 16)
                out_proj(w0_in, ffnsp, "f", lora=True, hT=hT, b0sb=b0sb,
                         blsb=blsb_g.ap(),
                         mtb=mtb_g.ap(), accb=accb_g.ap(),
                         yps=[yps0_g.ap(), yps1_g.ap(), yps2_g.ap(), yps3_g.ap()],
                         wsl_ap=SS.bitcast(BF16),
                         stat_buf=stats_f_g.ap(), stat_sem="fstat",
                         agg_buf=aggT)
        barrier()

        if True:
            p7 = contextlib.ExitStack()
            SSB7 = SS.bitcast(BF16)
            fts = [SSB7[:, 0:D], SSB7[:, D:2 * D]]
            x2s = [SSB7[:, 2 * D:3 * D], SSB7[:, 3 * D:4 * D]]
            rsf = p7.enter_context(nc.sbuf_tensor("rsf", [128, 4], F32)).ap()
            nmrb = p7.enter_context(nc.sbuf_tensor("nmrb", [128, 2], F32)).ap()
            obuf = p7.enter_context(nc.sbuf_tensor("obuf", [128, D], F32)).ap()
            p7ps = [obuf[:, k * 512:(k + 1) * 512] for k in range(8)]
            bsem = ["fina", "qspa", "finb", "qspb",
                    "outst", "kvga", "kvgb", "kvgv"]
            qeng = [nc.sync, nc.sync, nc.scalar, nc.scalar,
                    nc.gpsimd, nc.gpsimd, nc.gpsimd, nc.gpsimd]
            bbase = {s: S.val(s) for s in bsem}
            p7base = S.val("p7v")
            NCH = D // 512
            ln_val, nm_val, ex_val = {}, {}, {}
            for tc in range(NTC):
                p = tc % 2
                ft, x2t = fts[p], x2s[p]
                fsem = "xlda" if p == 0 else "xldb"
                xsem = "xldc" if p == 0 else "xldd"
                slot_free = p7base + NCH * (tc - 1)
                if tc >= 2:
                    S.wait(nc.scalar, "p7v", slot_free)
                vft = S.inc(nc.scalar.dma_start(
                    ft, ffnsp[tc * 128:(tc + 1) * 128, :]), fsem, 16)
                if tc >= 2:
                    S.wait(nc.sync, "p7v", slot_free)
                vx2 = S.inc(nc.sync.dma_start(
                    x2t, x2sp[tc * 128:(tc + 1) * 128, :]), xsem, 16)
                if tc >= 2:
                    S.wait(nc.scalar, "p7v", slot_free)
                S.wait(nc.scalar, "sself", S.val("sself"))
                ln_val[tc] = S.inc(nc.scalar.activation(
                    rsf[:, 2 * p:2 * p + 1], aggT[:, 2 * tc + 1:2 * tc + 2],
                    AF.Ln, bias=epsb), "sself")
                S.wait(nc.scalar, "sself", ln_val[tc])
                ex_val[tc] = S.inc(nc.scalar.activation(
                    rsf[:, 2 * p + 1:2 * p + 2], rsf[:, 2 * p:2 * p + 1],
                    AF.Exp, scale=-0.5), "lnd")
                S.wait(nc.vector, "lnd", ex_val[tc])
                if tc >= 2:
                    S.wait(nc.vector, "p7v", slot_free)
                nm_val[tc] = S.inc(nc.vector.tensor_scalar(
                    nmrb[:, p:p + 1], aggT[:, 2 * tc:2 * tc + 1],
                    rsf[:, 2 * p + 1:2 * p + 2], -1.0, ALU.mult, ALU.mult),
                    "lnd")
                S.wait(nc.scalar, "lnd", nm_val[tc])
                S.wait(nc.scalar, fsem, vft)
                S.wait(nc.vector, xsem, vx2)
                vafs = []
                for k in range(NCH):
                    if tc >= 1:
                        S.wait(nc.scalar, bsem[k], bbase[bsem[k]] + 16 * tc)
                    vact = S.inc(nc.scalar.activation(
                        p7ps[k], ft[:, k * 512:(k + 1) * 512], AF.Identity,
                        bias=nmrb[:, p:p + 1],
                        scale=rsf[:, 2 * p + 1:2 * p + 2]), "sself")
                    S.wait(nc.vector, "sself", vact)
                    vafs.append(S.inc(nc.vector.tensor_tensor(
                        p7ps[k], p7ps[k],
                        x2t[:, k * 512:(k + 1) * 512], ALU.add), "p7v"))
                for k in range(NCH):
                    S.wait(qeng[k], "p7v", vafs[k])
                    S.inc(qeng[k].dma_start(
                        out_d[tc * 128:(tc + 1) * 128, k * 512:(k + 1) * 512],
                        p7ps[k]), bsem[k], 16)
            for s in bsem:
                S.wait(nc.sync, s, S.val(s))
            p7.close()
        barrier()

    nc.finalize()
    return nc




def _threefry_uniform_key42(n):
    def rotl(x, d):
        return ((x << np.uint32(d)) | (x >> np.uint32(32 - d))).astype(np.uint32)

    def tf2x32(k0, k1, x0, x1):
        ks0, ks1 = np.uint32(k0), np.uint32(k1)
        ks2 = np.uint32(ks0 ^ ks1 ^ np.uint32(0x1BD11BDA))
        x0 = (x0 + ks0).astype(np.uint32)
        x1 = (x1 + ks1).astype(np.uint32)
        rots = [[13, 15, 26, 6], [17, 29, 16, 24]]
        ks = [ks1, ks2, ks0]
        for i in range(5):
            for r in rots[i % 2]:
                x0 = (x0 + x1).astype(np.uint32)
                x1 = rotl(x1, r)
                x1 = (x1 ^ x0).astype(np.uint32)
            x0 = (x0 + ks[i % 3]).astype(np.uint32)
            x1 = (x1 + ks[(i + 1) % 3] + np.uint32(i + 1)).astype(np.uint32)
        return x0, x1

    idx = np.arange(n, dtype=np.uint64)
    lo = (idx & np.uint64(0xFFFFFFFF)).astype(np.uint32)
    hi = (idx >> np.uint64(32)).astype(np.uint32)
    o0, o1 = tf2x32(0, 42, hi, lo)
    bits = o0 ^ o1
    return ((bits >> np.uint32(9)) | np.uint32(0x3F800000)).view(np.float32) - np.float32(1.0)


def _host_prep(Wq, Wo, W0, A, Bm, b0):
    WQH = np.ascontiguousarray(
        Wq.reshape(NH, 128, ND, 128).transpose(0, 3, 2, 1).reshape(NH, 128, D))
    nos = D // OS_W
    WOH = np.ascontiguousarray(
        Wo.reshape(nos, OS_W, ND, 128).transpose(0, 3, 2, 1).reshape(nos, 128, ND * OS_W))
    W0H = np.ascontiguousarray(
        W0.reshape(nos, OS_W, ND, 128).transpose(0, 3, 2, 1).reshape(nos, 128, ND * OS_W))
    ATH = np.ascontiguousarray(
        A.reshape(4, 16, ND, 128).transpose(3, 2, 0, 1).reshape(128, ND * 64))
    BLT = np.ascontiguousarray(Bm.transpose(0, 2, 1))
    b0h = np.ascontiguousarray(b0.reshape(1, D))

    Sq = 8192
    inv = (1.0 / (np.float32(10000.0) ** (np.arange(0, HD, 2, dtype=np.float32)
                                          / np.float32(HD)))).astype(np.float32)
    t = np.arange(Sq, dtype=np.float32)
    freqs = np.outer(t, inv)
    emb = np.concatenate([freqs, freqs], axis=-1)
    cos = np.cos(emb).astype(np.float32)
    sinf = np.sin(emb).astype(np.float32)
    sinf = sinf.copy()
    sinf[:, 0:64] *= -1.0

    u = _threefry_uniform_key42(2 * 8192 * D).reshape(16384, D)
    masks = []
    for l in range(1, 4):
        lo, hi = np.float32(0.25 * l), np.float32(0.25 * (l + 1))
        masks.append(((u >= lo) & (u < hi)).astype(np.uint8))

    ident = np.eye(128, dtype=BF)
    return WQH, WOH, W0H, ATH, BLT, b0h, cos, sinf, masks, ident


def make_in_maps(L0_attn_hidden_states, L0_attn_k_cache, L0_attn_v_cache, L0_attn_Wq,
                 L0_attn_Wo, L0_attn_block_table, L0_ffn_W0, L0_ffn_b0, L0_ffn_A,
                 L0_ffn_B, NT):
    WQH, WOH, W0H, ATH, BLT, b0h, cos, sinf, masks, ident = _host_prep(
        np.asarray(L0_attn_Wq, np.float32), np.asarray(L0_attn_Wo, np.float32),
        np.asarray(L0_ffn_W0, np.float32), np.asarray(L0_ffn_A, np.float32),
        np.asarray(L0_ffn_B, np.float32), np.asarray(L0_ffn_b0, np.float32))
    X = np.ascontiguousarray(
        np.asarray(L0_attn_hidden_states, np.float32).reshape(16384, D))
    kc = np.ascontiguousarray(np.asarray(L0_attn_k_cache, np.float32).transpose(1, 0, 2, 3))
    vc = np.ascontiguousarray(np.asarray(L0_attn_v_cache, np.float32).transpose(1, 0, 2, 3))
    bt = np.ascontiguousarray(np.asarray(L0_attn_block_table, np.int32))

    in_maps = []
    for c in range(N_CORES):
        r0 = c * NT
        p0 = r0 % 8192
        m = {
            "x": X[r0:r0 + NT], "wq": WQH, "wo": WOH, "w0": W0H,
            "kc": kc, "vc": vc, "bt": bt, "at": ATH, "bl": BLT, "b0": b0h,
            "cosT": np.ascontiguousarray(cos[p0:p0 + NT].T),
            "sinT": np.ascontiguousarray(sinf[p0:p0 + NT].T),
            "ident": ident,
        }
        for l in range(3):
            m[f"m{l + 1}"] = np.ascontiguousarray(masks[l][r0:r0 + NT])
        in_maps.append(m)
    return in_maps


def kernel(L0_attn_hidden_states, L0_attn_k_cache, L0_attn_v_cache, L0_attn_Wq,
           L0_attn_Wo, L0_attn_block_table, L0_ffn_W0, L0_ffn_b0, L0_ffn_A, L0_ffn_B):
    NT = 16384 // N_CORES
    in_maps = make_in_maps(L0_attn_hidden_states, L0_attn_k_cache, L0_attn_v_cache,
                           L0_attn_Wq, L0_attn_Wo, L0_attn_block_table, L0_ffn_W0,
                           L0_ffn_b0, L0_ffn_A, L0_ffn_B, NT)
    nc = build_program(NT)
    res = run_bass_kernel_spmd(nc, in_maps, core_ids=list(range(N_CORES)))
    out = np.concatenate([res.results[c]["out"] for c in range(N_CORES)], axis=0)
    return out.reshape(2, 8192, D).astype(np.float32)
